# version 17
# baseline (speedup 1.0000x reference)
import numpy as np
import concourse.bass as bass
import concourse.mybir as mybir
from concourse.bass_utils import run_bass_kernel_spmd

F32 = mybir.dt.float32
BF16 = mybir.dt.bfloat16
AF = mybir.ActivationFunctionType
ALU = mybir.AluOpType
AX = mybir.AxisListType
AP = bass.AP

MAGIC = 12582912.0
QHI, QLO = 127.49, -128.49
D = 1024
NH, HD, DL = 16, 64, 256
NI, IH = 22, 2816
SCALE_Q = float(1.0 / (np.float64(np.sqrt(np.float32(1024.0))) + 1e-8))
LAM = 1e-4
EPS = 1e-5
QT = 256


class Sem:
    def __init__(self, h, step=1):
        self.h, self.n, self.step = h, 0, step


class Ctx:
    ENGS = ("sync", "gpsimd", "vector", "scalar", "tensor")

    def __init__(self):
        self.streams = {e: [] for e in self.ENGS}

    def op(self, eng, method, *args, inc=None, **kwargs):
        incs = []
        if inc is not None:
            for s in (inc if isinstance(inc, list) else [inc]):
                s.n += s.step
                incs.append((s.h, s.step))

        import traceback
        tb = traceback.extract_stack()[-2]

        def fn(e, _m=method, _a=args, _k=kwargs, _i=incs, _tb=tb):
            try:
                inst = getattr(e, _m)(*_a, **_k)
            except Exception:
                print(f"FAILING OP: {_m} emitted at {_tb.filename}:{_tb.lineno}")
                raise
            for (h, k) in _i:
                inst = inst.then_inc(h, k)
        self.streams[eng].append(fn)

    def wait(self, eng, sem, val=None):
        v = sem.n if val is None else val
        if v <= 0:
            return
        self.streams[eng].append(lambda e, _h=sem.h, _v=v: e.wait_ge(_h, _v))

    def drain(self, eng):
        self.streams[eng].append(lambda e: e.drain())

    def wait_tok(self, eng, tok):
        if tok is None:
            return
        s, v = tok
        self.wait(eng, s, v)

    def wait_toks(self, eng, toks):
        best = {}
        for t in toks:
            if t is None:
                continue
            s, v = t
            if v > best.get(id(s), (s, 0))[1]:
                best[id(s)] = (s, v)
        for s, v in best.values():
            self.wait(eng, s, v)


class DmaQ:

    def __init__(self, ctx, es, nc, name, eng, nlanes):
        self.ctx, self.eng = ctx, eng
        self.lanes = [Sem(es.enter_context(nc.semaphore(name + str(i))), 16)
                      for i in range(nlanes)]
        self.rr = 0

    def dma(self, out, in_, eng=None):
        e = eng or self.eng
        ln = self.lanes[self.rr]
        self.rr = (self.rr + 1) % len(self.lanes)
        if ln.n > 0:
            self.ctx.wait(e, ln, ln.n)
        self.ctx.op(e, "dma_start", out=out, in_=in_, inc=ln)
        return (ln, ln.n)

    def snapshot(self):
        return [(l, l.n) for l in self.lanes if l.n > 0]


def build(T=2048, n_cores=8):
    TC = T // 128
    NTT = max(1, T // 512)
    TTW = min(512, T)
    NQT = T // QT
    NSUB = TTW // 128

    nc = bass.Bass("TRN2", target_bir_lowering=False)

    x_in = nc.dram_tensor("L0_attn_x", [T, D], F32, kind="ExternalInput").ap()
    Wq_d = nc.dram_tensor("L0_attn_Wq", [D, D], F32, kind="ExternalInput").ap()
    Wk_d = nc.dram_tensor("L0_attn_Wk", [D, D], F32, kind="ExternalInput").ap()
    Wv_d = nc.dram_tensor("L0_attn_Wv", [D, D], F32, kind="ExternalInput").ap()
    Wo_d = nc.dram_tensor("L0_attn_Wo", [D, D], F32, kind="ExternalInput").ap()
    Wdkv_d = nc.dram_tensor("L0_attn_W_down_kv", [D, DL], F32, kind="ExternalInput").ap()
    Wupk_d = nc.dram_tensor("L0_attn_W_up_k", [DL, D], F32, kind="ExternalInput").ap()
    Wupv_d = nc.dram_tensor("L0_attn_W_up_v", [DL, D], F32, kind="ExternalInput").ap()
    clipx_d = nc.dram_tensor("L0_ffn_clip_x", [1], F32, kind="ExternalInput").ap()
    clipd_d = nc.dram_tensor("L0_ffn_clip_down", [1], F32, kind="ExternalInput").ap()
    wg_d = nc.dram_tensor("L0_ffn_w_gate", [IH, D], F32, kind="ExternalInput").ap()
    wu_d = nc.dram_tensor("L0_ffn_w_up", [IH, D], F32, kind="ExternalInput").ap()
    wd_d = nc.dram_tensor("L0_ffn_w_down", [D, IH], F32, kind="ExternalInput").ap()
    out_d = nc.dram_tensor("out", [T, D], F32, kind="ExternalOutput").ap()

    scr_g = nc.dram_tensor("scr_g", [NI, 128, D], BF16).ap()
    scr_u = nc.dram_tensor("scr_u", [NI, 128, D], BF16).ap()
    scr_d = nc.dram_tensor("scr_d", [NI, 128, D], BF16).ap()
    scr_h = nc.dram_tensor("scr_h", [NI, 128, T], F32).ap()
    scr_q = nc.dram_tensor("scr_q", [8, 128, T], BF16).ap()
    cc_in = [nc.dram_tensor(f"cc_in{i}", [1, 1], F32).ap() for i in range(4)]
    cc_out = [nc.dram_tensor(f"cc_out{i}", [1, n_cores], F32).ap() for i in range(4)]

    from contextlib import ExitStack
    es = ExitStack()

    X = es.enter_context(nc.sbuf_tensor("x_arena", [128, TC * D], F32)).ap()

    CONST_SZ = 3700
    C = es.enter_context(nc.sbuf_tensor("c_arena", [128, CONST_SZ], F32)).ap()
    _co = [0]

    def calloc(n):
        o_ = _co[0]
        _co[0] += n
        assert _co[0] <= CONST_SZ, _co[0]
        return o_

    identb_o = calloc(64)
    identf_o = calloc(128)
    onesf_o = calloc(128)
    swg_o = calloc(NI)
    swu_o = calloc(NI)
    swd_o = calloc(8)
    combg_o = calloc(NI)
    combu_o = calloc(NI)
    mr_o = calloc(2 * TC)
    ymax_o = calloc(TC)
    hmax_o = calloc(NI * NTT)
    sc_o = calloc(16)
    tmpa_o = calloc(D)
    ytile_o = calloc(D)
    swdb2_o = ytile_o
    row_o = calloc(48)
    nmagic_o = calloc(1)
    magic_o = calloc(1)
    eps_o = calloc(1)
    lnv_o = calloc(TC)
    lnm2_o = calloc(TC)
    swdrow_o = calloc(128)
    rec_o = calloc(2 * QT)
    avtmp_o = calloc(QT)

    def cF(off, n):
        return C[:, off:off + n]

    def cB(off, nbf):
        return C.bitcast(BF16)[:, 2 * off:2 * off + nbf]

    identb = cB(identb_o, 128)
    identf = cF(identf_o, 128)
    onesf = cF(onesf_o, 128)

    SC_R127X, SC_SX, SC_R127H, SC_SH05 = 0, 1, 2, 3

    def scv(i):
        return cF(sc_o + i, 1)

    def r0(col, n=1):
        return C[0:1, row_o + col:row_o + col + n]

    WORK_SZ = 32900
    W = es.enter_context(nc.sbuf_tensor("w_arena", [128, WORK_SZ], F32)).ap()

    def wF(off, n):
        return W[:, off:off + n]

    def wB(off, nbf):
        return W.bitcast(BF16)[:, 2 * off:2 * off + nbf]

    XQT_O = 0
    STG_O = XQT_O + 4 * T
    WLD_O = STG_O + 2048
    WQT_O = WLD_O + 2048
    WQI_O = WQT_O + IH
    SILU_O = WQI_O + IH // 2
    HOUT_O = SILU_O + 1024
    F_END = HOUT_O + 1024
    assert F_END <= WORK_SZ
    HLD_O = 0
    HQ_O = HLD_O + 4096
    OUTB_O = HQ_O + 2 * NI * 256
    SSEL_O = OUTB_O + 1024
    RHS2_O = SSEL_O + 1024
    WDT_O = max(F_END, RHS2_O + 1024)
    assert WDT_O + NI * 512 <= WORK_SZ
    WDQ_O = WORK_SZ - 4 * IH
    assert WDQ_O >= F_END + IH

    Y2T_O = 0
    KX_O = Y2T_O
    VTOK_O = Y2T_O + 4 * T
    VTOK_SZ = (NH * TC * 65 + 1) // 2
    EXPS_O = VTOK_O + VTOK_SZ
    EXPS_SZ = TC * QT
    WBUF_O = EXPS_O + EXPS_SZ
    LATT_O = WBUF_O + 4096
    OUTT_O = LATT_O
    QSTR_O = LATT_O + max(T, 8 * QT)
    TBL_O = QSTR_O + 2048
    KMIX_O = TBL_O + 2 * (T // 2)
    MSK_O = KMIX_O + 1024
    SCM_O = MSK_O + QT
    A_END = SCM_O + 2 * QT
    assert A_END <= WORK_SZ, A_END

    ps_big = es.enter_context(nc.psum_tensor("ps_big", [128, 2048], F32)).ap()
    ps_ava = es.enter_context(nc.psum_tensor("ps_ava", [128, 512], F32)).ap()
    ps_avb = es.enter_context(nc.psum_tensor("ps_avb", [128, 512], F32)).ap()
    ps_tp0f = es.enter_context(nc.psum_tensor("ps_tp0", [128, 512], F32)).ap()
    ps_tp1f = es.enter_context(nc.psum_tensor("ps_tp1", [128, 512], F32)).ap()
    ps_tp0 = ps_tp0f.bitcast(BF16)
    ps_tp1 = ps_tp1f.bitcast(BF16)

    CCS = Sem(es.enter_context(nc.semaphore("CCS")))
    TE = Sem(es.enter_context(nc.semaphore("TE")))
    VE = Sem(es.enter_context(nc.semaphore("VE")))
    SE = Sem(es.enter_context(nc.semaphore("SE")))
    GE = Sem(es.enter_context(nc.semaphore("GE")))

    ctx = Ctx()
    o, wt, dr = ctx.op, ctx.wait, ctx.drain
    ev = {}
    QL = DmaQ(ctx, es, nc, "QL", "sync", 6)
    QS = DmaQ(ctx, es, nc, "QS", "scalar", 5)
    QG = DmaQ(ctx, es, nc, "QG", "gpsimd", 3)

    tk_x = QL.dma(out=X.rearrange("p (tc d) -> p tc d", tc=TC),
                  in_=x_in.rearrange("(tc p) d -> p tc d", p=128))
    tk_cx = QL.dma(out=r0(0), in_=clipx_d.unsqueeze(0))
    tk_cd = QL.dma(out=r0(1), in_=clipd_d.unsqueeze(0))

    o("gpsimd", "memset", identf, 1.0)
    o("gpsimd", "memset", onesf, 1.0)
    dr("gpsimd")
    o("gpsimd", "affine_select", out=identf, in_=identf, pattern=[[1, 128]],
      compare_op=ALU.is_equal, fill=0.0, base=0, channel_multiplier=-1)
    dr("gpsimd")
    o("gpsimd", "tensor_copy", out=identb, in_=identf)
    o("gpsimd", "memset", cF(nmagic_o, 1), -MAGIC)
    o("gpsimd", "memset", cF(magic_o, 1), MAGIC)
    o("gpsimd", "memset", cF(eps_o, 1), EPS)
    dr("gpsimd")
    o("gpsimd", "engine_nop", inc=GE)
    g_consts = GE.n
    wt("scalar", GE, g_consts)
    wt("vector", GE, g_consts)
    ctx.wait_toks("scalar", [tk_cx, tk_cd])
    o("scalar", "activation", out=r0(2, 2), in_=r0(0, 2), func=AF.Sigmoid, inc=SE)
    ev["sig"] = SE.n

    def quant3(src, dst_bf, rec_pp, tmp):
        o("vector", "tensor_scalar", out=tmp, in0=src, scalar1=rec_pp,
          scalar2=QHI, op0=ALU.mult, op1=ALU.min)
        dr("vector")
        o("vector", "tensor_scalar", out=tmp, in0=tmp, scalar1=QLO,
          scalar2=MAGIC, op0=ALU.max, op1=ALU.add, inc=VE)
        wt("scalar", VE, VE.n)
        o("scalar", "activation", out=dst_bf, in_=tmp, func=AF.Identity,
          bias=cF(nmagic_o, 1), inc=SE)
        return SE.n

    def rowquant(wld, n, sw_slot, t0, t1):
        o("vector", "reduce_max", out=t0, in_=wld, axis=AX.X,
          apply_absolute_value=True)
        dr("vector")
        o("vector", "reciprocal", out=t1, in_=t0)
        dr("vector")
        o("vector", "tensor_scalar_mul", out=t1, in0=t1, scalar1=127.0)
        o("vector", "tensor_scalar_mul", out=sw_slot, in0=t0, scalar1=1.0 / 127.0)
        dr("vector")

    def prep_gu(w_dram, scr, sw_off, tag):
        for ic in range(NI):
            par = ic % 2
            wld = wF(WLD_O + par * D, D)
            if ic >= 2:
                wt("gpsimd", SE, ev[(tag, "w1", ic - 2)])
            elif tag == "u":
                wt("gpsimd", SE, ev[("g", "w1", NI - 2 + ic)])
            tk = QG.dma(out=wld, in_=w_dram[ic * 128:(ic + 1) * 128, :])
            ctx.wait_tok("vector", tk)
            t0 = wF(WQT_O + par * 2, 1)
            t1 = wF(WQT_O + par * 2 + 1, 1)
            if ic >= 2:
                wt("vector", SE, ev[(tag, "w1", ic - 2)])
            elif tag == "u":
                wt("vector", SE, ev[("g", "w1", NI - 2 + ic)])
            o("vector", "reduce_max", out=t0, in_=wld, axis=AX.X,
              apply_absolute_value=True)
            dr("vector")
            o("vector", "reciprocal", out=t1, in_=t0)
            dr("vector")
            o("vector", "tensor_scalar_mul", out=t1, in0=t1, scalar1=127.0)
            o("vector", "tensor_scalar_mul", out=cF(sw_off + ic, 1), in0=t0,
              scalar1=1.0 / 127.0, inc=VE)
            ev[(tag, "wq", ic)] = VE.n
            wq = wF(WQT_O + 4 + par * D, D)
            wt("scalar", VE, VE.n)
            o("scalar", "activation", out=wq, in_=wld, func=AF.Identity,
              scale=t1, bias=cF(magic_o, 1), inc=SE)
            ev[(tag, "w1", ic)] = SE.n
            dr("scalar")
            if ic >= 2:
                wt("scalar", TE, ev[(tag, "tp", ic - 2)])
            wqi = wB(WQI_O + par * 512, D)
            o("scalar", "activation", out=wqi, in_=wq, func=AF.Identity,
              bias=cF(nmagic_o, 1), inc=SE)
            ev[(tag, "wqi", ic)] = SE.n
            ps = ps_tp0 if par == 0 else ps_tp1
            wt("tensor", SE, SE.n)
            if ic < 2:
                wt("tensor", VE, VE.n)
            if ic >= 2:
                wt("tensor", SE, ev[(tag, "ev", ic - 2)])
            for hc in range(8):
                o("tensor", "transpose", out=ps[:, hc * 128:(hc + 1) * 128],
                  in_=wqi[:, hc * 128:(hc + 1) * 128], identity=identb,
                  inc=(TE if hc == 7 else None))
            ev[(tag, "tp", ic)] = TE.n
            stg = wB(STG_O + (1024 if tag == "u" else 0) + par * 512, D)
            wt("scalar", TE, TE.n)
            if ic >= 2:
                ctx.wait_tok("scalar", ev[(tag, "st", ic - 2)])
            o("scalar", "activation", out=stg, in_=ps[:], func=AF.Copy, inc=SE)
            ev[(tag, "ev", ic)] = SE.n
            wt("scalar", SE, SE.n)
            ev[(tag, "st", ic)] = QS.dma(out=scr[ic], in_=stg)

    def prep_down():
        wdqi = wB(WDQ_O, 8 * IH)
        for hc in range(8):
            wld = wF(WQT_O, IH) if hc % 2 == 0 else wF(F_END, IH)
            par = hc % 2
            if hc >= 2:
                wt("gpsimd", SE, ev[("d", "wqi", hc - 2)])
            else:
                wt("gpsimd", SE, ev[("u", "wqi", NI - 1)])
                wt("gpsimd", VE, ev[("u", "cmb", NI - 1)])
            tk = QG.dma(out=wld, in_=wd_d[hc * 128:(hc + 1) * 128, :])
            ctx.wait_tok("vector", tk)
            t0 = wF(SILU_O + par * 2, 1)
            t1 = wF(SILU_O + par * 2 + 1, 1)
            if hc >= 2:
                wt("vector", SE, ev[("d", "w1", hc - 2)])
            o("vector", "reduce_max", out=t0, in_=wld, axis=AX.X,
              apply_absolute_value=True)
            dr("vector")
            o("vector", "reciprocal", out=t1, in_=t0)
            dr("vector")
            o("vector", "tensor_scalar_mul", out=t1, in0=t1, scalar1=127.0)
            o("vector", "tensor_scalar_mul", out=cF(swd_o + hc, 1), in0=t0,
              scalar1=1.0 / 127.0, inc=VE)
            ev[("d", "wq", hc)] = VE.n
            wt("scalar", VE, VE.n)
            o("scalar", "activation", out=wld, in_=wld, func=AF.Identity,
              scale=t1, bias=cF(magic_o, 1), inc=SE)
            ev[("d", "w1", hc)] = SE.n
            dr("scalar")
            o("scalar", "activation", out=wdqi[:, hc * IH:(hc + 1) * IH], in_=wld,
              func=AF.Identity, bias=cF(nmagic_o, 1), inc=SE)
            ev[("d", "wqi", hc)] = SE.n
        ev["wdq"] = SE.n
        for ic in range(NI):
            par = ic % 2
            ps = ps_tp0 if par == 0 else ps_tp1
            wt("tensor", SE, ev["wdq"])
            if ic < 2:
                wt("tensor", VE, VE.n)
            if ic >= 2:
                wt("tensor", SE, ev[("d", "ev", ic - 2)])
            for hc in range(8):
                o("tensor", "transpose", out=ps[:, hc * 128:(hc + 1) * 128],
                  in_=wdqi[:, hc * IH + ic * 128:hc * IH + (ic + 1) * 128],
                  identity=identb, inc=(TE if hc == 7 else None))
            ev[("d", "tp", ic)] = TE.n
            stg = wB(WLD_O + (ic % 2) * 512, D)
            wt("scalar", TE, TE.n)
            if ic >= 2:
                ctx.wait_tok("scalar", ev[("d", "st", ic - 2)])
            o("scalar", "activation", out=stg, in_=ps[:], func=AF.Copy, inc=SE)
            ev[("d", "ev", ic)] = SE.n
            wt("scalar", SE, SE.n)
            ev[("d", "st", ic)] = QS.dma(out=scr_d[ic], in_=stg)
        ev[("d", "done")] = [ev[("d", "st", i)] for i in range(NI)]
        ev["prepd_T"] = TE.n

    def layernorm(ph, absmax, ybf_consumer=None):
        ctx.wait_tok("vector", tk_x)
        ctx.wait_tok("scalar", tk_x)
        mcols = cF(mr_o, TC)
        rcols = cF(mr_o + TC, TC)
        vcols = cF(lnv_o, TC)
        for tc in range(TC):
            o("vector", "reduce_sum", out=cF(mr_o + tc, 1),
              in_=X[:, tc * D:(tc + 1) * D], axis=AX.X)
        dr("vector")
        o("vector", "tensor_scalar_mul", out=mcols, in0=mcols, scalar1=1.0 / D,
          inc=VE)
        for tc in range(TC):
            par = tc % 2
            sq = cF(tmpa_o, D) if par == 0 else cF(ytile_o, D)
            if tc >= 2:
                wt("scalar", VE, ev[(ph, "lnsq", tc - 2)])
            o("scalar", "activation", out=sq, in_=X[:, tc * D:(tc + 1) * D],
              func=AF.Square, inc=SE)
            wt("vector", SE, SE.n)
            o("vector", "reduce_sum", out=cF(lnv_o + tc, 1), in_=sq, axis=AX.X,
              inc=VE)
            ev[(ph, "lnsq", tc)] = VE.n
        dr("vector")
        m2c = cF(lnm2_o, TC)
        wt("scalar", VE, VE.n)
        o("scalar", "activation", out=m2c, in_=mcols, func=AF.Square, inc=SE)
        wt("vector", SE, SE.n)
        o("vector", "scalar_tensor_tensor", out=vcols, in0=vcols, scalar=1.0 / D,
          in1=m2c, op0=ALU.mult, op1=ALU.subtract)
        dr("vector")
        o("vector", "tensor_scalar_add", out=vcols, in0=vcols,
          scalar1=cF(eps_o, 1), inc=VE)
        wt("scalar", VE, VE.n)
        o("scalar", "activation", out=vcols, in_=vcols, func=AF.Sqrt, inc=SE)
        wt("vector", SE, SE.n)
        o("vector", "reciprocal", out=rcols, in_=vcols)
        dr("vector")
        if absmax:
            for tc in range(TC):
                yt = cF(ytile_o, D) if tc % 2 else cF(tmpa_o, D)
                o("vector", "tensor_scalar", out=yt,
                  in0=X[:, tc * D:(tc + 1) * D], scalar1=cF(mr_o + tc, 1),
                  scalar2=cF(mr_o + TC + tc, 1), op0=ALU.subtract, op1=ALU.mult)
                dr("vector")
                o("vector", "reduce_max", out=cF(ymax_o + tc, 1), in_=yt,
                  axis=AX.X, apply_absolute_value=True, inc=VE)
            dr("vector")
        if ybf_consumer is not None:
            for tc in range(TC):
                if tc >= 1:
                    wt("vector", TE, ev[(ph, "ytp", tc - 1)])
                yt = cF(ytile_o, D)
                o("vector", "tensor_scalar", out=yt,
                  in0=X[:, tc * D:(tc + 1) * D], scalar1=cF(mr_o + tc, 1),
                  scalar2=cF(mr_o + TC + tc, 1), op0=ALU.subtract, op1=ALU.mult)
                dr("vector")
                ybf = cB(ytile_o, D)
                o("vector", "tensor_copy", out=ybf, in_=yt, inc=VE)
                ybf_consumer(tc, VE.n)
        ev[(ph, "ln")] = VE.n

    def global_scale(idx, max_off, ncols, sig_col, r127_slot, s_slot, s_mult):
        mx = cF(sc_o + 10, 1)
        dr("vector")
        o("vector", "reduce_max", out=mx, in_=cF(max_off, ncols), axis=AX.X, inc=VE)
        wt("tensor", VE, VE.n)
        wt("tensor", VE, ev.get("avb_free", 0))
        o("tensor", "transpose", out=ps_ava[0:1, 0:128], in_=mx, identity=identf,
          inc=TE)
        wt("vector", TE, TE.n)
        o("vector", "reduce_max", out=r0(4), in_=ps_ava[0:1, 0:128], axis=AX.X,
          inc=VE)
        wt("gpsimd", VE, VE.n)
        tk1 = QG.dma(out=cc_in[idx], in_=r0(4))
        ctx.wait_tok("gpsimd", tk1)
        o("gpsimd", "collective_compute", "AllGather", ALU.bypass,
          replica_groups=[list(range(n_cores))],
          ins=[cc_in[idx]], outs=[cc_out[idx]], inc=CCS)
        wt("gpsimd", CCS, CCS.n)
        tk2 = QG.dma(out=r0(8, n_cores), in_=cc_out[idx])
        ctx.wait_tok("vector", tk2)
        o("vector", "reduce_max", out=r0(5), in_=r0(8, n_cores), axis=AX.X)
        dr("vector")
        wt("vector", SE, ev["sig"])
        o("vector", "tensor_mul", out=r0(6), in0=r0(5), in1=r0(sig_col))
        dr("vector")
        o("vector", "reciprocal", out=r0(7), in_=r0(6))
        dr("vector")
        o("vector", "tensor_scalar_mul", out=r0(16), in0=r0(7), scalar1=127.0)
        o("vector", "tensor_scalar_mul", out=r0(17), in0=r0(6),
          scalar1=float(s_mult / 127.0), inc=VE)
        wt("tensor", VE, VE.n)
        o("tensor", "matmul", out=ps_ava[:, 128:129], lhsT=onesf[0:1, :],
          rhs=r0(16), start=True, stop=True)
        o("tensor", "matmul", out=ps_ava[:, 129:130], lhsT=onesf[0:1, :],
          rhs=r0(17), start=True, stop=True, inc=TE)
        wt("vector", TE, TE.n)
        o("vector", "tensor_copy", out=scv(r127_slot), in_=ps_ava[:, 128:129])
        o("vector", "tensor_copy", out=scv(s_slot), in_=ps_ava[:, 129:130], inc=VE)
        dr("vector")
        ev["avb_free"] = VE.n
        return VE.n

    def ffn_pre(p, cc_base):
        ph = f"f{p}"
        layernorm(ph, absmax=True)
        v_sc = global_scale(cc_base, ymax_o, TC, 2, SC_R127X, SC_SX, 1.0)



    def ffn_mm(p, cc_base):
        ph = f"f{p}"
        if p == 2:
            dr("vector")
            o("vector", "tensor_tensor", out=cF(combg_o, NI), in0=cF(swg_o, NI),
              in1=scv(SC_SX).broadcast_to([128, NI]), op=ALU.mult)
            o("vector", "tensor_tensor", out=cF(combu_o, NI), in0=cF(swu_o, NI),
              in1=scv(SC_SX).broadcast_to([128, NI]), op=ALU.mult, inc=VE)
            dr("vector")
            ev[(ph, "vcomb")] = VE.n
        xqT = wB(XQT_O, 8 * T)
        prev_sQ = 0
        for tc in range(TC):
            xt = X[:, tc * D:(tc + 1) * D]
            qt = cF(tmpa_o, D)
            if tc == 0:
                wt("vector", TE, TE.n)
            if tc >= 1:
                wt("vector", TE, ev[(ph, "ytp", tc - 1)])
                wt("vector", SE, prev_sQ)
            o("vector", "tensor_scalar", out=qt, in0=xt,
              scalar1=cF(mr_o + tc, 1), scalar2=cF(mr_o + TC + tc, 1),
              op0=ALU.subtract, op1=ALU.mult)
            dr("vector")
            sQ = quant3(qt, cB(ytile_o, D), scv(SC_R127X), qt)
            prev_sQ = sQ
            par = tc % 2
            ps = ps_tp0 if par == 0 else ps_tp1
            wt("tensor", SE, sQ)
            if tc == 0:
                wt("tensor", VE, VE.n)
            if tc >= 2:
                wt("tensor", VE, ev[(ph, "xqev", tc - 2)])
            for hc in range(8):
                o("tensor", "transpose", out=ps[:, hc * 128:(hc + 1) * 128],
                  in_=cB(ytile_o, D)[:, hc * 128:(hc + 1) * 128], identity=identb,
                  inc=(TE if hc == 7 else None))
            ev[(ph, "ytp", tc)] = TE.n
            wt("vector", TE, TE.n)
            out_ap = AP(tensor=W.bitcast(BF16).tensor,
                        offset=2 * XQT_O + tc * 128,
                        ap=[[2 * WORK_SZ, 128], [T, 8], [1, 128]])
            o("vector", "tensor_copy", out=out_ap,
              in_=ps[:].rearrange("p (h j) -> p h j", h=8), inc=VE)
            ev[(ph, "xqev", tc)] = VE.n
        ev[(ph, "xq")] = VE.n
        xqT = wB(XQT_O, 8 * T)

        def xq_ap(hc, t0, tw):
            return xqT.rearrange("p (h t) -> p h t", h=8)[:, hc, t0:t0 + tw]

        def prep_chunk(tag, ic):
            par = ic % 2
            w_dram, scr, sw_off = ((wg_d, scr_g, swg_o) if tag == "g" else
                                   (wu_d, scr_u, swu_o))
            wldo = (WLD_O if tag == "g" else WQT_O + 4) + par * D
            wld = wF(wldo, D)
            tsl = WQT_O + 2052 + (0 if tag == "g" else 4) + par * 2
            t0, t1 = wF(tsl, 1), wF(tsl + 1, 1)
            if ic >= 2:
                wt("gpsimd", SE, ev[(tag, "wqi", ic - 2)])
            tk = QG.dma(out=wld, in_=w_dram[ic * 128:(ic + 1) * 128, :])
            ctx.wait_tok("vector", tk)
            if ic >= 2:
                wt("vector", SE, ev[(tag, "w1", ic - 2)])
            o("vector", "reduce_max", out=t0, in_=wld, axis=AX.X,
              apply_absolute_value=True)
            dr("vector")
            o("vector", "reciprocal", out=t1, in_=t0)
            dr("vector")
            o("vector", "tensor_scalar_mul", out=t1, in0=t1, scalar1=127.0)
            o("vector", "tensor_scalar_mul", out=cF(sw_off + ic, 1), in0=t0,
              scalar1=1.0 / 127.0, inc=VE)
            v_t1 = VE.n
            cmb_o = combg_o if tag == "g" else combu_o
            o("vector", "scalar_tensor_tensor", out=cF(cmb_o + ic, 1), in0=t0,
              scalar=1.0 / 127.0, in1=scv(SC_SX), op0=ALU.mult, op1=ALU.mult,
              inc=VE)
            ev[(tag, "cmb", ic)] = VE.n
            dr("vector")
            wt("scalar", VE, v_t1)
            ctx.wait_tok("scalar", tk)
            o("scalar", "activation", out=wld, in_=wld, func=AF.Identity,
              scale=t1, bias=cF(magic_o, 1), inc=SE)
            ev[(tag, "w1", ic)] = SE.n
            dr("scalar")
            if tag == "u":
                wt("scalar", TE, ev[("g", "tp", ic)])
            elif ic >= 2:
                wt("scalar", TE, ev[("u", "tp", ic - 2)])
            wqi = wB(WQI_O + par * 512, D)
            o("scalar", "activation", out=wqi, in_=wld, func=AF.Identity,
              bias=cF(nmagic_o, 1), inc=SE)
            ev[(tag, "wqi", ic)] = SE.n
            ps = ps_tp0 if tag == "g" else ps_tp1
            wt("tensor", SE, SE.n)
            if ic == 0:
                wt("tensor", VE,
                   ev[(ph, "xqev", TC - 2 if tag == "g" else TC - 1)])
            else:
                wt("tensor", SE, ev[(tag, "evac", ic - 1)])
            for hc in range(8):
                o("tensor", "transpose", out=ps[:, hc * 128:(hc + 1) * 128],
                  in_=wqi[:, hc * 128:(hc + 1) * 128], identity=identb,
                  inc=(TE if hc == 7 else None))
            ev[(tag, "tp", ic)] = TE.n
            stg = wB(STG_O + (1024 if tag == "u" else 0) + par * 512, D)
            wt("scalar", TE, TE.n)
            if ic >= 2:
                ctx.wait_tok("scalar", ev[(tag, "st", ic - 2)])
                wt("scalar", TE, ev[(ph, "gumm", ic - 2)])
            o("scalar", "activation", out=stg, in_=ps[:], func=AF.Copy, inc=SE)
            ev[(tag, "evac", ic)] = SE.n
            wt("scalar", SE, SE.n)
            ev[(tag, "st", ic)] = QS.dma(out=scr[ic], in_=stg)

        for ic in range(NI):
            par = ic % 2
            stg_g = wB(STG_O + par * 512, D)
            stg_u = wB(STG_O + 1024 + par * 512, D)
            if p == 1:
                prep_chunk("g", ic)
                prep_chunk("u", ic)
                wt("tensor", SE, ev[("g", "evac", ic)])
                wt("tensor", SE, ev[("u", "evac", ic)])
            else:
                ctx.wait_tok("sync", ev[("g", "st", ic)])
                ctx.wait_tok("sync", ev[("u", "st", ic)])
                if ic >= 2:
                    wt("sync", TE, ev[(ph, "gumm", ic - 2)])
                if ic < 2:
                    wt("sync", TE, ev[("f1", "guload_ok", ic)])
                tk_sg = QL.dma(out=stg_g, in_=scr_g[ic])
                tk_su = QL.dma(out=stg_u, in_=scr_u[ic])
                ctx.wait_toks("tensor", [tk_sg, tk_su])
            for tt in range(NTT):
                u = ic * NTT + tt
                gslot = ps_big[:, (u % 2) * 512:(u % 2) * 512 + TTW]
                uslot = ps_big[:, 1024 + (u % 2) * 512:1024 + (u % 2) * 512 + TTW]
                if ic == 0:
                    wt("tensor", VE,
                       ev[(ph, "xqev", min(TC - 1, (tt + 1) * NSUB - 1))])
                if u >= 2:
                    pic, ptt = divmod(u - 2, NTT)
                    wt("tensor", VE, ev[(ph, "hout", pic, ptt)])
                for hc in range(8):
                    o("tensor", "matmul", out=gslot,
                      lhsT=stg_g[:, hc * 128:(hc + 1) * 128],
                      rhs=xq_ap(hc, tt * TTW, TTW), start=(hc == 0), stop=(hc == 7))
                for hc in range(8):
                    o("tensor", "matmul", out=uslot,
                      lhsT=stg_u[:, hc * 128:(hc + 1) * 128],
                      rhs=xq_ap(hc, tt * TTW, TTW), start=(hc == 0), stop=(hc == 7),
                      inc=(TE if hc == 7 else None))
                ev[(ph, "gumm", ic)] = TE.n
                par2 = u % 2
                silu = wF(SILU_O + par2 * 512, TTW)
                hout = wF(HOUT_O + par2 * 512, TTW)
                wt("scalar", TE, TE.n)
                wt("scalar", VE, ev[("g", "cmb", ic)] if p == 1 else
                   ev[(ph, "vcomb")])
                if u >= 2:
                    pic, ptt = divmod(u - 2, NTT)
                    wt("scalar", VE, ev[(ph, "hout", pic, ptt)])
                o("scalar", "activation", out=silu, in_=gslot, func=AF.Silu,
                  scale=cF(combg_o + ic, 1), inc=SE)
                wt("vector", SE, SE.n)
                if u >= 2:
                    pic, ptt = divmod(u - 2, NTT)
                    ctx.wait_tok("vector", ev[(ph, "hdma", pic, ptt)])
                o("vector", "scalar_tensor_tensor", out=hout, in0=uslot,
                  scalar=cF(combu_o + ic, 1), in1=silu, op0=ALU.mult,
                  op1=ALU.mult, inc=VE)
                ev[(ph, "hout", ic, tt)] = VE.n
                dr("vector")
                o("vector", "reduce_max", out=cF(hmax_o + u, 1), in_=hout,
                  axis=AX.X, apply_absolute_value=True, inc=VE)
                wt("scalar", VE, VE.n)
                if p == 2 and u == 0:
                    ctx.wait_toks("scalar", QL.snapshot())
                ev[(ph, "hdma", ic, tt)] = QS.dma(
                    out=scr_h[ic, :, tt * TTW:(tt + 1) * TTW], in_=hout)
        ev[(ph, "h_all")] = QS.snapshot()
        if p == 1:
            ev[("f1", "guload_ok", 0)] = TE.n
            ev[("f1", "guload_ok", 1)] = TE.n

    def ffn_down(p, cc_base):
        ph = f"f{p}"
        ctx.wait_toks("sync", ev[("d", "done")])
        wt("sync", TE, ev["prepd_T"])
        wdt_toks = []
        for ic in range(NI):
            wdt_toks.append(QL.dma(out=wB(WDT_O + ic * 512, D), in_=scr_d[ic]))

        v_sc2 = global_scale(cc_base + 1, hmax_o, NI * NTT, 3, SC_R127H,
                             SC_SH05, 0.5)

        Ssel = wF(SSEL_O, 1024)
        ctx.wait_toks("gpsimd", ev[(ph, "h_all")])
        wt("gpsimd", VE, VE.n)
        o("gpsimd", "memset", Ssel, 1.0)
        dr("gpsimd")
        o("gpsimd", "affine_select", out=Ssel, in_=Ssel, pattern=[[1, 1024]],
          compare_op=ALU.is_ge, fill=0.0, base=0, channel_multiplier=-128)
        dr("gpsimd")
        o("gpsimd", "affine_select", out=Ssel, in_=Ssel, pattern=[[-1, 1024]],
          compare_op=ALU.is_ge, fill=0.0, base=127, channel_multiplier=128)
        dr("gpsimd")
        o("gpsimd", "engine_nop", inc=GE)
        g_sel = GE.n
        wt("tensor", VE, v_sc2)
        o("tensor", "transpose", out=ps_avb[0:8, 0:128], in_=cF(swd_o, 8),
          identity=identf, inc=TE)
        wt("vector", TE, TE.n)
        o("vector", "tensor_copy", out=cF(swdrow_o, 128)[0:8, :],
          in_=ps_avb[0:8, 0:128], inc=VE)
        dr("vector")
        rhs2 = wF(RHS2_O, 1024)
        wt("vector", GE, g_sel)
        o("vector", "tensor_tensor", out=rhs2[0:8, :].rearrange(
            "p (r j) -> p r j", r=8), in0=Ssel[0:8, :].rearrange(
            "p (r j) -> p r j", r=8),
          in1=cF(swdrow_o, 128)[0:8, :].unsqueeze(1).broadcast_to([8, 8, 128]),
          op=ALU.mult, inc=VE)
        dr("vector")
        wt("tensor", VE, VE.n)
        o("tensor", "matmul", out=ps_ava[:, 0:512],
          lhsT=onesf[0:8, :], rhs=rhs2[0:8, 0:512],
          start=True, stop=True)
        o("tensor", "matmul", out=ps_avb[:, 0:512],
          lhsT=onesf[0:8, :], rhs=rhs2[0:8, 512:1024],
          start=True, stop=True, inc=TE)
        t_swdb2 = TE.n
        wt("vector", TE, t_swdb2)
        o("vector", "tensor_scalar_mul", out=cF(swdb2_o, 512),
          in0=ps_ava[:, 0:512], scalar1=scv(SC_SH05))
        o("vector", "tensor_scalar_mul", out=cF(swdb2_o + 512, 512),
          in0=ps_avb[:, 0:512], scalar1=scv(SC_SH05), inc=VE)
        dr("vector")
        v_swdb2 = VE.n
        ev["avb_free"] = VE.n

        ngrp = (NI + 3) // 4
        ctx.wait_toks("scalar", ev[(ph, "h_all")])

        def emit_loadquant_group(tt, g):
            i0, i1 = g * 4, min(NI, g * 4 + 4)
            gu = tt * ngrp + g
            par = gu % 2
            hld = wF(HLD_O + par * 2048, (i1 - i0) * TTW)
            if gu >= 2:
                pt_, pg_ = divmod(gu - 2, ngrp)
                wt("sync", SE, ev[(ph, "hq", pt_, pg_)])
            ctx.wait_toks("sync", ev[(ph, "h_all")])
            if tt == 0 and g == 0:
                wt("sync", TE, ev[(ph, "gumm", NI - 1)])
                wt("sync", VE, VE.n)
                wt("sync", SE, SE.n)
                if p == 1:
                    wt("sync", VE, ev[("d", "wq", 7)])
                    wt("sync", SE, ev[("d", "wqi", 7)])
            tk_h = QL.dma(
                out=hld.rearrange("p (i t) -> p i t", i=i1 - i0),
                in_=scr_h[i0:i1, :, tt * TTW:(tt + 1) * TTW].transpose([1, 0, 2]))
            ctx.wait_tok("vector", tk_h)
            if g == 0:
                wt("vector", VE, v_sc2)
                if tt >= 2:
                    wt("scalar", TE, ev[(ph, "dmm", tt - 2)])
                    wt("vector", TE, ev[(ph, "dmm", tt - 2)])
            for ii in range(i0, i1):
                src_ = wF(HLD_O + par * 2048 + (ii - i0) * TTW, TTW)
                if (tt, g, ii) != (0, 0, 0):
                    wt("vector", SE, SE.n)
                quant3(src_, wB(HQ_O + (tt % 2) * NI * 256 + ii * 256, TTW),
                       scv(SC_R127H), cF(tmpa_o, TTW))
            ev[(ph, "hq", tt, g)] = SE.n

        def emit_loadquant(tt):
            for g in range(ngrp):
                emit_loadquant_group(tt, g)
            ev[(ph, "hqtt", tt)] = SE.n

        def emit_mms(tt):
            hq0 = HQ_O + (tt % 2) * NI * 256
            for tsub in range(NSUB):
                tc = tt * NSUB + tsub
                for hh in range(2):
                    k = tc * 2 + hh
                    slot = ps_big[:, (k % 3) * 512:(k % 3) * 512 + 512]
                    if k >= 3:
                        wt("tensor", VE, ev[(ph, "dres", (k - 3) // 2, (k - 3) % 2)])
                    wt("tensor", SE, ev[(ph, "hqtt", tt)])
                    if tt == 0 and k == 0:
                        ctx.wait_toks("tensor", wdt_toks)
                    for ii in range(NI):
                        o("tensor", "matmul", out=slot,
                          lhsT=wB(hq0 + ii * 256, TTW)[:, tsub * 128:(tsub + 1) * 128],
                          rhs=wB(WDT_O + ii * 512, D)[:, hh * 512:(hh + 1) * 512],
                          start=(ii == 0), stop=(ii == NI - 1),
                          inc=(TE if ii == NI - 1 else None))
                    ev[(ph, "dmm", tt)] = TE.n
                    par3 = k % 2
                    ob = wF(OUTB_O + par3 * 512, 512)
                    wt("vector", TE, TE.n)
                    wt("vector", VE, v_swdb2)
                    if p == 2 and k >= 2:
                        ctx.wait_tok("vector", ev[(ph, "odma", k - 2)])
                    o("vector", "tensor_tensor", out=ob, in0=slot,
                      in1=cF(swdb2_o + hh * 512, 512), op=ALU.mult)
                    dr("vector")
                    xsl = X[:, tc * D + hh * 512:tc * D + (hh + 1) * 512]
                    o("vector", "tensor_add", out=xsl, in0=xsl, in1=ob, inc=VE)
                    ev[(ph, "dres", tc, hh)] = VE.n
                    dr("vector")
                    if p == 2:
                        wt("scalar", VE, VE.n)
                        ev[(ph, "odma", k)] = QS.dma(
                            out=out_d[tc * 128:(tc + 1) * 128,
                                      hh * 512:(hh + 1) * 512],
                            in_=xsl)

        def emit_unit(tt, tsub, hh):
            hq0 = HQ_O + (tt % 2) * NI * 256
            tc = tt * NSUB + tsub
            k = tc * 2 + hh
            slot = ps_big[:, (k % 3) * 512:(k % 3) * 512 + 512]
            if k >= 3:
                wt("tensor", VE, ev[(ph, "dres", (k - 3) // 2, (k - 3) % 2)])
            wt("tensor", SE, ev[(ph, "hqtt", tt)])
            if tt == 0 and k == 0:
                ctx.wait_toks("tensor", wdt_toks)
            for ii in range(NI):
                o("tensor", "matmul", out=slot,
                  lhsT=wB(hq0 + ii * 256, TTW)[:, tsub * 128:(tsub + 1) * 128],
                  rhs=wB(WDT_O + ii * 512, D)[:, hh * 512:(hh + 1) * 512],
                  start=(ii == 0), stop=(ii == NI - 1),
                  inc=(TE if ii == NI - 1 else None))
            ev[(ph, "dmm", tt)] = TE.n
            par3 = k % 2
            ob = wF(OUTB_O + par3 * 512, 512)
            wt("vector", TE, TE.n)
            wt("vector", VE, v_swdb2)
            if p == 2 and k >= 2:
                ctx.wait_tok("vector", ev[(ph, "odma", k - 2)])
            o("vector", "tensor_tensor", out=ob, in0=slot,
              in1=cF(swdb2_o + hh * 512, 512), op=ALU.mult)
            dr("vector")
            xsl = X[:, tc * D + hh * 512:tc * D + (hh + 1) * 512]
            o("vector", "tensor_add", out=xsl, in0=xsl, in1=ob, inc=VE)
            ev[(ph, "dres", tc, hh)] = VE.n
            dr("vector")
            if p == 2:
                wt("scalar", VE, VE.n)
                ev[(ph, "odma", k)] = QS.dma(
                    out=out_d[tc * 128:(tc + 1) * 128,
                              hh * 512:(hh + 1) * 512],
                    in_=xsl)

        units = [(tsub, hh) for tsub in range(NSUB) for hh in range(2)]
        emit_loadquant(0)
        for tt in range(NTT):
            if tt + 1 < NTT:
                nu = len(units)
                for gi in range(ngrp):
                    emit_loadquant_group(tt + 1, gi)
                    for (tsub, hh) in units[gi * nu // ngrp:(gi + 1) * nu // ngrp]:
                        emit_unit(tt, tsub, hh)
                ev[(ph, "hqtt", tt + 1)] = SE.n
            else:
                for (tsub, hh) in units:
                    emit_unit(tt, tsub, hh)
        ev[(ph, "down_v")] = VE.n
        ev[(ph, "down_T")] = TE.n
        if p == 2:
            ev["outdma"] = QS.snapshot()

    def attn_block():
        ph = "a"
        y2T = wB(Y2T_O, 8 * T)
        kX = wB(KX_O, 8 * T)
        EQA = wB(TBL_O, T)
        EQB = wB(TBL_O + T // 2, T)
        I16 = W.bitcast(mybir.dt.int16)
        MB0 = I16[:, 2 * MSK_O:2 * MSK_O + QT]
        MB1 = I16[:, 2 * MSK_O + QT:2 * MSK_O + 2 * QT]
        kmix = wB(KMIX_O, 8 * 256)
        scm = wB(SCM_O, 4 * QT)
        wbuf = wB(WBUF_O, 8 * D)
        exps = wB(EXPS_O, 2 * TC * QT)

        dtmp = wF(WBUF_O, T)
        wt("gpsimd", TE, ev[("f1", "down_T")])
        wt("gpsimd", VE, ev[("f1", "down_v")])
        o("gpsimd", "iota", dtmp, pattern=[[1, T]], base=0,
          channel_multiplier=0, allow_small_or_imprecise_dtypes=True)
        o("gpsimd", "memset", MB0, 1)
        o("gpsimd", "memset", MB1, 1)
        dr("gpsimd")
        o("gpsimd", "affine_select", out=MB0, in_=MB0, pattern=[[1, QT]],
          compare_op=ALU.is_ge, fill=0.0, base=-1, channel_multiplier=-1)
        o("gpsimd", "affine_select", out=MB1, in_=MB1, pattern=[[1, QT]],
          compare_op=ALU.is_ge, fill=0.0, base=-129, channel_multiplier=-1)
        dr("gpsimd")
        o("gpsimd", "engine_nop", inc=GE)
        wt("scalar", GE, GE.n)
        o("scalar", "activation", out=EQA, in_=dtmp, func=AF.Exp, scale=LAM)
        o("scalar", "activation", out=EQB, in_=dtmp, func=AF.Exp, scale=-LAM,
          inc=SE)
        ev["tbl"] = SE.n

        def y2_consumer(tc, vtag):
            par = tc % 2
            ps = ps_tp0 if par == 0 else ps_tp1
            wt("tensor", VE, vtag)
            if tc == 0:
                wt("tensor", VE, VE.n)
            if tc >= 2:
                wt("tensor", VE, ev[(ph, "y2ev", tc - 2)])
            for hc in range(8):
                o("tensor", "transpose", out=ps[:, hc * 128:(hc + 1) * 128],
                  in_=cB(ytile_o, D)[:, hc * 128:(hc + 1) * 128], identity=identb,
                  inc=(TE if hc == 7 else None))
            ev[(ph, "ytp", tc)] = TE.n
            wt("vector", TE, TE.n)
            out_ap = AP(tensor=W.bitcast(BF16).tensor,
                        offset=2 * Y2T_O + tc * 128,
                        ap=[[2 * WORK_SZ, 128], [T, 8], [1, 128]])
            o("vector", "tensor_copy", out=out_ap,
              in_=ps[:].rearrange("p (h j) -> p h j", h=8), inc=VE)
            ev[(ph, "y2ev", tc)] = VE.n

        layernorm(ph, absmax=False, ybf_consumer=y2_consumer)
        ev["y2T"] = VE.n

        def y2_ap(kc, t0, tw):
            return y2T.rearrange("p (k t) -> p k t", k=8)[:, kc, t0:t0 + tw]

        wt("gpsimd", VE, ev["y2T"])
        wt("gpsimd", SE, ev["tbl"])
        tk_wq = QG.dma(out=wbuf.rearrange("p (k d) -> p k d", k=8),
                       in_=Wq_d.rearrange("(k p) d -> p k d", p=128))
        ctx.wait_tok("tensor", tk_wq)
        wt("tensor", VE, ev["y2T"])
        QS0 = KMIX_O
        for dc in range(8):
            for tt in range(NTT):
                u = dc * NTT + tt
                slot = ps_big[:, (u % 4) * 512:(u % 4) * 512 + TTW]
                if u >= 4:
                    wt("tensor", SE, ev[(ph, "qev", u - 4)])
                for kc in range(8):
                    o("tensor", "matmul", out=slot,
                      lhsT=wbuf[:, kc * D + dc * 128:kc * D + (dc + 1) * 128],
                      rhs=y2_ap(kc, tt * TTW, TTW), start=(kc == 0), stop=(kc == 7),
                      inc=(TE if kc == 7 else None))
                wt("scalar", TE, TE.n)
                if u >= 2:
                    ctx.wait_tok("scalar", ev[(ph, "qdma", u - 2)])
                qstg = wB(QS0 + (u % 2) * 256, TTW)
                o("scalar", "activation", out=qstg, in_=slot, func=AF.Copy,
                  scale=SCALE_Q, inc=SE)
                ev[(ph, "qev", u)] = SE.n
                wt("scalar", SE, SE.n)
                ev[(ph, "qdma", u)] = QS.dma(
                    out=scr_q[dc, :, tt * TTW:(tt + 1) * TTW], in_=qstg)
        ev["qdma"] = QS.snapshot()
        ev["q_T"] = TE.n

        wt("gpsimd", TE, ev["q_T"])
        tk_wk = QG.dma(out=wbuf.rearrange("p (k d) -> p k d", k=8),
                       in_=Wk_d.rearrange("(k p) d -> p k d", p=128))
        ctx.wait_tok("vector", tk_wk)
        wvc = wB(QS0 + 512, 1024)
        ctx.wait_toks("gpsimd", ev["qdma"])
        last_tt = 0
        for kc8 in range(8):
            if kc8 > 0:
                wt("gpsimd", VE, last_tt)
            tk_wv = QG.dma(out=wvc, in_=Wv_d[kc8 * 128:(kc8 + 1) * 128, :])
            ctx.wait_tok("vector", tk_wv)
            o("vector", "tensor_add", out=wbuf[:, kc8 * D:(kc8 + 1) * D],
              in0=wbuf[:, kc8 * D:(kc8 + 1) * D], in1=wvc, inc=VE)
            dr("vector")
            last_tt = VE.n
        v_wsum = VE.n
        wdkv = wB(VTOK_O, 2048)
        wt("gpsimd", SE, ev[(ph, "qev", 8 * NTT - 1)])
        tk_wdkv = QG.dma(out=wdkv.rearrange("p (k l) -> p k l", k=8),
                         in_=Wdkv_d.rearrange("(k p) l -> p k l", p=128))

        latT = wB(LATT_O, 2 * T)
        kvbf = wB(QSTR_O, 8 * 512)
        for tt in range(NTT):
            for dc in range(8):
                slot = ps_big[:, (dc % 2) * 512:(dc % 2) * 512 + TTW]
                u = tt * 8 + dc
                if u >= 2:
                    pu = u - 2
                    wt("tensor", VE, ev[(ph, "kvev", pu // 8, pu % 8)])
                if u == 0:
                    wt("tensor", VE, v_wsum)
                for kc in range(8):
                    o("tensor", "matmul", out=slot,
                      lhsT=wbuf[:, kc * D + dc * 128:kc * D + (dc + 1) * 128],
                      rhs=y2_ap(kc, tt * TTW, TTW), start=(kc == 0), stop=(kc == 7),
                      inc=(TE if kc == 7 else None))
                wt("vector", TE, TE.n)
                if tt > 0 and dc == 0:
                    wt("vector", TE, ev[(ph, "latmm", tt - 1)])
                o("vector", "tensor_copy", out=kvbf[:, dc * TTW:(dc + 1) * TTW],
                  in_=slot, inc=VE)
                ev[(ph, "kvev", tt, dc)] = VE.n
            wt("tensor", VE, ev[(ph, "kvev", tt, 7)])
            if tt == 0:
                ctx.wait_tok("tensor", tk_wdkv)
            for lc in range(2):
                slot = ps_big[:, 1024 + lc * 512:1024 + lc * 512 + TTW]
                for kc in range(8):
                    o("tensor", "matmul", out=slot,
                      lhsT=wdkv[:, kc * 256 + lc * 128:kc * 256 + (lc + 1) * 128],
                      rhs=kvbf[:, kc * TTW:(kc + 1) * TTW], start=(kc == 0),
                      stop=(kc == 7), inc=(TE if kc == 7 else None))
                ev[(ph, "latmm", tt)] = TE.n
                wt("vector", TE, TE.n)
                o("vector", "tensor_copy",
                  out=latT.rearrange("p (l t) -> p l t", l=2)[
                      :, lc, tt * TTW:(tt + 1) * TTW], in_=slot, inc=VE)
            ev[(ph, "latev", tt)] = VE.n
        ev["lat"] = VE.n

        def lat_ap(lc, t0, tw):
            return latT.rearrange("p (l t) -> p l t", l=2)[:, lc, t0:t0 + tw]

        wupk = wB(WBUF_O, 2 * D)
        wt("gpsimd", VE, ev["lat"])
        tk_upk = QG.dma(out=wupk.rearrange("p (l d) -> p l d", l=2),
                        in_=Wupk_d.rearrange("(l p) d -> p l d", p=128))
        ctx.wait_tok("tensor", tk_upk)
        wt("tensor", VE, ev["lat"])
        for dc in range(8):
            for tt in range(NTT):
                u = dc * NTT + tt
                slot = ps_big[:, (u % 4) * 512:(u % 4) * 512 + TTW]
                if u >= 4:
                    wt("tensor", VE, ev[(ph, "krev", u - 4)])
                for lc in range(2):
                    o("tensor", "matmul", out=slot,
                      lhsT=wupk[:, lc * D + dc * 128:lc * D + (dc + 1) * 128],
                      rhs=lat_ap(lc, tt * TTW, TTW), start=(lc == 0), stop=(lc == 1),
                      inc=(TE if lc == 1 else None))
                wt("vector", TE, TE.n)
                if u == 0:
                    wt("vector", SE, ev["tbl"])
                o("vector", "tensor_tensor",
                  out=kX.rearrange("p (d t) -> p d t", d=8)[
                      :, dc, tt * TTW:(tt + 1) * TTW],
                  in0=slot, in1=EQB[:, tt * TTW:(tt + 1) * TTW],
                  op=ALU.mult, inc=VE)
                ev[(ph, "krev", u)] = VE.n
        ev["krec"] = VE.n
        ev["krec_T"] = TE.n

        wupv = wB(WBUF_O, 2 * D)
        wt("gpsimd", TE, ev["krec_T"])
        tk_upv = QG.dma(out=wupv.rearrange("p (l d) -> p l d", l=2),
                        in_=Wupv_d.rearrange("(l p) d -> p l d", p=128))
        vtok = wB(VTOK_O, NH * TC * 65)
        wt("vector", TE, ev[(ph, "latmm", NTT - 1)])
        o("vector", "memset",
          vtok.rearrange("p (n c) -> p n c", c=65)[:, :, 64:65], 1.0, inc=VE)
        dr("vector")
        v_ones = VE.n
        ctx.wait_tok("tensor", tk_upv)
        wt("tensor", VE, v_ones)
        for tc in range(TC):
            for dh in range(2):
                u = tc * 2 + dh
                slot = ps_big[:, (u % 4) * 512:(u % 4) * 512 + 512]
                if u >= 4:
                    wt("tensor", VE, ev[(ph, "vtev", u - 4)])
                for lc in range(2):
                    o("tensor", "matmul", out=slot,
                      lhsT=lat_ap(lc, tc * 128, 128),
                      rhs=wupv[:, lc * D + dh * 512:lc * D + (dh + 1) * 512],
                      start=(lc == 0), stop=(lc == 1),
                      inc=(TE if lc == 1 else None))
                wt("vector", TE, TE.n)
                out_ap = AP(tensor=W.bitcast(BF16).tensor,
                            offset=2 * VTOK_O + (dh * 8 * TC + tc) * 65,
                            ap=[[2 * WORK_SZ, 128], [65 * TC, 8], [1, 64]])
                o("vector", "tensor_copy", out=out_ap,
                  in_=slot.rearrange("p (h e) -> p h e", h=8), inc=VE)
                ev[(ph, "vtev", u)] = VE.n
        ev["vtok"] = VE.n

        wo = wB(WBUF_O, 8 * D)
        wt("gpsimd", TE, TE.n)
        tk_wo = QG.dma(out=wo.rearrange("p (k d) -> p k d", k=8),
                       in_=Wo_d.rearrange("(k p) d -> p k d", p=128))

        qA = wB(QSTR_O, 8 * QT)
        qB = wB(QSTR_O + 1024, 8 * QT)
        exp_hist = []
        lkx = kX.rearrange("p (d t) -> p d t", d=8)
        NG2 = TC // 2

        def eqslice(tbl, qt):
            return AP(tensor=W.bitcast(BF16).tensor,
                      offset=2 * TBL_O + (0 if tbl is EQA else
                                          (T if tbl is EQB else 2 * T)) + qt * QT,
                      ap=[[2 * WORK_SZ, 128], [0, 8], [1, QT]])

        def do_av_mm(h, qt):
            hp, hi = h // 2, h % 2
            avps = ps_ava if hi == 0 else ps_avb
            avslot = avps[0:65, 0:QT]
            if hi == 0 and (qt, hp) > (0, 0):
                ph_, hp_ = (qt, hp - 1) if hp >= 1 else (qt - 1, NH // 2 - 1)
                wt("tensor", VE, ev[(ph, "avTT", ph_, hp_)])
            estrip = hi * TC * QT
            for g in range(NG2):
                wt("tensor", SE, ev[(ph, "exp", qt, hp, g)])
                if g == qt:
                    wt("tensor", VE, ev[(ph, "sel", qt, hp)])
                for j in range(2):
                    kc = g * 2 + j
                    o("tensor", "matmul", out=avslot,
                      lhsT=vtok[:, (h * TC + kc) * 65:(h * TC + kc) * 65 + 65],
                      rhs=exps[:, estrip + kc * QT:estrip + (kc + 1) * QT],
                      start=(kc == 0), stop=(kc == TC - 1),
                      inc=(TE if kc == TC - 1 else None))
            ev[(ph, "avmm", qt, h)] = TE.n

        def do_pair_norm(hp, qt):
            den0 = C[64:65, rec_o:rec_o + QT]
            den1 = C[64:65, rec_o + QT:rec_o + 2 * QT]
            wt("vector", TE, ev[(ph, "avmm", qt, 2 * hp)])
            o("vector", "tensor_copy", out=den0, in_=ps_ava[64:65, 0:QT],
              inc=VE)
            v_cp0 = VE.n
            wt("vector", TE, ev[(ph, "avmm", qt, 2 * hp + 1)])
            o("vector", "tensor_copy", out=den1, in_=ps_avb[64:65, 0:QT],
              inc=VE)
            v_cp1 = VE.n
            wt("tensor", VE, v_cp0)
            o("tensor", "matmul", out=ps_ava[0:64, 256:256 + QT],
              lhsT=onesf[64:65, 0:64], rhs=den0, start=True, stop=True)
            wt("tensor", VE, v_cp1)
            o("tensor", "matmul", out=ps_ava[64:128, 256:256 + QT],
              lhsT=onesf[64:65, 0:64], rhs=den1, start=True, stop=True, inc=TE)
            t_bc = TE.n
            rec128 = C[:, avtmp_o:avtmp_o + QT]
            wt("vector", TE, t_bc)
            o("vector", "reciprocal", out=rec128, in_=ps_ava[:, 256:256 + QT],
              inc=VE)
            dr("vector")
            outT = wB(OUTT_O + (qt % 2) * 1024, 8 * QT)
            oT = outT.rearrange("p (c t) -> p c t", c=8)
            o("vector", "tensor_tensor", out=oT[0:64, hp, :],
              in0=ps_ava[0:64, 0:QT], in1=rec128[0:64, :], op=ALU.mult)
            o("vector", "tensor_tensor", out=oT[64:128, hp, :],
              in0=ps_avb[0:64, 0:QT], in1=rec128[64:128, :], op=ALU.mult,
              inc=VE)
            ev[(ph, "avTT", qt, hp)] = VE.n

        for qt in range(NQT):
            ctx.wait_toks("sync", ev["qdma"])
            if qt >= 1:
                wt("sync", TE, ev[(ph, "qused", qt - 1)])
            else:
                wt("sync", TE, ev[(ph, "latmm", NTT - 1)])
                wt("sync", VE, ev["lat"])
            tk_q = QL.dma(
                out=qA.rearrange("p (d t) -> p d t", d=8),
                in_=scr_q[:, :, qt * QT:(qt + 1) * QT].transpose([1, 0, 2]))
            ctx.wait_tok("vector", tk_q)
            if qt >= 1:
                wt("vector", TE, ev[(ph, "qused", qt - 1)])
            o("vector", "tensor_tensor",
              out=qB.rearrange("p (d t) -> p d t", d=8),
              in0=qA.rearrange("p (d t) -> p d t", d=8),
              in1=eqslice(EQB, qt), op=ALU.mult)
            dr("vector")
            o("vector", "tensor_tensor",
              out=qA.rearrange("p (d t) -> p d t", d=8),
              in0=qA.rearrange("p (d t) -> p d t", d=8),
              in1=eqslice(EQA, qt), op=ALU.mult, inc=VE)
            v_qk = VE.n
            if qt >= 1:
                wt("vector", VE, ev[(ph, "kcv", qt - 1)])
            eqa_s = AP(tensor=W.bitcast(BF16).tensor,
                       offset=2 * TBL_O + 2 * qt * 128,
                       ap=[[2 * WORK_SZ, 128], [0, 8], [1, 256]])
            o("vector", "tensor_tensor",
              out=kmix.rearrange("p (d t) -> p d t", d=8),
              in0=AP(tensor=W.bitcast(BF16).tensor, offset=2 * KX_O + 2 * qt * 128,
                     ap=[[2 * WORK_SZ, 128], [T, 8], [1, 256]]),
              in1=eqa_s, op=ALU.mult)
            dr("vector")
            o("vector", "tensor_tensor",
              out=kmix.rearrange("p (d t) -> p d t", d=8),
              in0=kmix.rearrange("p (d t) -> p d t", d=8),
              in1=eqa_s, op=ALU.mult, inc=VE)
            v_km = VE.n
            for hp in range(NH // 2):
                hc = hp
                for g in range(NG2):
                    u = hp * NG2 + g
                    sslot = ps_big[:, (u % 2) * 1024:(u % 2) * 1024 + 4 * QT]
                    if len(exp_hist) >= 2:
                        wt("tensor", SE, exp_hist[-2])
                    if qt >= 1 and u < 2:
                        wt("tensor", VE,
                           ev[(ph, "xres", qt * (QT // 128) * 2 - 1)])
                    if hp == 0 and g == 0:
                        ctx.wait_tok("tensor", tk_q)
                        wt("tensor", VE, v_qk)
                        wt("tensor", VE, ev["krec"])
                        wt("tensor", VE, ev["vtok"])
                    for j in range(2):
                        kc = g * 2 + j
                        if kc >= 2 * qt + 2:
                            rhs_q = qA
                        elif kc <= 2 * qt - 1:
                            rhs_q = qB
                        else:
                            rhs_q = qA
                        rq = rhs_q.rearrange("p (d t) -> p d t", d=8)
                        for hi in range(2):
                            hr = hi * 64
                            o("tensor", "matmul",
                              out=sslot[:, hi * 512 + j * QT:hi * 512 + (j + 1) * QT],
                              lhsT=lkx[hr:hr + 64, hc, kc * 128:(kc + 1) * 128],
                              rhs=rq[hr:hr + 64, hc, :],
                              start=True, stop=True,
                              inc=(TE if (j == 1 and hi == 1) else None))
                    t_sc = TE.n
                    if g == qt:
                        if hp == 0:
                            wt("tensor", VE, v_km)
                        if hp >= 1:
                            wt("tensor", SE, ev[(ph, "expB", qt, hp - 1)])
                        elif qt >= 1:
                            wt("tensor", SE, ev[(ph, "expB", qt - 1, NH // 2 - 1)])
                        km = kmix.rearrange("p (d t) -> p d t", d=8)
                        qb = qB.rearrange("p (d t) -> p d t", d=8)
                        for j2 in range(2):
                            for hi in range(2):
                                hr = hi * 64
                                tp = ps_tp0f if hi == 0 else ps_tp1f
                                o("tensor", "matmul",
                                  out=tp[:, j2 * QT:(j2 + 1) * QT],
                                  lhsT=km[hr:hr + 64, hc, j2 * 128:(j2 + 1) * 128],
                                  rhs=qb[hr:hr + 64, hc, :],
                                  start=True, stop=True,
                                  inc=(TE if (j2 == 1 and hi == 1) else None))
                        t_b = TE.n
                        wt("scalar", TE, t_b)
                        if hp >= 1:
                            wt("scalar", VE, ev[(ph, "sel", qt, hp - 1)])
                        elif qt >= 1:
                            wt("scalar", VE, ev[(ph, "sel", qt - 1, NH // 2 - 1)])
                        o("scalar", "activation", out=scm[:, 0:2 * QT],
                          in_=ps_tp0f[:, 0:2 * QT], func=AF.Exp)
                        o("scalar", "activation", out=scm[:, 2 * QT:4 * QT],
                          in_=ps_tp1f[:, 0:2 * QT], func=AF.Exp, inc=SE)
                        ev[(ph, "expB", qt, hp)] = SE.n
                    wt("scalar", TE, t_sc)
                    if g == 0:
                        if hp >= 1:
                            wt("scalar", TE, ev[(ph, "avmm", qt, 2 * hp - 1)])
                        elif qt >= 1:
                            wt("scalar", TE, ev[(ph, "avmm", qt - 1, NH - 1)])
                    exp_out = AP(tensor=W.bitcast(BF16).tensor,
                                 offset=2 * EXPS_O + g * 2 * QT,
                                 ap=[[2 * WORK_SZ, 128], [TC * QT, 2], [1, 2 * QT]])
                    o("scalar", "activation", out=exp_out,
                      in_=sslot, func=AF.Exp, inc=SE)
                    ev[(ph, "exp", qt, hp, g)] = SE.n
                    exp_hist.append(SE.n)
                    if g == qt:
                        wt("scalar", VE, VE.n)
                        wt("vector", SE, SE.n)
                        for hi in range(2):
                            estrip = hi * TC * QT
                            for j2 in range(2):
                                kc = 2 * qt + j2
                                o("vector", "copy_predicated",
                                  out=exps[:, estrip + kc * QT:
                                           estrip + (kc + 1) * QT],
                                  mask=(MB0 if j2 == 0 else MB1),
                                  data=scm[:, (hi * 2 + j2) * QT:
                                           (hi * 2 + j2 + 1) * QT], inc=(
                                      VE if (hi == 1 and j2 == 1) else None))
                        ev[(ph, "sel", qt, hp)] = VE.n
                do_av_mm(2 * hp, qt)
                do_av_mm(2 * hp + 1, qt)
                do_pair_norm(hp, qt)
            if qt == NQT - 1:
                ev["kx_done"] = TE.n
            ev[(ph, "qused", qt)] = TE.n
            wt("vector", TE, ev[(ph, "qused", qt)])
            o("vector", "tensor_copy",
              out=AP(tensor=W.bitcast(BF16).tensor, offset=2 * KX_O + 2 * qt * 128,
                     ap=[[2 * WORK_SZ, 128], [T, 8], [1, 256]]),
              in_=kmix.rearrange("p (d t) -> p d t", d=8), inc=VE)
            dr("vector")
            ev[(ph, "kcv", qt)] = VE.n
            wt("tensor", VE, ev[(ph, "avTT", qt, NH // 2 - 1)])
            if qt == 0:
                ctx.wait_tok("tensor", tk_wo)
            outT = wB(OUTT_O + (qt % 2) * 1024, 8 * QT)
            loT = outT.rearrange("p (c t) -> p c t", c=8)
            for tsub in range(QT // 128):
                tc = (qt * QT) // 128 + tsub
                for dh in range(2):
                    u = qt * (QT // 128) * 2 + tsub * 2 + dh
                    slot = ps_big[:, 1024 + (u % 2) * 512:1024 + (u % 2) * 512 + 512]
                    if u >= 2:
                        wt("tensor", VE, ev[(ph, "xres", u - 2)])
                    for kc in range(8):
                        o("tensor", "matmul", out=slot,
                          lhsT=loT[:, kc, tsub * 128:(tsub + 1) * 128],
                          rhs=wo[:, kc * D + dh * 512:kc * D + (dh + 1) * 512],
                          start=(kc == 0), stop=(kc == 7),
                          inc=(TE if kc == 7 else None))
                    wt("vector", TE, TE.n)
                    xsl = X[:, tc * D + dh * 512:tc * D + (dh + 1) * 512]
                    o("vector", "tensor_add", out=xsl, in0=xsl, in1=slot, inc=VE)
                    ev[(ph, "xres", u)] = VE.n
                    dr("vector")
        ev["attn_v"] = VE.n
        ev["attn_T"] = TE.n

    ffn_pre(1, 0)
    ffn_mm(1, 0)
    prep_down()
    ffn_down(1, 0)
    attn_block()
    ffn_pre(2, 2)
    wt("vector", TE, ev["kx_done"])
    wt("scalar", TE, ev["kx_done"])
    wt("sync", TE, ev["attn_T"])
    wt("sync", VE, ev["attn_v"])
    wt("sync", SE, SE.n)
    wt("gpsimd", TE, ev["attn_T"])
    wt("gpsimd", VE, ev["attn_v"])
    ffn_mm(2, 2)
    ffn_down(2, 2)
    ctx.wait_toks("sync", ev["outdma"])

    with nc.Block() as block:
        @block.sync
        def _(eng):
            for fn in ctx.streams["sync"]:
                fn(eng)

        @block.gpsimd
        def _(eng):
            for fn in ctx.streams["gpsimd"]:
                fn(eng)

        @block.vector
        def _(eng):
            for fn in ctx.streams["vector"]:
                fn(eng)

        @block.scalar
        def _(eng):
            for fn in ctx.streams["scalar"]:
                fn(eng)

        @block.tensor
        def _(eng):
            for fn in ctx.streams["tensor"]:
                fn(eng)

    es.close()
    nc._dbg_sems = {"CCS": CCS.n, "TE": TE.n, "VE": VE.n, "SE": SE.n,
                    "GE": GE.n}
    return nc


def kernel(**inputs):
    n = 8
    x = np.ascontiguousarray(np.asarray(inputs["L0_attn_x"], dtype=np.float32))
    T = x.shape[1]
    nc = build(T=T, n_cores=n)
    wk = {k: np.ascontiguousarray(np.asarray(v, dtype=np.float32))
          for k, v in inputs.items() if k != "L0_attn_x"}
    in_maps = [dict(wk, L0_attn_x=np.ascontiguousarray(x[c])) for c in range(n)]
    res = run_bass_kernel_spmd(nc, in_maps, core_ids=list(range(n)))
    return np.stack([res.results[c]["out"] for c in range(n)], axis=0)


# version 19
# speedup vs baseline: 1.0127x; 1.0127x over previous
import numpy as np
import concourse.bass as bass
import concourse.mybir as mybir
from concourse.bass_utils import run_bass_kernel_spmd

F32 = mybir.dt.float32
BF16 = mybir.dt.bfloat16
AF = mybir.ActivationFunctionType
ALU = mybir.AluOpType
AX = mybir.AxisListType
AP = bass.AP

MAGIC = 12582912.0
QHI, QLO = 127.49, -128.49
D = 1024
NH, HD, DL = 16, 64, 256
NI, IH = 22, 2816
SCALE_Q = float(1.0 / (np.float64(np.sqrt(np.float32(1024.0))) + 1e-8))
LAM = 1e-4
EPS = 1e-5
QT = 256


class Sem:
    def __init__(self, h, step=1):
        self.h, self.n, self.step = h, 0, step


class Ctx:
    ENGS = ("sync", "gpsimd", "vector", "scalar", "tensor")

    def __init__(self):
        self.streams = {e: [] for e in self.ENGS}

    def op(self, eng, method, *args, inc=None, **kwargs):
        incs = []
        if inc is not None:
            for s in (inc if isinstance(inc, list) else [inc]):
                s.n += s.step
                incs.append((s.h, s.step))

        import traceback
        tb = traceback.extract_stack()[-2]

        def fn(e, _m=method, _a=args, _k=kwargs, _i=incs, _tb=tb):
            try:
                inst = getattr(e, _m)(*_a, **_k)
            except Exception:
                print(f"FAILING OP: {_m} emitted at {_tb.filename}:{_tb.lineno}")
                raise
            for (h, k) in _i:
                inst = inst.then_inc(h, k)
        self.streams[eng].append(fn)

    def wait(self, eng, sem, val=None):
        v = sem.n if val is None else val
        if v <= 0:
            return
        self.streams[eng].append(lambda e, _h=sem.h, _v=v: e.wait_ge(_h, _v))

    def drain(self, eng):
        self.streams[eng].append(lambda e: e.drain())

    def wait_tok(self, eng, tok):
        if tok is None:
            return
        s, v = tok
        self.wait(eng, s, v)

    def wait_toks(self, eng, toks):
        best = {}
        for t in toks:
            if t is None:
                continue
            s, v = t
            if v > best.get(id(s), (s, 0))[1]:
                best[id(s)] = (s, v)
        for s, v in best.values():
            self.wait(eng, s, v)


class DmaQ:

    def __init__(self, ctx, es, nc, name, eng, nlanes):
        self.ctx, self.eng = ctx, eng
        self.lanes = [Sem(es.enter_context(nc.semaphore(name + str(i))), 16)
                      for i in range(nlanes)]
        self.rr = 0

    def dma(self, out, in_, eng=None):
        e = eng or self.eng
        ln = self.lanes[self.rr]
        self.rr = (self.rr + 1) % len(self.lanes)
        if ln.n > 0:
            self.ctx.wait(e, ln, ln.n)
        self.ctx.op(e, "dma_start", out=out, in_=in_, inc=ln)
        return (ln, ln.n)

    def snapshot(self):
        return [(l, l.n) for l in self.lanes if l.n > 0]


def build(T=2048, n_cores=8):
    TC = T // 128
    NTT = max(1, T // 512)
    TTW = min(512, T)
    NQT = T // QT
    NSUB = TTW // 128

    nc = bass.Bass("TRN2", target_bir_lowering=False)

    x_in = nc.dram_tensor("L0_attn_x", [T, D], F32, kind="ExternalInput").ap()
    Wq_d = nc.dram_tensor("L0_attn_Wq", [D, D], F32, kind="ExternalInput").ap()
    Wk_d = nc.dram_tensor("L0_attn_Wk", [D, D], F32, kind="ExternalInput").ap()
    Wv_d = nc.dram_tensor("L0_attn_Wv", [D, D], F32, kind="ExternalInput").ap()
    Wo_d = nc.dram_tensor("L0_attn_Wo", [D, D], F32, kind="ExternalInput").ap()
    Wdkv_d = nc.dram_tensor("L0_attn_W_down_kv", [D, DL], F32, kind="ExternalInput").ap()
    Wupk_d = nc.dram_tensor("L0_attn_W_up_k", [DL, D], F32, kind="ExternalInput").ap()
    Wupv_d = nc.dram_tensor("L0_attn_W_up_v", [DL, D], F32, kind="ExternalInput").ap()
    clipx_d = nc.dram_tensor("L0_ffn_clip_x", [1], F32, kind="ExternalInput").ap()
    clipd_d = nc.dram_tensor("L0_ffn_clip_down", [1], F32, kind="ExternalInput").ap()
    wg_d = nc.dram_tensor("L0_ffn_w_gate", [IH, D], F32, kind="ExternalInput").ap()
    wu_d = nc.dram_tensor("L0_ffn_w_up", [IH, D], F32, kind="ExternalInput").ap()
    wd_d = nc.dram_tensor("L0_ffn_w_down", [D, IH], F32, kind="ExternalInput").ap()
    out_d = nc.dram_tensor("out", [T, D], F32, kind="ExternalOutput").ap()

    scr_g = nc.dram_tensor("scr_g", [NI, 128, D], BF16).ap()
    scr_u = nc.dram_tensor("scr_u", [NI, 128, D], BF16).ap()
    scr_d = nc.dram_tensor("scr_d", [NI, 128, D], BF16).ap()
    scr_h = nc.dram_tensor("scr_h", [NI, 128, T], F32).ap()
    scr_q = nc.dram_tensor("scr_q", [8, 128, T], BF16).ap()
    cc_in = [nc.dram_tensor(f"cc_in{i}", [1, 1], F32).ap() for i in range(4)]
    cc_out = [nc.dram_tensor(f"cc_out{i}", [1, n_cores], F32).ap() for i in range(4)]

    from contextlib import ExitStack
    es = ExitStack()

    X = es.enter_context(nc.sbuf_tensor("x_arena", [128, TC * D], F32)).ap()

    CONST_SZ = 3700
    C = es.enter_context(nc.sbuf_tensor("c_arena", [128, CONST_SZ], F32)).ap()
    _co = [0]

    def calloc(n):
        o_ = _co[0]
        _co[0] += n
        assert _co[0] <= CONST_SZ, _co[0]
        return o_

    identb_o = calloc(64)
    identf_o = calloc(128)
    onesf_o = calloc(128)
    swg_o = calloc(NI)
    swu_o = calloc(NI)
    swd_o = calloc(8)
    combg_o = calloc(NI)
    combu_o = calloc(NI)
    mr_o = calloc(2 * TC)
    ymax_o = calloc(TC)
    hmax_o = calloc(NI * NTT)
    sc_o = calloc(16)
    tmpa_o = calloc(D)
    ytile_o = calloc(D)
    swdb2_o = ytile_o
    row_o = calloc(48)
    nmagic_o = calloc(1)
    magic_o = calloc(1)
    c127_o = calloc(1)
    eps_o = calloc(1)
    lnv_o = calloc(TC)
    lnm2_o = calloc(TC)
    swdrow_o = calloc(128)
    rec_o = calloc(2 * QT)
    avtmp_o = calloc(QT)

    def cF(off, n):
        return C[:, off:off + n]

    def cB(off, nbf):
        return C.bitcast(BF16)[:, 2 * off:2 * off + nbf]

    identb = cB(identb_o, 128)
    identf = cF(identf_o, 128)
    onesf = cF(onesf_o, 128)

    SC_R127X, SC_SX, SC_R127H, SC_SH05 = 0, 1, 2, 3

    def scv(i):
        return cF(sc_o + i, 1)

    def r0(col, n=1):
        return C[0:1, row_o + col:row_o + col + n]

    WORK_SZ = 32900
    W = es.enter_context(nc.sbuf_tensor("w_arena", [128, WORK_SZ], F32)).ap()

    def wF(off, n):
        return W[:, off:off + n]

    def wB(off, nbf):
        return W.bitcast(BF16)[:, 2 * off:2 * off + nbf]

    XQT_O = 0
    STG_O = XQT_O + 4 * T
    WLD_O = STG_O + 2048
    WQT_O = WLD_O + 2048
    WQI_O = WQT_O + IH
    SILU_O = WQI_O + IH // 2
    HOUT_O = SILU_O + 1024
    F_END = HOUT_O + 1024
    assert F_END <= WORK_SZ
    HLD_O = 0
    HQ_O = HLD_O + 4096
    OUTB_O = HQ_O + 2 * NI * 256
    SSEL_O = OUTB_O + 1024
    RHS2_O = SSEL_O + 1024
    WDT_O = max(F_END, RHS2_O + 1024)
    assert WDT_O + NI * 512 <= WORK_SZ
    WDQ_O = WORK_SZ - 4 * IH
    assert WDQ_O >= F_END + IH

    Y2T_O = 0
    KX_O = Y2T_O
    VTOK_O = Y2T_O + 4 * T
    VTOK_SZ = (NH * TC * 65 + 1) // 2
    EXPS_O = VTOK_O + VTOK_SZ
    EXPS_SZ = TC * QT
    WBUF_O = EXPS_O + EXPS_SZ
    LATT_O = WBUF_O + 4096
    OUTT_O = LATT_O
    QSTR_O = LATT_O + max(T, 8 * QT)
    TBL_O = QSTR_O + 2048
    KMIX_O = TBL_O + 2 * (T // 2)
    MSK_O = KMIX_O + 1024
    SCM_O = MSK_O + QT
    A_END = SCM_O + 2 * QT
    assert A_END <= WORK_SZ, A_END

    ps_big = es.enter_context(nc.psum_tensor("ps_big", [128, 2048], F32)).ap()
    ps_ava = es.enter_context(nc.psum_tensor("ps_ava", [128, 512], F32)).ap()
    ps_avb = es.enter_context(nc.psum_tensor("ps_avb", [128, 512], F32)).ap()
    ps_tp0f = es.enter_context(nc.psum_tensor("ps_tp0", [128, 512], F32)).ap()
    ps_tp1f = es.enter_context(nc.psum_tensor("ps_tp1", [128, 512], F32)).ap()
    ps_tp0 = ps_tp0f.bitcast(BF16)
    ps_tp1 = ps_tp1f.bitcast(BF16)

    CCS = Sem(es.enter_context(nc.semaphore("CCS")))
    TE = Sem(es.enter_context(nc.semaphore("TE")))
    VE = Sem(es.enter_context(nc.semaphore("VE")))
    SE = Sem(es.enter_context(nc.semaphore("SE")))
    GE = Sem(es.enter_context(nc.semaphore("GE")))

    ctx = Ctx()
    o, wt, dr = ctx.op, ctx.wait, ctx.drain
    ev = {}
    QL = DmaQ(ctx, es, nc, "QL", "sync", 6)
    QS = DmaQ(ctx, es, nc, "QS", "scalar", 5)
    QG = DmaQ(ctx, es, nc, "QG", "gpsimd", 3)

    tk_x = QL.dma(out=X.rearrange("p (tc d) -> p tc d", tc=TC),
                  in_=x_in.rearrange("(tc p) d -> p tc d", p=128))
    tk_cx = QL.dma(out=r0(0), in_=clipx_d.unsqueeze(0))
    tk_cd = QL.dma(out=r0(1), in_=clipd_d.unsqueeze(0))

    o("gpsimd", "memset", identf, 1.0)
    o("gpsimd", "memset", onesf, 1.0)
    dr("gpsimd")
    o("gpsimd", "affine_select", out=identf, in_=identf, pattern=[[1, 128]],
      compare_op=ALU.is_equal, fill=0.0, base=0, channel_multiplier=-1)
    dr("gpsimd")
    o("gpsimd", "tensor_copy", out=identb, in_=identf)
    o("gpsimd", "memset", cF(nmagic_o, 1), -MAGIC)
    o("gpsimd", "memset", cF(magic_o, 1), MAGIC)
    o("gpsimd", "memset", cF(c127_o, 1), 127.0)
    o("gpsimd", "memset", cF(eps_o, 1), EPS)
    dr("gpsimd")
    o("gpsimd", "engine_nop", inc=GE)
    g_consts = GE.n
    wt("scalar", GE, g_consts)
    wt("vector", GE, g_consts)
    ctx.wait_toks("scalar", [tk_cx, tk_cd])
    o("scalar", "activation", out=r0(2, 2), in_=r0(0, 2), func=AF.Sigmoid, inc=SE)
    ev["sig"] = SE.n

    def quant3(src, dst_bf, rec_pp, tmp):
        o("vector", "tensor_scalar", out=tmp, in0=src, scalar1=rec_pp,
          scalar2=QHI, op0=ALU.mult, op1=ALU.min)
        dr("vector")
        o("vector", "tensor_scalar", out=tmp, in0=tmp, scalar1=QLO,
          scalar2=MAGIC, op0=ALU.max, op1=ALU.add, inc=VE)
        wt("scalar", VE, VE.n)
        o("scalar", "activation", out=dst_bf, in_=tmp, func=AF.Identity,
          bias=cF(nmagic_o, 1), inc=SE)
        return SE.n

    def rowquant(wld, n, sw_slot, t0, t1):
        o("vector", "reduce_max", out=t0, in_=wld, axis=AX.X,
          apply_absolute_value=True)
        dr("vector")
        o("vector", "reciprocal", out=t1, in_=t0)
        dr("vector")
        o("vector", "tensor_scalar_mul", out=t1, in0=t1, scalar1=127.0)
        o("vector", "tensor_scalar_mul", out=sw_slot, in0=t0, scalar1=1.0 / 127.0)
        dr("vector")

    def prep_gu(w_dram, scr, sw_off, tag):
        for ic in range(NI):
            par = ic % 2
            wld = wF(WLD_O + par * D, D)
            if ic >= 2:
                wt("gpsimd", SE, ev[(tag, "w1", ic - 2)])
            elif tag == "u":
                wt("gpsimd", SE, ev[("g", "w1", NI - 2 + ic)])
            tk = QG.dma(out=wld, in_=w_dram[ic * 128:(ic + 1) * 128, :])
            ctx.wait_tok("vector", tk)
            t0 = wF(WQT_O + par * 2, 1)
            t1 = wF(WQT_O + par * 2 + 1, 1)
            if ic >= 2:
                wt("vector", SE, ev[(tag, "w1", ic - 2)])
            elif tag == "u":
                wt("vector", SE, ev[("g", "w1", NI - 2 + ic)])
            o("vector", "reduce_max", out=t0, in_=wld, axis=AX.X,
              apply_absolute_value=True)
            dr("vector")
            o("vector", "reciprocal", out=t1, in_=t0)
            dr("vector")
            o("vector", "tensor_scalar_mul", out=t1, in0=t1, scalar1=127.0)
            o("vector", "tensor_scalar_mul", out=cF(sw_off + ic, 1), in0=t0,
              scalar1=1.0 / 127.0, inc=VE)
            ev[(tag, "wq", ic)] = VE.n
            wq = wF(WQT_O + 4 + par * D, D)
            wt("scalar", VE, VE.n)
            o("scalar", "activation", out=wq, in_=wld, func=AF.Identity,
              scale=t1, bias=cF(magic_o, 1), inc=SE)
            ev[(tag, "w1", ic)] = SE.n
            dr("scalar")
            if ic >= 2:
                wt("scalar", TE, ev[(tag, "tp", ic - 2)])
            wqi = wB(WQI_O + par * 512, D)
            o("scalar", "activation", out=wqi, in_=wq, func=AF.Identity,
              bias=cF(nmagic_o, 1), inc=SE)
            ev[(tag, "wqi", ic)] = SE.n
            ps = ps_tp0 if par == 0 else ps_tp1
            wt("tensor", SE, SE.n)
            if ic < 2:
                wt("tensor", VE, VE.n)
            if ic >= 2:
                wt("tensor", SE, ev[(tag, "ev", ic - 2)])
            for hc in range(8):
                o("tensor", "transpose", out=ps[:, hc * 128:(hc + 1) * 128],
                  in_=wqi[:, hc * 128:(hc + 1) * 128], identity=identb,
                  inc=(TE if hc == 7 else None))
            ev[(tag, "tp", ic)] = TE.n
            stg = wB(STG_O + (1024 if tag == "u" else 0) + par * 512, D)
            wt("scalar", TE, TE.n)
            if ic >= 2:
                ctx.wait_tok("scalar", ev[(tag, "st", ic - 2)])
            o("scalar", "activation", out=stg, in_=ps[:], func=AF.Copy, inc=SE)
            ev[(tag, "ev", ic)] = SE.n
            wt("scalar", SE, SE.n)
            ev[(tag, "st", ic)] = QS.dma(out=scr[ic], in_=stg)

    def prep_down():
        wdqi = wB(WDQ_O, 8 * IH)
        for hc in range(8):
            wld = wF(WQT_O, IH) if hc % 2 == 0 else wF(F_END, IH)
            par = hc % 2
            if hc >= 2:
                wt("gpsimd", SE, ev[("d", "wqi", hc - 2)])
            else:
                wt("gpsimd", SE, ev[("u", "wqi", NI - 1)])
                wt("gpsimd", VE, ev[("u", "cmb", NI - 1)])
            tk = QG.dma(out=wld, in_=wd_d[hc * 128:(hc + 1) * 128, :])
            ctx.wait_tok("vector", tk)
            t0 = wF(SILU_O + par * 2, 1)
            t1 = wF(SILU_O + par * 2 + 1, 1)
            if hc >= 2:
                wt("vector", SE, ev[("d", "w1", hc - 2)])
            o("vector", "reduce_max", out=t0, in_=wld, axis=AX.X,
              apply_absolute_value=True)
            dr("vector")
            o("vector", "reciprocal", out=t1, in_=t0)
            dr("vector")
            o("vector", "tensor_scalar_mul", out=t1, in0=t1, scalar1=127.0)
            o("vector", "tensor_scalar_mul", out=cF(swd_o + hc, 1), in0=t0,
              scalar1=1.0 / 127.0, inc=VE)
            dr("vector")
            ev[("d", "wq", hc)] = VE.n
            wt("scalar", VE, VE.n)
            o("scalar", "activation", out=wld, in_=wld, func=AF.Identity,
              scale=t1, bias=cF(magic_o, 1), inc=SE)
            ev[("d", "w1", hc)] = SE.n
            dr("scalar")
            o("scalar", "activation", out=wdqi[:, hc * IH:(hc + 1) * IH], in_=wld,
              func=AF.Identity, bias=cF(nmagic_o, 1), inc=SE)
            ev[("d", "wqi", hc)] = SE.n
        ev["wdq"] = SE.n
        for ic in range(NI):
            par = ic % 2
            ps = ps_tp0 if par == 0 else ps_tp1
            wt("tensor", SE, ev["wdq"])
            if ic < 2:
                wt("tensor", VE, VE.n)
            if ic >= 2:
                wt("tensor", SE, ev[("d", "ev", ic - 2)])
            for hc in range(8):
                o("tensor", "transpose", out=ps[:, hc * 128:(hc + 1) * 128],
                  in_=wdqi[:, hc * IH + ic * 128:hc * IH + (ic + 1) * 128],
                  identity=identb, inc=(TE if hc == 7 else None))
            ev[("d", "tp", ic)] = TE.n
            stg = wB(WLD_O + (ic % 2) * 512, D)
            wt("scalar", TE, TE.n)
            if ic >= 2:
                ctx.wait_tok("scalar", ev[("d", "st", ic - 2)])
            o("scalar", "activation", out=stg, in_=ps[:], func=AF.Copy, inc=SE)
            ev[("d", "ev", ic)] = SE.n
            wt("scalar", SE, SE.n)
            ev[("d", "st", ic)] = QS.dma(out=scr_d[ic], in_=stg)
        ev[("d", "done")] = [ev[("d", "st", i)] for i in range(NI)]
        ev["prepd_T"] = TE.n

    def layernorm(ph, absmax, ybf_consumer=None):
        ctx.wait_tok("vector", tk_x)
        ctx.wait_tok("scalar", tk_x)
        mcols = cF(mr_o, TC)
        rcols = cF(mr_o + TC, TC)
        vcols = cF(lnv_o, TC)
        for tc in range(TC):
            o("vector", "reduce_sum", out=cF(mr_o + tc, 1),
              in_=X[:, tc * D:(tc + 1) * D], axis=AX.X)
        dr("vector")
        o("vector", "tensor_scalar_mul", out=mcols, in0=mcols, scalar1=1.0 / D,
          inc=VE)
        for tc in range(TC):
            par = tc % 2
            sq = cF(tmpa_o, D) if par == 0 else cF(ytile_o, D)
            if tc >= 2:
                wt("scalar", VE, ev[(ph, "lnsq", tc - 2)])
            o("scalar", "activation", out=sq, in_=X[:, tc * D:(tc + 1) * D],
              func=AF.Square, inc=SE)
            wt("vector", SE, SE.n)
            o("vector", "reduce_sum", out=cF(lnv_o + tc, 1), in_=sq, axis=AX.X,
              inc=VE)
            ev[(ph, "lnsq", tc)] = VE.n
        dr("vector")
        m2c = cF(lnm2_o, TC)
        wt("scalar", VE, VE.n)
        o("scalar", "activation", out=m2c, in_=mcols, func=AF.Square, inc=SE)
        wt("vector", SE, SE.n)
        o("vector", "scalar_tensor_tensor", out=vcols, in0=vcols, scalar=1.0 / D,
          in1=m2c, op0=ALU.mult, op1=ALU.subtract)
        dr("vector")
        o("vector", "tensor_scalar_add", out=vcols, in0=vcols,
          scalar1=cF(eps_o, 1), inc=VE)
        wt("scalar", VE, VE.n)
        o("scalar", "activation", out=vcols, in_=vcols, func=AF.Sqrt, inc=SE)
        wt("vector", SE, SE.n)
        o("vector", "reciprocal", out=rcols, in_=vcols)
        dr("vector")
        if absmax:
            for tc in range(TC):
                yt = cF(ytile_o, D) if tc % 2 else cF(tmpa_o, D)
                o("vector", "tensor_scalar", out=yt,
                  in0=X[:, tc * D:(tc + 1) * D], scalar1=cF(mr_o + tc, 1),
                  scalar2=cF(mr_o + TC + tc, 1), op0=ALU.subtract, op1=ALU.mult)
                dr("vector")
                o("vector", "reduce_max", out=cF(ymax_o + tc, 1), in_=yt,
                  axis=AX.X, apply_absolute_value=True, inc=VE)
            dr("vector")
        if ybf_consumer is not None:
            for tc in range(TC):
                if tc >= 1:
                    wt("vector", TE, ev[(ph, "ytp", tc - 1)])
                yt = cF(ytile_o, D)
                o("vector", "tensor_scalar", out=yt,
                  in0=X[:, tc * D:(tc + 1) * D], scalar1=cF(mr_o + tc, 1),
                  scalar2=cF(mr_o + TC + tc, 1), op0=ALU.subtract, op1=ALU.mult)
                dr("vector")
                ybf = cB(ytile_o, D)
                o("vector", "tensor_copy", out=ybf, in_=yt, inc=VE)
                ybf_consumer(tc, VE.n)
        ev[(ph, "ln")] = VE.n

    def global_scale(idx, max_off, ncols, sig_col, r127_slot, s_slot, s_mult):
        mx = cF(sc_o + 10, 1)
        dr("vector")
        o("vector", "reduce_max", out=mx, in_=cF(max_off, ncols), axis=AX.X, inc=VE)
        wt("tensor", VE, VE.n)
        wt("tensor", VE, ev.get("avb_free", 0))
        o("tensor", "transpose", out=ps_ava[0:1, 0:128], in_=mx, identity=identf,
          inc=TE)
        wt("vector", TE, TE.n)
        o("vector", "reduce_max", out=r0(4), in_=ps_ava[0:1, 0:128], axis=AX.X,
          inc=VE)
        wt("gpsimd", VE, VE.n)
        tk1 = QG.dma(out=cc_in[idx], in_=r0(4))
        ctx.wait_tok("gpsimd", tk1)
        o("gpsimd", "collective_compute", "AllGather", ALU.bypass,
          replica_groups=[list(range(n_cores))],
          ins=[cc_in[idx]], outs=[cc_out[idx]], inc=CCS)
        wt("gpsimd", CCS, CCS.n)
        tk2 = QG.dma(out=r0(8, n_cores), in_=cc_out[idx])
        ctx.wait_tok("vector", tk2)
        o("vector", "reduce_max", out=r0(5), in_=r0(8, n_cores), axis=AX.X)
        dr("vector")
        wt("vector", SE, ev["sig"])
        o("vector", "tensor_mul", out=r0(6), in0=r0(5), in1=r0(sig_col))
        dr("vector")
        o("vector", "reciprocal", out=r0(7), in_=r0(6))
        dr("vector")
        o("vector", "tensor_scalar_mul", out=r0(16), in0=r0(7), scalar1=127.0)
        o("vector", "tensor_scalar_mul", out=r0(17), in0=r0(6),
          scalar1=float(s_mult / 127.0), inc=VE)
        wt("tensor", VE, VE.n)
        o("tensor", "matmul", out=ps_ava[:, 128:129], lhsT=onesf[0:1, :],
          rhs=r0(16), start=True, stop=True)
        o("tensor", "matmul", out=ps_ava[:, 129:130], lhsT=onesf[0:1, :],
          rhs=r0(17), start=True, stop=True, inc=TE)
        wt("vector", TE, TE.n)
        o("vector", "tensor_copy", out=scv(r127_slot), in_=ps_ava[:, 128:129])
        o("vector", "tensor_copy", out=scv(s_slot), in_=ps_ava[:, 129:130], inc=VE)
        dr("vector")
        ev["avb_free"] = VE.n
        return VE.n

    def ffn_pre(p, cc_base):
        ph = f"f{p}"
        layernorm(ph, absmax=True)
        v_sc = global_scale(cc_base, ymax_o, TC, 2, SC_R127X, SC_SX, 1.0)



    def ffn_mm(p, cc_base):
        ph = f"f{p}"
        if p == 2:
            dr("vector")
            o("vector", "tensor_tensor", out=cF(combg_o, NI), in0=cF(swg_o, NI),
              in1=scv(SC_SX).broadcast_to([128, NI]), op=ALU.mult)
            o("vector", "tensor_tensor", out=cF(combu_o, NI), in0=cF(swu_o, NI),
              in1=scv(SC_SX).broadcast_to([128, NI]), op=ALU.mult, inc=VE)
            dr("vector")
            ev[(ph, "vcomb")] = VE.n
        xqT = wB(XQT_O, 8 * T)
        prev_sQ = 0
        for tc in range(TC):
            xt = X[:, tc * D:(tc + 1) * D]
            qt = cF(tmpa_o, D)
            if tc == 0:
                wt("vector", TE, TE.n)
            if tc >= 1:
                wt("vector", TE, ev[(ph, "ytp", tc - 1)])
                wt("vector", SE, prev_sQ)
            o("vector", "tensor_scalar", out=qt, in0=xt,
              scalar1=cF(mr_o + tc, 1), scalar2=cF(mr_o + TC + tc, 1),
              op0=ALU.subtract, op1=ALU.mult)
            dr("vector")
            sQ = quant3(qt, cB(ytile_o, D), scv(SC_R127X), qt)
            prev_sQ = sQ
            par = tc % 2
            ps = ps_tp0 if par == 0 else ps_tp1
            wt("tensor", SE, sQ)
            if tc == 0:
                wt("tensor", VE, VE.n)
            if tc >= 2:
                wt("tensor", VE, ev[(ph, "xqev", tc - 2)])
            for hc in range(8):
                o("tensor", "transpose", out=ps[:, hc * 128:(hc + 1) * 128],
                  in_=cB(ytile_o, D)[:, hc * 128:(hc + 1) * 128], identity=identb,
                  inc=(TE if hc == 7 else None))
            ev[(ph, "ytp", tc)] = TE.n
            wt("vector", TE, TE.n)
            out_ap = AP(tensor=W.bitcast(BF16).tensor,
                        offset=2 * XQT_O + tc * 128,
                        ap=[[2 * WORK_SZ, 128], [T, 8], [1, 128]])
            o("vector", "tensor_copy", out=out_ap,
              in_=ps[:].rearrange("p (h j) -> p h j", h=8), inc=VE)
            ev[(ph, "xqev", tc)] = VE.n
        ev[(ph, "xq")] = VE.n
        xqT = wB(XQT_O, 8 * T)

        def xq_ap(hc, t0, tw):
            return xqT.rearrange("p (h t) -> p h t", h=8)[:, hc, t0:t0 + tw]

        def prep_chunk(tag, ic):
            par = ic % 2
            w_dram, scr, sw_off = ((wg_d, scr_g, swg_o) if tag == "g" else
                                   (wu_d, scr_u, swu_o))
            wldo = (WLD_O if tag == "g" else WQT_O + 4) + par * D
            wld = wF(wldo, D)
            tsl = WQT_O + 2052 + (0 if tag == "g" else 4) + par * 2
            t0, t1 = wF(tsl, 1), wF(tsl + 1, 1)
            if ic >= 2:
                wt("gpsimd", SE, ev[(tag, "wqi", ic - 2)])
            tk = QG.dma(out=wld, in_=w_dram[ic * 128:(ic + 1) * 128, :])
            ctx.wait_tok("vector", tk)
            if ic >= 2:
                wt("vector", SE, ev[(tag, "w1", ic - 2)])
            o("vector", "reduce_max", out=t0, in_=wld, axis=AX.X,
              apply_absolute_value=True)
            dr("vector")
            o("vector", "reciprocal", out=t1, in_=t0)
            dr("vector")
            o("vector", "tensor_scalar_mul", out=t1, in0=t1, scalar1=127.0)
            o("vector", "tensor_scalar_mul", out=cF(sw_off + ic, 1), in0=t0,
              scalar1=1.0 / 127.0, inc=VE)
            v_t1 = VE.n
            cmb_o = combg_o if tag == "g" else combu_o
            o("vector", "scalar_tensor_tensor", out=cF(cmb_o + ic, 1), in0=t0,
              scalar=1.0 / 127.0, in1=scv(SC_SX), op0=ALU.mult, op1=ALU.mult,
              inc=VE)
            ev[(tag, "cmb", ic)] = VE.n
            dr("vector")
            wt("scalar", VE, v_t1)
            ctx.wait_tok("scalar", tk)
            o("scalar", "activation", out=wld, in_=wld, func=AF.Identity,
              scale=t1, bias=cF(magic_o, 1), inc=SE)
            ev[(tag, "w1", ic)] = SE.n
            dr("scalar")
            if tag == "u":
                wt("scalar", TE, ev[("g", "tp", ic)])
            elif ic >= 2:
                wt("scalar", TE, ev[("u", "tp", ic - 2)])
            wqi = wB(WQI_O + par * 512, D)
            o("scalar", "activation", out=wqi, in_=wld, func=AF.Identity,
              bias=cF(nmagic_o, 1), inc=SE)
            ev[(tag, "wqi", ic)] = SE.n
            ps = ps_tp0 if tag == "g" else ps_tp1
            wt("tensor", SE, ev[(tag, "wqi", ic)])
            if ic == 0:
                wt("tensor", VE,
                   ev[(ph, "xqev", TC - 2 if tag == "g" else TC - 1)])
            else:
                wt("tensor", SE, ev[(tag, "evac", ic - 1)])
            for hc in range(8):
                o("tensor", "transpose", out=ps[:, hc * 128:(hc + 1) * 128],
                  in_=wqi[:, hc * 128:(hc + 1) * 128], identity=identb,
                  inc=(TE if hc == 7 else None))
            ev[(tag, "tp", ic)] = TE.n
            stg = wB(STG_O + (1024 if tag == "u" else 0) + par * 512, D)
            wt("scalar", TE, TE.n)
            if ic >= 2:
                ctx.wait_tok("scalar", ev[(tag, "st", ic - 2)])
                wt("scalar", TE, ev[(ph, "gumm", ic - 2)])
            o("scalar", "activation", out=stg, in_=ps[:], func=AF.Copy, inc=SE)
            ev[(tag, "evac", ic)] = SE.n
            wt("scalar", SE, SE.n)
            ev[(tag, "st", ic)] = QS.dma(out=scr[ic], in_=stg)

        def emit_mm(ic):
            par = ic % 2
            stg_g = wB(STG_O + par * 512, D)
            stg_u = wB(STG_O + 1024 + par * 512, D)
            if p == 1:
                wt("tensor", SE, ev[("g", "evac", ic)])
                wt("tensor", SE, ev[("u", "evac", ic)])
            else:
                ctx.wait_toks("tensor", ev[("gu", "ld", ic)])
            for tt in range(NTT):
                u = ic * NTT + tt
                gslot = ps_big[:, (u % 2) * 512:(u % 2) * 512 + TTW]
                uslot = ps_big[:, 1024 + (u % 2) * 512:1024 + (u % 2) * 512 + TTW]
                if ic == 0:
                    wt("tensor", VE,
                       ev[(ph, "xqev", min(TC - 1, (tt + 1) * NSUB - 1))])
                if u >= 2:
                    pic, ptt = divmod(u - 2, NTT)
                    wt("tensor", VE, ev[(ph, "hout", pic, ptt)])
                for hc in range(8):
                    o("tensor", "matmul", out=gslot,
                      lhsT=stg_g[:, hc * 128:(hc + 1) * 128],
                      rhs=xq_ap(hc, tt * TTW, TTW), start=(hc == 0), stop=(hc == 7))
                for hc in range(8):
                    o("tensor", "matmul", out=uslot,
                      lhsT=stg_u[:, hc * 128:(hc + 1) * 128],
                      rhs=xq_ap(hc, tt * TTW, TTW), start=(hc == 0), stop=(hc == 7),
                      inc=(TE if hc == 7 else None))
                ev[(ph, "gumm", ic)] = TE.n
                par2 = u % 2
                silu = wF(SILU_O + par2 * 512, TTW)
                hout = wF(HOUT_O + par2 * 512, TTW)
                wt("scalar", TE, TE.n)
                wt("scalar", VE, ev[("g", "cmb", ic)] if p == 1 else
                   ev[(ph, "vcomb")])
                if u >= 2:
                    pic, ptt = divmod(u - 2, NTT)
                    wt("scalar", VE, ev[(ph, "hout", pic, ptt)])
                o("scalar", "activation", out=silu, in_=gslot, func=AF.Silu,
                  scale=cF(combg_o + ic, 1), inc=SE)
                wt("vector", SE, SE.n)
                if u >= 2:
                    pic, ptt = divmod(u - 2, NTT)
                    ctx.wait_tok("vector", ev[(ph, "hdma", pic, ptt)])
                o("vector", "scalar_tensor_tensor", out=hout, in0=uslot,
                  scalar=cF(combu_o + ic, 1), in1=silu, op0=ALU.mult,
                  op1=ALU.mult, inc=VE)
                ev[(ph, "hout", ic, tt)] = VE.n
                dr("vector")
                o("vector", "reduce_max", out=cF(hmax_o + u, 1), in_=hout,
                  axis=AX.X, apply_absolute_value=True, inc=VE)
                wt("scalar", VE, VE.n)
                if p == 2 and u == 0:
                    ctx.wait_toks("scalar", QL.snapshot())
                ev[(ph, "hdma", ic, tt)] = QS.dma(
                    out=scr_h[ic, :, tt * TTW:(tt + 1) * TTW], in_=hout)

        if p == 1:
            for j in range(NI + 2):
                if j >= 2:
                    emit_mm(j - 2)
                if j < NI:
                    prep_chunk("g", j)
                    prep_chunk("u", j)
        else:
            for ic in range(NI):
                par = ic % 2
                ctx.wait_tok("sync", ev[("g", "st", ic)])
                ctx.wait_tok("sync", ev[("u", "st", ic)])
                if ic >= 2:
                    wt("sync", TE, ev[(ph, "gumm", ic - 2)])
                if ic < 2:
                    wt("sync", TE, ev[("f1", "guload_ok", ic)])
                tk_sg = QL.dma(out=wB(STG_O + par * 512, D), in_=scr_g[ic])
                tk_su = QL.dma(out=wB(STG_O + 1024 + par * 512, D),
                               in_=scr_u[ic])
                ev[("gu", "ld", ic)] = [tk_sg, tk_su]
                emit_mm(ic)
        ev[(ph, "h_all")] = QS.snapshot()
        if p == 1:
            ev[("f1", "guload_ok", 0)] = TE.n
            ev[("f1", "guload_ok", 1)] = TE.n

    def ffn_down(p, cc_base):
        ph = f"f{p}"
        ctx.wait_toks("sync", ev[("d", "done")])
        wt("sync", TE, ev["prepd_T"])
        wdt_toks = []
        for ic in range(NI):
            wdt_toks.append(QL.dma(out=wB(WDT_O + ic * 512, D), in_=scr_d[ic]))

        v_sc2 = global_scale(cc_base + 1, hmax_o, NI * NTT, 3, SC_R127H,
                             SC_SH05, 0.5)

        Ssel = wF(SSEL_O, 1024)
        ctx.wait_toks("gpsimd", ev[(ph, "h_all")])
        wt("gpsimd", VE, VE.n)
        o("gpsimd", "memset", Ssel, 1.0)
        dr("gpsimd")
        o("gpsimd", "affine_select", out=Ssel, in_=Ssel, pattern=[[1, 1024]],
          compare_op=ALU.is_ge, fill=0.0, base=0, channel_multiplier=-128)
        dr("gpsimd")
        o("gpsimd", "affine_select", out=Ssel, in_=Ssel, pattern=[[-1, 1024]],
          compare_op=ALU.is_ge, fill=0.0, base=127, channel_multiplier=128)
        dr("gpsimd")
        o("gpsimd", "engine_nop", inc=GE)
        g_sel = GE.n
        wt("tensor", VE, v_sc2)
        o("tensor", "transpose", out=ps_avb[0:8, 0:128], in_=cF(swd_o, 8),
          identity=identf, inc=TE)
        wt("vector", TE, TE.n)
        o("vector", "tensor_copy", out=cF(swdrow_o, 128)[0:8, :],
          in_=ps_avb[0:8, 0:128], inc=VE)
        dr("vector")
        rhs2 = wF(RHS2_O, 1024)
        wt("vector", GE, g_sel)
        o("vector", "tensor_tensor", out=rhs2[0:8, :].rearrange(
            "p (r j) -> p r j", r=8), in0=Ssel[0:8, :].rearrange(
            "p (r j) -> p r j", r=8),
          in1=cF(swdrow_o, 128)[0:8, :].unsqueeze(1).broadcast_to([8, 8, 128]),
          op=ALU.mult, inc=VE)
        dr("vector")
        wt("tensor", VE, VE.n)
        o("tensor", "matmul", out=ps_ava[:, 0:512],
          lhsT=onesf[0:8, :], rhs=rhs2[0:8, 0:512],
          start=True, stop=True)
        o("tensor", "matmul", out=ps_avb[:, 0:512],
          lhsT=onesf[0:8, :], rhs=rhs2[0:8, 512:1024],
          start=True, stop=True, inc=TE)
        t_swdb2 = TE.n
        wt("vector", TE, t_swdb2)
        o("vector", "tensor_scalar_mul", out=cF(swdb2_o, 512),
          in0=ps_ava[:, 0:512], scalar1=scv(SC_SH05))
        o("vector", "tensor_scalar_mul", out=cF(swdb2_o + 512, 512),
          in0=ps_avb[:, 0:512], scalar1=scv(SC_SH05), inc=VE)
        dr("vector")
        v_swdb2 = VE.n
        ev["avb_free"] = VE.n

        ngrp = (NI + 3) // 4
        ctx.wait_toks("scalar", ev[(ph, "h_all")])

        def emit_loadquant_group(tt, g):
            i0, i1 = g * 4, min(NI, g * 4 + 4)
            gu = tt * ngrp + g
            par = gu % 2
            hld = wF(HLD_O + par * 2048, (i1 - i0) * TTW)
            if gu >= 2:
                pt_, pg_ = divmod(gu - 2, ngrp)
                wt("sync", SE, ev[(ph, "hq", pt_, pg_)])
            ctx.wait_toks("sync", ev[(ph, "h_all")])
            if tt == 0 and g == 0:
                wt("sync", TE, ev[(ph, "gumm", NI - 1)])
                wt("sync", VE, VE.n)
                wt("sync", SE, SE.n)
                if p == 1:
                    wt("sync", VE, ev[("d", "wq", 7)])
                    wt("sync", SE, ev[("d", "wqi", 7)])
            tk_h = QL.dma(
                out=hld.rearrange("p (i t) -> p i t", i=i1 - i0),
                in_=scr_h[i0:i1, :, tt * TTW:(tt + 1) * TTW].transpose([1, 0, 2]))
            ctx.wait_tok("vector", tk_h)
            if g == 0:
                wt("vector", VE, v_sc2)
                if tt >= 2:
                    wt("scalar", TE, ev[(ph, "dmm", tt - 2)])
                    wt("vector", TE, ev[(ph, "dmm", tt - 2)])
            for ii in range(i0, i1):
                src_ = wF(HLD_O + par * 2048 + (ii - i0) * TTW, TTW)
                if (tt, g, ii) != (0, 0, 0):
                    wt("vector", SE, SE.n)
                quant3(src_, wB(HQ_O + (tt % 2) * NI * 256 + ii * 256, TTW),
                       scv(SC_R127H), cF(tmpa_o, TTW))
            ev[(ph, "hq", tt, g)] = SE.n

        def emit_loadquant(tt):
            for g in range(ngrp):
                emit_loadquant_group(tt, g)
            ev[(ph, "hqtt", tt)] = SE.n

        def emit_mms(tt):
            hq0 = HQ_O + (tt % 2) * NI * 256
            for tsub in range(NSUB):
                tc = tt * NSUB + tsub
                for hh in range(2):
                    k = tc * 2 + hh
                    slot = ps_big[:, (k % 3) * 512:(k % 3) * 512 + 512]
                    if k >= 3:
                        wt("tensor", VE, ev[(ph, "dres", (k - 3) // 2, (k - 3) % 2)])
                    wt("tensor", SE, ev[(ph, "hqtt", tt)])
                    if tt == 0 and k == 0:
                        ctx.wait_toks("tensor", wdt_toks)
                    for ii in range(NI):
                        o("tensor", "matmul", out=slot,
                          lhsT=wB(hq0 + ii * 256, TTW)[:, tsub * 128:(tsub + 1) * 128],
                          rhs=wB(WDT_O + ii * 512, D)[:, hh * 512:(hh + 1) * 512],
                          start=(ii == 0), stop=(ii == NI - 1),
                          inc=(TE if ii == NI - 1 else None))
                    ev[(ph, "dmm", tt)] = TE.n
                    par3 = k % 2
                    ob = wF(OUTB_O + par3 * 512, 512)
                    wt("vector", TE, TE.n)
                    wt("vector", VE, v_swdb2)
                    if p == 2 and k >= 2:
                        ctx.wait_tok("vector", ev[(ph, "odma", k - 2)])
                    o("vector", "tensor_tensor", out=ob, in0=slot,
                      in1=cF(swdb2_o + hh * 512, 512), op=ALU.mult)
                    dr("vector")
                    xsl = X[:, tc * D + hh * 512:tc * D + (hh + 1) * 512]
                    o("vector", "tensor_add", out=xsl, in0=xsl, in1=ob, inc=VE)
                    ev[(ph, "dres", tc, hh)] = VE.n
                    dr("vector")
                    if p == 2:
                        wt("scalar", VE, VE.n)
                        ev[(ph, "odma", k)] = QS.dma(
                            out=out_d[tc * 128:(tc + 1) * 128,
                                      hh * 512:(hh + 1) * 512],
                            in_=xsl)

        def emit_unit(tt, tsub, hh):
            hq0 = HQ_O + (tt % 2) * NI * 256
            tc = tt * NSUB + tsub
            k = tc * 2 + hh
            slot = ps_big[:, (k % 3) * 512:(k % 3) * 512 + 512]
            if k >= 3:
                wt("tensor", VE, ev[(ph, "dres", (k - 3) // 2, (k - 3) % 2)])
            wt("tensor", SE, ev[(ph, "hqtt", tt)])
            if tt == 0 and k == 0:
                ctx.wait_toks("tensor", wdt_toks)
            for ii in range(NI):
                o("tensor", "matmul", out=slot,
                  lhsT=wB(hq0 + ii * 256, TTW)[:, tsub * 128:(tsub + 1) * 128],
                  rhs=wB(WDT_O + ii * 512, D)[:, hh * 512:(hh + 1) * 512],
                  start=(ii == 0), stop=(ii == NI - 1),
                  inc=(TE if ii == NI - 1 else None))
            ev[(ph, "dmm", tt)] = TE.n
            par3 = k % 2
            ob = wF(OUTB_O + par3 * 512, 512)
            wt("vector", TE, TE.n)
            wt("vector", VE, v_swdb2)
            if p == 2 and k >= 2:
                ctx.wait_tok("vector", ev[(ph, "odma", k - 2)])
            o("vector", "tensor_tensor", out=ob, in0=slot,
              in1=cF(swdb2_o + hh * 512, 512), op=ALU.mult)
            dr("vector")
            xsl = X[:, tc * D + hh * 512:tc * D + (hh + 1) * 512]
            o("vector", "tensor_add", out=xsl, in0=xsl, in1=ob, inc=VE)
            ev[(ph, "dres", tc, hh)] = VE.n
            dr("vector")
            if p == 2:
                wt("scalar", VE, VE.n)
                ev[(ph, "odma", k)] = QS.dma(
                    out=out_d[tc * 128:(tc + 1) * 128,
                              hh * 512:(hh + 1) * 512],
                    in_=xsl)

        units = [(tsub, hh) for tsub in range(NSUB) for hh in range(2)]
        emit_loadquant(0)
        for tt in range(NTT):
            if tt + 1 < NTT:
                nu = len(units)
                for gi in range(ngrp):
                    emit_loadquant_group(tt + 1, gi)
                    for (tsub, hh) in units[gi * nu // ngrp:(gi + 1) * nu // ngrp]:
                        emit_unit(tt, tsub, hh)
                ev[(ph, "hqtt", tt + 1)] = SE.n
            else:
                for (tsub, hh) in units:
                    emit_unit(tt, tsub, hh)
        ev[(ph, "down_v")] = VE.n
        ev[(ph, "down_T")] = TE.n
        if p == 2:
            ev["outdma"] = QS.snapshot()

    def attn_block():
        ph = "a"
        y2T = wB(Y2T_O, 8 * T)
        kX = wB(KX_O, 8 * T)
        EQA = wB(TBL_O, T)
        EQB = wB(TBL_O + T // 2, T)
        I16 = W.bitcast(mybir.dt.int16)
        MB0 = I16[:, 2 * MSK_O:2 * MSK_O + QT]
        MB1 = I16[:, 2 * MSK_O + QT:2 * MSK_O + 2 * QT]
        kmix = wB(KMIX_O, 8 * 256)
        scm = wB(SCM_O, 4 * QT)
        wbuf = wB(WBUF_O, 8 * D)
        exps = wB(EXPS_O, 2 * TC * QT)

        dtmp = wF(WBUF_O, T)
        wt("gpsimd", TE, ev[("f1", "down_T")])
        wt("gpsimd", VE, ev[("f1", "down_v")])
        o("gpsimd", "iota", dtmp, pattern=[[1, T]], base=0,
          channel_multiplier=0, allow_small_or_imprecise_dtypes=True)
        o("gpsimd", "memset", MB0, 1)
        o("gpsimd", "memset", MB1, 1)
        dr("gpsimd")
        o("gpsimd", "affine_select", out=MB0, in_=MB0, pattern=[[1, QT]],
          compare_op=ALU.is_ge, fill=0.0, base=-1, channel_multiplier=-1)
        o("gpsimd", "affine_select", out=MB1, in_=MB1, pattern=[[1, QT]],
          compare_op=ALU.is_ge, fill=0.0, base=-129, channel_multiplier=-1)
        dr("gpsimd")
        o("gpsimd", "engine_nop", inc=GE)
        wt("scalar", GE, GE.n)
        o("scalar", "activation", out=EQA, in_=dtmp, func=AF.Exp, scale=LAM)
        o("scalar", "activation", out=EQB, in_=dtmp, func=AF.Exp, scale=-LAM,
          inc=SE)
        ev["tbl"] = SE.n

        def y2_consumer(tc, vtag):
            par = tc % 2
            ps = ps_tp0 if par == 0 else ps_tp1
            wt("tensor", VE, vtag)
            if tc == 0:
                wt("tensor", VE, VE.n)
            if tc >= 2:
                wt("tensor", VE, ev[(ph, "y2ev", tc - 2)])
            for hc in range(8):
                o("tensor", "transpose", out=ps[:, hc * 128:(hc + 1) * 128],
                  in_=cB(ytile_o, D)[:, hc * 128:(hc + 1) * 128], identity=identb,
                  inc=(TE if hc == 7 else None))
            ev[(ph, "ytp", tc)] = TE.n
            wt("vector", TE, TE.n)
            out_ap = AP(tensor=W.bitcast(BF16).tensor,
                        offset=2 * Y2T_O + tc * 128,
                        ap=[[2 * WORK_SZ, 128], [T, 8], [1, 128]])
            o("vector", "tensor_copy", out=out_ap,
              in_=ps[:].rearrange("p (h j) -> p h j", h=8), inc=VE)
            ev[(ph, "y2ev", tc)] = VE.n

        layernorm(ph, absmax=False, ybf_consumer=y2_consumer)
        ev["y2T"] = VE.n

        def y2_ap(kc, t0, tw):
            return y2T.rearrange("p (k t) -> p k t", k=8)[:, kc, t0:t0 + tw]

        wt("gpsimd", VE, ev["y2T"])
        wt("gpsimd", SE, ev["tbl"])
        tk_wq = QG.dma(out=wbuf.rearrange("p (k d) -> p k d", k=8),
                       in_=Wq_d.rearrange("(k p) d -> p k d", p=128))
        ctx.wait_tok("tensor", tk_wq)
        wt("tensor", VE, ev["y2T"])
        QS0 = KMIX_O
        for dc in range(8):
            for tt in range(NTT):
                u = dc * NTT + tt
                slot = ps_big[:, (u % 4) * 512:(u % 4) * 512 + TTW]
                if u >= 4:
                    wt("tensor", SE, ev[(ph, "qev", u - 4)])
                for kc in range(8):
                    o("tensor", "matmul", out=slot,
                      lhsT=wbuf[:, kc * D + dc * 128:kc * D + (dc + 1) * 128],
                      rhs=y2_ap(kc, tt * TTW, TTW), start=(kc == 0), stop=(kc == 7),
                      inc=(TE if kc == 7 else None))
                wt("scalar", TE, TE.n)
                if u >= 2:
                    ctx.wait_tok("scalar", ev[(ph, "qdma", u - 2)])
                qstg = wB(QS0 + (u % 2) * 256, TTW)
                o("scalar", "activation", out=qstg, in_=slot, func=AF.Copy,
                  scale=SCALE_Q, inc=SE)
                ev[(ph, "qev", u)] = SE.n
                wt("scalar", SE, SE.n)
                ev[(ph, "qdma", u)] = QS.dma(
                    out=scr_q[dc, :, tt * TTW:(tt + 1) * TTW], in_=qstg)
        ev["qdma"] = QS.snapshot()
        ev["q_T"] = TE.n

        wt("gpsimd", TE, ev["q_T"])
        tk_wk = QG.dma(out=wbuf.rearrange("p (k d) -> p k d", k=8),
                       in_=Wk_d.rearrange("(k p) d -> p k d", p=128))
        ctx.wait_tok("vector", tk_wk)
        wvc = wB(QS0 + 512, 1024)
        ctx.wait_toks("gpsimd", ev["qdma"])
        last_tt = 0
        for kc8 in range(8):
            if kc8 > 0:
                wt("gpsimd", VE, last_tt)
            tk_wv = QG.dma(out=wvc, in_=Wv_d[kc8 * 128:(kc8 + 1) * 128, :])
            ctx.wait_tok("vector", tk_wv)
            o("vector", "tensor_add", out=wbuf[:, kc8 * D:(kc8 + 1) * D],
              in0=wbuf[:, kc8 * D:(kc8 + 1) * D], in1=wvc, inc=VE)
            dr("vector")
            last_tt = VE.n
        v_wsum = VE.n
        wdkv = wB(VTOK_O, 2048)
        wt("gpsimd", SE, ev[(ph, "qev", 8 * NTT - 1)])
        tk_wdkv = QG.dma(out=wdkv.rearrange("p (k l) -> p k l", k=8),
                         in_=Wdkv_d.rearrange("(k p) l -> p k l", p=128))

        latT = wB(LATT_O, 2 * T)
        kvbf = wB(QSTR_O, 8 * 512)
        for tt in range(NTT):
            for dc in range(8):
                slot = ps_big[:, (dc % 2) * 512:(dc % 2) * 512 + TTW]
                u = tt * 8 + dc
                if u >= 2:
                    pu = u - 2
                    wt("tensor", VE, ev[(ph, "kvev", pu // 8, pu % 8)])
                if u == 0:
                    wt("tensor", VE, v_wsum)
                for kc in range(8):
                    o("tensor", "matmul", out=slot,
                      lhsT=wbuf[:, kc * D + dc * 128:kc * D + (dc + 1) * 128],
                      rhs=y2_ap(kc, tt * TTW, TTW), start=(kc == 0), stop=(kc == 7),
                      inc=(TE if kc == 7 else None))
                wt("vector", TE, TE.n)
                if tt > 0 and dc == 0:
                    wt("vector", TE, ev[(ph, "latmm", tt - 1)])
                o("vector", "tensor_copy", out=kvbf[:, dc * TTW:(dc + 1) * TTW],
                  in_=slot, inc=VE)
                ev[(ph, "kvev", tt, dc)] = VE.n
            wt("tensor", VE, ev[(ph, "kvev", tt, 7)])
            if tt == 0:
                ctx.wait_tok("tensor", tk_wdkv)
            for lc in range(2):
                slot = ps_big[:, 1024 + lc * 512:1024 + lc * 512 + TTW]
                for kc in range(8):
                    o("tensor", "matmul", out=slot,
                      lhsT=wdkv[:, kc * 256 + lc * 128:kc * 256 + (lc + 1) * 128],
                      rhs=kvbf[:, kc * TTW:(kc + 1) * TTW], start=(kc == 0),
                      stop=(kc == 7), inc=(TE if kc == 7 else None))
                ev[(ph, "latmm", tt)] = TE.n
                wt("vector", TE, TE.n)
                o("vector", "tensor_copy",
                  out=latT.rearrange("p (l t) -> p l t", l=2)[
                      :, lc, tt * TTW:(tt + 1) * TTW], in_=slot, inc=VE)
            ev[(ph, "latev", tt)] = VE.n
        ev["lat"] = VE.n

        def lat_ap(lc, t0, tw):
            return latT.rearrange("p (l t) -> p l t", l=2)[:, lc, t0:t0 + tw]

        wupk = wB(WBUF_O, 2 * D)
        wt("gpsimd", VE, ev["lat"])
        tk_upk = QG.dma(out=wupk.rearrange("p (l d) -> p l d", l=2),
                        in_=Wupk_d.rearrange("(l p) d -> p l d", p=128))
        ctx.wait_tok("tensor", tk_upk)
        wt("tensor", VE, ev["lat"])
        for dc in range(8):
            for tt in range(NTT):
                u = dc * NTT + tt
                slot = ps_big[:, (u % 4) * 512:(u % 4) * 512 + TTW]
                if u >= 4:
                    wt("tensor", VE, ev[(ph, "krev", u - 4)])
                for lc in range(2):
                    o("tensor", "matmul", out=slot,
                      lhsT=wupk[:, lc * D + dc * 128:lc * D + (dc + 1) * 128],
                      rhs=lat_ap(lc, tt * TTW, TTW), start=(lc == 0), stop=(lc == 1),
                      inc=(TE if lc == 1 else None))
                wt("vector", TE, TE.n)
                if u == 0:
                    wt("vector", SE, ev["tbl"])
                o("vector", "tensor_tensor",
                  out=kX.rearrange("p (d t) -> p d t", d=8)[
                      :, dc, tt * TTW:(tt + 1) * TTW],
                  in0=slot, in1=EQB[:, tt * TTW:(tt + 1) * TTW],
                  op=ALU.mult, inc=VE)
                ev[(ph, "krev", u)] = VE.n
        ev["krec"] = VE.n
        ev["krec_T"] = TE.n

        wupv = wB(WBUF_O, 2 * D)
        wt("gpsimd", TE, ev["krec_T"])
        tk_upv = QG.dma(out=wupv.rearrange("p (l d) -> p l d", l=2),
                        in_=Wupv_d.rearrange("(l p) d -> p l d", p=128))
        vtok = wB(VTOK_O, NH * TC * 65)
        wt("vector", TE, ev[(ph, "latmm", NTT - 1)])
        o("vector", "memset",
          vtok.rearrange("p (n c) -> p n c", c=65)[:, :, 64:65], 1.0, inc=VE)
        dr("vector")
        v_ones = VE.n
        ctx.wait_tok("tensor", tk_upv)
        wt("tensor", VE, v_ones)
        for tc in range(TC):
            for dh in range(2):
                u = tc * 2 + dh
                slot = ps_big[:, (u % 4) * 512:(u % 4) * 512 + 512]
                if u >= 4:
                    wt("tensor", VE, ev[(ph, "vtev", u - 4)])
                for lc in range(2):
                    o("tensor", "matmul", out=slot,
                      lhsT=lat_ap(lc, tc * 128, 128),
                      rhs=wupv[:, lc * D + dh * 512:lc * D + (dh + 1) * 512],
                      start=(lc == 0), stop=(lc == 1),
                      inc=(TE if lc == 1 else None))
                wt("vector", TE, TE.n)
                out_ap = AP(tensor=W.bitcast(BF16).tensor,
                            offset=2 * VTOK_O + (dh * 8 * TC + tc) * 65,
                            ap=[[2 * WORK_SZ, 128], [65 * TC, 8], [1, 64]])
                o("vector", "tensor_copy", out=out_ap,
                  in_=slot.rearrange("p (h e) -> p h e", h=8), inc=VE)
                ev[(ph, "vtev", u)] = VE.n
        ev["vtok"] = VE.n

        wo = wB(WBUF_O, 8 * D)
        wt("gpsimd", TE, TE.n)
        tk_wo = QG.dma(out=wo.rearrange("p (k d) -> p k d", k=8),
                       in_=Wo_d.rearrange("(k p) d -> p k d", p=128))

        qA = wB(QSTR_O, 8 * QT)
        qB = wB(QSTR_O + 1024, 8 * QT)
        exp_hist = []
        lkx = kX.rearrange("p (d t) -> p d t", d=8)
        NG2 = TC // 2

        def eqslice(tbl, qt):
            return AP(tensor=W.bitcast(BF16).tensor,
                      offset=2 * TBL_O + (0 if tbl is EQA else
                                          (T if tbl is EQB else 2 * T)) + qt * QT,
                      ap=[[2 * WORK_SZ, 128], [0, 8], [1, QT]])

        def do_av_mm(h, qt):
            hp, hi = h // 2, h % 2
            avps = ps_ava if hi == 0 else ps_avb
            avslot = avps[0:65, 0:QT]
            if hi == 0 and (qt, hp) > (0, 0):
                ph_, hp_ = (qt, hp - 1) if hp >= 1 else (qt - 1, NH // 2 - 1)
                wt("tensor", VE, ev[(ph, "avTT", ph_, hp_)])
            estrip = hi * TC * QT
            for g in range(NG2):
                wt("tensor", SE, ev[(ph, "exp", qt, hp, g)])
                if g == qt:
                    wt("tensor", VE, ev[(ph, "sel", qt, hp)])
                for j in range(2):
                    kc = g * 2 + j
                    o("tensor", "matmul", out=avslot,
                      lhsT=vtok[:, (h * TC + kc) * 65:(h * TC + kc) * 65 + 65],
                      rhs=exps[:, estrip + kc * QT:estrip + (kc + 1) * QT],
                      start=(kc == 0), stop=(kc == TC - 1),
                      inc=(TE if kc == TC - 1 else None))
            ev[(ph, "avmm", qt, h)] = TE.n

        def do_pair_norm(hp, qt):
            den0 = C[64:65, rec_o:rec_o + QT]
            den1 = C[64:65, rec_o + QT:rec_o + 2 * QT]
            wt("vector", TE, ev[(ph, "avmm", qt, 2 * hp)])
            o("vector", "tensor_copy", out=den0, in_=ps_ava[64:65, 0:QT],
              inc=VE)
            v_cp0 = VE.n
            wt("vector", TE, ev[(ph, "avmm", qt, 2 * hp + 1)])
            o("vector", "tensor_copy", out=den1, in_=ps_avb[64:65, 0:QT],
              inc=VE)
            v_cp1 = VE.n
            wt("tensor", VE, v_cp0)
            o("tensor", "matmul", out=ps_ava[0:64, 256:256 + QT],
              lhsT=onesf[64:65, 0:64], rhs=den0, start=True, stop=True)
            wt("tensor", VE, v_cp1)
            o("tensor", "matmul", out=ps_ava[64:128, 256:256 + QT],
              lhsT=onesf[64:65, 0:64], rhs=den1, start=True, stop=True, inc=TE)
            t_bc = TE.n
            rec128 = C[:, avtmp_o:avtmp_o + QT]
            wt("vector", TE, t_bc)
            o("vector", "reciprocal", out=rec128, in_=ps_ava[:, 256:256 + QT],
              inc=VE)
            dr("vector")
            outT = wB(OUTT_O + (qt % 2) * 1024, 8 * QT)
            oT = outT.rearrange("p (c t) -> p c t", c=8)
            o("vector", "tensor_tensor", out=oT[0:64, hp, :],
              in0=ps_ava[0:64, 0:QT], in1=rec128[0:64, :], op=ALU.mult)
            o("vector", "tensor_tensor", out=oT[64:128, hp, :],
              in0=ps_avb[0:64, 0:QT], in1=rec128[64:128, :], op=ALU.mult,
              inc=VE)
            ev[(ph, "avTT", qt, hp)] = VE.n

        for qt in range(NQT):
            ctx.wait_toks("sync", ev["qdma"])
            if qt >= 1:
                wt("sync", TE, ev[(ph, "qused", qt - 1)])
            else:
                wt("sync", TE, ev[(ph, "latmm", NTT - 1)])
                wt("sync", VE, ev["lat"])
            tk_q = QL.dma(
                out=qA.rearrange("p (d t) -> p d t", d=8),
                in_=scr_q[:, :, qt * QT:(qt + 1) * QT].transpose([1, 0, 2]))
            ctx.wait_tok("vector", tk_q)
            if qt >= 1:
                wt("vector", TE, ev[(ph, "qused", qt - 1)])
            o("vector", "tensor_tensor",
              out=qB.rearrange("p (d t) -> p d t", d=8),
              in0=qA.rearrange("p (d t) -> p d t", d=8),
              in1=eqslice(EQB, qt), op=ALU.mult)
            dr("vector")
            o("vector", "tensor_tensor",
              out=qA.rearrange("p (d t) -> p d t", d=8),
              in0=qA.rearrange("p (d t) -> p d t", d=8),
              in1=eqslice(EQA, qt), op=ALU.mult, inc=VE)
            v_qk = VE.n
            if qt >= 1:
                wt("vector", VE, ev[(ph, "kcv", qt - 1)])
            eqa_s = AP(tensor=W.bitcast(BF16).tensor,
                       offset=2 * TBL_O + 2 * qt * 128,
                       ap=[[2 * WORK_SZ, 128], [0, 8], [1, 256]])
            o("vector", "tensor_tensor",
              out=kmix.rearrange("p (d t) -> p d t", d=8),
              in0=AP(tensor=W.bitcast(BF16).tensor, offset=2 * KX_O + 2 * qt * 128,
                     ap=[[2 * WORK_SZ, 128], [T, 8], [1, 256]]),
              in1=eqa_s, op=ALU.mult)
            dr("vector")
            o("vector", "tensor_tensor",
              out=kmix.rearrange("p (d t) -> p d t", d=8),
              in0=kmix.rearrange("p (d t) -> p d t", d=8),
              in1=eqa_s, op=ALU.mult, inc=VE)
            v_km = VE.n
            for hp in range(NH // 2):
                hc = hp
                for g in range(NG2):
                    u = hp * NG2 + g
                    sslot = ps_big[:, (u % 2) * 1024:(u % 2) * 1024 + 4 * QT]
                    if len(exp_hist) >= 2:
                        wt("tensor", SE, exp_hist[-2])
                    if qt >= 1 and u < 2:
                        wt("tensor", VE,
                           ev[(ph, "xres", qt * (QT // 128) * 2 - 1)])
                    if hp == 0 and g == 0:
                        ctx.wait_tok("tensor", tk_q)
                        wt("tensor", VE, v_qk)
                        wt("tensor", VE, ev["krec"])
                        wt("tensor", VE, ev["vtok"])
                    for j in range(2):
                        kc = g * 2 + j
                        if kc >= 2 * qt + 2:
                            rhs_q = qA
                        elif kc <= 2 * qt - 1:
                            rhs_q = qB
                        else:
                            rhs_q = qA
                        rq = rhs_q.rearrange("p (d t) -> p d t", d=8)
                        for hi in range(2):
                            hr = hi * 64
                            o("tensor", "matmul",
                              out=sslot[:, hi * 512 + j * QT:hi * 512 + (j + 1) * QT],
                              lhsT=lkx[hr:hr + 64, hc, kc * 128:(kc + 1) * 128],
                              rhs=rq[hr:hr + 64, hc, :],
                              start=True, stop=True,
                              inc=(TE if (j == 1 and hi == 1) else None))
                    t_sc = TE.n
                    if g == qt:
                        if hp == 0:
                            wt("tensor", VE, v_km)
                        if hp >= 1:
                            wt("tensor", SE, ev[(ph, "expB", qt, hp - 1)])
                        elif qt >= 1:
                            wt("tensor", SE, ev[(ph, "expB", qt - 1, NH // 2 - 1)])
                        km = kmix.rearrange("p (d t) -> p d t", d=8)
                        qb = qB.rearrange("p (d t) -> p d t", d=8)
                        for j2 in range(2):
                            for hi in range(2):
                                hr = hi * 64
                                tp = ps_tp0f if hi == 0 else ps_tp1f
                                o("tensor", "matmul",
                                  out=tp[:, j2 * QT:(j2 + 1) * QT],
                                  lhsT=km[hr:hr + 64, hc, j2 * 128:(j2 + 1) * 128],
                                  rhs=qb[hr:hr + 64, hc, :],
                                  start=True, stop=True,
                                  inc=(TE if (j2 == 1 and hi == 1) else None))
                        t_b = TE.n
                        wt("scalar", TE, t_b)
                        if hp >= 1:
                            wt("scalar", VE, ev[(ph, "sel", qt, hp - 1)])
                        elif qt >= 1:
                            wt("scalar", VE, ev[(ph, "sel", qt - 1, NH // 2 - 1)])
                        o("scalar", "activation", out=scm[:, 0:2 * QT],
                          in_=ps_tp0f[:, 0:2 * QT], func=AF.Exp)
                        o("scalar", "activation", out=scm[:, 2 * QT:4 * QT],
                          in_=ps_tp1f[:, 0:2 * QT], func=AF.Exp, inc=SE)
                        ev[(ph, "expB", qt, hp)] = SE.n
                    wt("scalar", TE, t_sc)
                    if g == 0:
                        if hp >= 1:
                            wt("scalar", TE, ev[(ph, "avmm", qt, 2 * hp - 1)])
                        elif qt >= 1:
                            wt("scalar", TE, ev[(ph, "avmm", qt - 1, NH - 1)])
                    exp_out = AP(tensor=W.bitcast(BF16).tensor,
                                 offset=2 * EXPS_O + g * 2 * QT,
                                 ap=[[2 * WORK_SZ, 128], [TC * QT, 2], [1, 2 * QT]])
                    o("scalar", "activation", out=exp_out,
                      in_=sslot, func=AF.Exp, inc=SE)
                    ev[(ph, "exp", qt, hp, g)] = SE.n
                    exp_hist.append(SE.n)
                    if g == qt:
                        wt("scalar", VE, VE.n)
                        wt("vector", SE, SE.n)
                        for hi in range(2):
                            estrip = hi * TC * QT
                            for j2 in range(2):
                                kc = 2 * qt + j2
                                o("vector", "copy_predicated",
                                  out=exps[:, estrip + kc * QT:
                                           estrip + (kc + 1) * QT],
                                  mask=(MB0 if j2 == 0 else MB1),
                                  data=scm[:, (hi * 2 + j2) * QT:
                                           (hi * 2 + j2 + 1) * QT], inc=(
                                      VE if (hi == 1 and j2 == 1) else None))
                        ev[(ph, "sel", qt, hp)] = VE.n
                do_av_mm(2 * hp, qt)
                do_av_mm(2 * hp + 1, qt)
                do_pair_norm(hp, qt)
            if qt == NQT - 1:
                ev["kx_done"] = TE.n
            ev[(ph, "qused", qt)] = TE.n
            wt("vector", TE, ev[(ph, "qused", qt)])
            o("vector", "tensor_copy",
              out=AP(tensor=W.bitcast(BF16).tensor, offset=2 * KX_O + 2 * qt * 128,
                     ap=[[2 * WORK_SZ, 128], [T, 8], [1, 256]]),
              in_=kmix.rearrange("p (d t) -> p d t", d=8), inc=VE)
            dr("vector")
            ev[(ph, "kcv", qt)] = VE.n
            wt("tensor", VE, ev[(ph, "avTT", qt, NH // 2 - 1)])
            if qt == 0:
                ctx.wait_tok("tensor", tk_wo)
            outT = wB(OUTT_O + (qt % 2) * 1024, 8 * QT)
            loT = outT.rearrange("p (c t) -> p c t", c=8)
            for tsub in range(QT // 128):
                tc = (qt * QT) // 128 + tsub
                for dh in range(2):
                    u = qt * (QT // 128) * 2 + tsub * 2 + dh
                    slot = ps_big[:, 1024 + (u % 2) * 512:1024 + (u % 2) * 512 + 512]
                    if u >= 2:
                        wt("tensor", VE, ev[(ph, "xres", u - 2)])
                    for kc in range(8):
                        o("tensor", "matmul", out=slot,
                          lhsT=loT[:, kc, tsub * 128:(tsub + 1) * 128],
                          rhs=wo[:, kc * D + dh * 512:kc * D + (dh + 1) * 512],
                          start=(kc == 0), stop=(kc == 7),
                          inc=(TE if kc == 7 else None))
                    wt("vector", TE, TE.n)
                    xsl = X[:, tc * D + dh * 512:tc * D + (dh + 1) * 512]
                    o("vector", "tensor_add", out=xsl, in0=xsl, in1=slot, inc=VE)
                    ev[(ph, "xres", u)] = VE.n
                    dr("vector")
        ev["attn_v"] = VE.n
        ev["attn_T"] = TE.n

    ffn_pre(1, 0)
    ffn_mm(1, 0)
    prep_down()
    ffn_down(1, 0)
    attn_block()
    ffn_pre(2, 2)
    wt("vector", TE, ev["kx_done"])
    wt("scalar", TE, ev["kx_done"])
    wt("sync", TE, ev["attn_T"])
    wt("sync", VE, ev["attn_v"])
    wt("sync", SE, SE.n)
    wt("gpsimd", TE, ev["attn_T"])
    wt("gpsimd", VE, ev["attn_v"])
    ffn_mm(2, 2)
    ffn_down(2, 2)
    ctx.wait_toks("sync", ev["outdma"])

    with nc.Block() as block:
        @block.sync
        def _(eng):
            for fn in ctx.streams["sync"]:
                fn(eng)

        @block.gpsimd
        def _(eng):
            for fn in ctx.streams["gpsimd"]:
                fn(eng)

        @block.vector
        def _(eng):
            for fn in ctx.streams["vector"]:
                fn(eng)

        @block.scalar
        def _(eng):
            for fn in ctx.streams["scalar"]:
                fn(eng)

        @block.tensor
        def _(eng):
            for fn in ctx.streams["tensor"]:
                fn(eng)

    es.close()
    nc._dbg_sems = {"CCS": CCS.n, "TE": TE.n, "VE": VE.n, "SE": SE.n,
                    "GE": GE.n}
    return nc


def kernel(**inputs):
    n = 8
    x = np.ascontiguousarray(np.asarray(inputs["L0_attn_x"], dtype=np.float32))
    T = x.shape[1]
    nc = build(T=T, n_cores=n)
    wk = {k: np.ascontiguousarray(np.asarray(v, dtype=np.float32))
          for k, v in inputs.items() if k != "L0_attn_x"}
    in_maps = [dict(wk, L0_attn_x=np.ascontiguousarray(x[c])) for c in range(n)]
    res = run_bass_kernel_spmd(nc, in_maps, core_ids=list(range(n)))
    return np.stack([res.results[c]["out"] for c in range(n)], axis=0)


# version 20
# speedup vs baseline: 1.0546x; 1.0413x over previous
import numpy as np
import concourse.bass as bass
import concourse.mybir as mybir
from concourse.bass_utils import run_bass_kernel_spmd

F32 = mybir.dt.float32
BF16 = mybir.dt.bfloat16
AF = mybir.ActivationFunctionType
ALU = mybir.AluOpType
AX = mybir.AxisListType
AP = bass.AP

MAGIC = 12582912.0
QHI, QLO = 127.49, -128.49
D = 1024
NH, HD, DL = 16, 64, 256
NI, IH = 22, 2816
SCALE_Q = float(1.0 / (np.float64(np.sqrt(np.float32(1024.0))) + 1e-8))
LAM = 1e-4
EPS = 1e-5
QT = 256


class Sem:
    def __init__(self, h, step=1):
        self.h, self.n, self.step = h, 0, step


class Ctx:
    ENGS = ("sync", "gpsimd", "vector", "scalar", "tensor")

    def __init__(self):
        self.streams = {e: [] for e in self.ENGS}

    def op(self, eng, method, *args, inc=None, **kwargs):
        incs = []
        if inc is not None:
            for s in (inc if isinstance(inc, list) else [inc]):
                s.n += s.step
                incs.append((s.h, s.step))

        import traceback
        tb = traceback.extract_stack()[-2]

        def fn(e, _m=method, _a=args, _k=kwargs, _i=incs, _tb=tb):
            try:
                inst = getattr(e, _m)(*_a, **_k)
            except Exception:
                print(f"FAILING OP: {_m} emitted at {_tb.filename}:{_tb.lineno}")
                raise
            for (h, k) in _i:
                inst = inst.then_inc(h, k)
        self.streams[eng].append(fn)

    def wait(self, eng, sem, val=None):
        v = sem.n if val is None else val
        if v <= 0:
            return
        self.streams[eng].append(lambda e, _h=sem.h, _v=v: e.wait_ge(_h, _v))

    def drain(self, eng):
        self.streams[eng].append(lambda e: e.drain())

    def wait_tok(self, eng, tok):
        if tok is None:
            return
        s, v = tok
        self.wait(eng, s, v)

    def wait_toks(self, eng, toks):
        best = {}
        for t in toks:
            if t is None:
                continue
            s, v = t
            if v > best.get(id(s), (s, 0))[1]:
                best[id(s)] = (s, v)
        for s, v in best.values():
            self.wait(eng, s, v)


class DmaQ:

    def __init__(self, ctx, es, nc, name, eng, nlanes):
        self.ctx, self.eng = ctx, eng
        self.lanes = [Sem(es.enter_context(nc.semaphore(name + str(i))), 16)
                      for i in range(nlanes)]
        self.rr = 0

    def dma(self, out, in_, eng=None):
        e = eng or self.eng
        ln = self.lanes[self.rr]
        self.rr = (self.rr + 1) % len(self.lanes)
        if ln.n > 0:
            self.ctx.wait(e, ln, ln.n)
        self.ctx.op(e, "dma_start", out=out, in_=in_, inc=ln)
        return (ln, ln.n)

    def snapshot(self):
        return [(l, l.n) for l in self.lanes if l.n > 0]


def build(T=2048, n_cores=8):
    TC = T // 128
    NTT = max(1, T // 512)
    TTW = min(512, T)
    NQT = T // QT
    NSUB = TTW // 128

    nc = bass.Bass("TRN2", target_bir_lowering=False)

    x_in = nc.dram_tensor("L0_attn_x", [T, D], F32, kind="ExternalInput").ap()
    Wq_d = nc.dram_tensor("L0_attn_Wq", [D, D], F32, kind="ExternalInput").ap()
    Wk_d = nc.dram_tensor("L0_attn_Wk", [D, D], F32, kind="ExternalInput").ap()
    Wv_d = nc.dram_tensor("L0_attn_Wv", [D, D], F32, kind="ExternalInput").ap()
    Wo_d = nc.dram_tensor("L0_attn_Wo", [D, D], F32, kind="ExternalInput").ap()
    Wdkv_d = nc.dram_tensor("L0_attn_W_down_kv", [D, DL], F32, kind="ExternalInput").ap()
    Wupk_d = nc.dram_tensor("L0_attn_W_up_k", [DL, D], F32, kind="ExternalInput").ap()
    Wupv_d = nc.dram_tensor("L0_attn_W_up_v", [DL, D], F32, kind="ExternalInput").ap()
    clipx_d = nc.dram_tensor("L0_ffn_clip_x", [1], F32, kind="ExternalInput").ap()
    clipd_d = nc.dram_tensor("L0_ffn_clip_down", [1], F32, kind="ExternalInput").ap()
    wg_d = nc.dram_tensor("L0_ffn_w_gate", [IH, D], F32, kind="ExternalInput").ap()
    wu_d = nc.dram_tensor("L0_ffn_w_up", [IH, D], F32, kind="ExternalInput").ap()
    wd_d = nc.dram_tensor("L0_ffn_w_down", [D, IH], F32, kind="ExternalInput").ap()
    out_d = nc.dram_tensor("out", [T, D], F32, kind="ExternalOutput").ap()

    scr_g = nc.dram_tensor("scr_g", [NI, 128, D], BF16).ap()
    scr_u = nc.dram_tensor("scr_u", [NI, 128, D], BF16).ap()
    scr_d = nc.dram_tensor("scr_d", [NI, 128, D], BF16).ap()
    scr_h = nc.dram_tensor("scr_h", [NI, 128, T], F32).ap()
    scr_q = nc.dram_tensor("scr_q", [8, 128, T], BF16).ap()
    cc_in = [nc.dram_tensor(f"cc_in{i}", [1, 1], F32).ap() for i in range(4)]
    cc_out = [nc.dram_tensor(f"cc_out{i}", [1, n_cores], F32).ap() for i in range(4)]

    from contextlib import ExitStack
    es = ExitStack()

    X = es.enter_context(nc.sbuf_tensor("x_arena", [128, TC * D], F32)).ap()

    CONST_SZ = 3700
    C = es.enter_context(nc.sbuf_tensor("c_arena", [128, CONST_SZ], F32)).ap()
    _co = [0]

    def calloc(n):
        o_ = _co[0]
        _co[0] += n
        assert _co[0] <= CONST_SZ, _co[0]
        return o_

    identb_o = calloc(64)
    identf_o = calloc(128)
    onesf_o = calloc(128)
    swg_o = calloc(NI)
    swu_o = calloc(NI)
    swd_o = calloc(8)
    combg_o = calloc(NI)
    combu_o = calloc(NI)
    mr_o = calloc(2 * TC)
    ymax_o = calloc(TC)
    hmax_o = calloc(NI * NTT)
    sc_o = calloc(16)
    tmpa_o = calloc(D)
    ytile_o = calloc(D)
    swdb2_o = ytile_o
    row_o = calloc(48)
    nmagic_o = calloc(1)
    eps_o = calloc(1)
    lnv_o = calloc(TC)
    lnm2_o = calloc(TC)
    swdrow_o = calloc(128)
    rec_o = calloc(2 * QT)
    avtmp_o = calloc(QT)

    def cF(off, n):
        return C[:, off:off + n]

    def cB(off, nbf):
        return C.bitcast(BF16)[:, 2 * off:2 * off + nbf]

    identb = cB(identb_o, 128)
    identf = cF(identf_o, 128)
    onesf = cF(onesf_o, 128)

    SC_R127X, SC_SX, SC_R127H, SC_SH05 = 0, 1, 2, 3

    def scv(i):
        return cF(sc_o + i, 1)

    def r0(col, n=1):
        return C[0:1, row_o + col:row_o + col + n]

    WORK_SZ = 32900
    W = es.enter_context(nc.sbuf_tensor("w_arena", [128, WORK_SZ], F32)).ap()

    def wF(off, n):
        return W[:, off:off + n]

    def wB(off, nbf):
        return W.bitcast(BF16)[:, 2 * off:2 * off + nbf]

    XQT_O = 0
    STG_O = XQT_O + 4 * T
    WLD_O = STG_O + 2048
    WQT_O = WLD_O + 2048
    WQI_O = WQT_O + IH
    SILU_O = WQI_O + IH // 2
    HOUT_O = SILU_O + 1024
    F_END = HOUT_O + 1024
    assert F_END <= WORK_SZ
    HLD_O = 0
    HQ_O = HLD_O + 4096
    OUTB_O = HQ_O + 2 * NI * 256
    SSEL_O = OUTB_O + 1024
    RHS2_O = SSEL_O + 1024
    WDT_O = max(F_END, RHS2_O + 1024)
    assert WDT_O + NI * 512 <= WORK_SZ
    WDQ_O = WORK_SZ - 4 * IH
    assert WDQ_O >= F_END + IH

    Y2T_O = 0
    KX_O = Y2T_O
    VTOK_O = Y2T_O + 4 * T
    VTOK_SZ = (NH * TC * 65 + 1) // 2
    EXPS_O = VTOK_O + VTOK_SZ
    EXPS_SZ = TC * QT
    WBUF_O = EXPS_O + EXPS_SZ
    LATT_O = WBUF_O + 4096
    OUTT_O = LATT_O
    QSTR_O = LATT_O + max(T, 8 * QT)
    TBL_O = QSTR_O + 2048
    KMIX_O = TBL_O + 2 * (T // 2)
    MSK_O = KMIX_O + 1024
    SCM_O = MSK_O + QT
    A_END = SCM_O + 2 * QT
    assert A_END <= WORK_SZ, A_END

    ps_big = es.enter_context(nc.psum_tensor("ps_big", [128, 2048], F32)).ap()
    ps_ava = es.enter_context(nc.psum_tensor("ps_ava", [128, 512], F32)).ap()
    ps_avb = es.enter_context(nc.psum_tensor("ps_avb", [128, 512], F32)).ap()
    ps_tp0f = es.enter_context(nc.psum_tensor("ps_tp0", [128, 512], F32)).ap()
    ps_tp1f = es.enter_context(nc.psum_tensor("ps_tp1", [128, 512], F32)).ap()
    ps_tp0 = ps_tp0f.bitcast(BF16)
    ps_tp1 = ps_tp1f.bitcast(BF16)

    CCS = Sem(es.enter_context(nc.semaphore("CCS")))
    TE = Sem(es.enter_context(nc.semaphore("TE")))
    VE = Sem(es.enter_context(nc.semaphore("VE")))
    SE = Sem(es.enter_context(nc.semaphore("SE")))
    GE = Sem(es.enter_context(nc.semaphore("GE")))

    ctx = Ctx()
    o, wt, dr = ctx.op, ctx.wait, ctx.drain
    ev = {}
    QL = DmaQ(ctx, es, nc, "QL", "sync", 6)
    QS = DmaQ(ctx, es, nc, "QS", "scalar", 5)
    QG = DmaQ(ctx, es, nc, "QG", "gpsimd", 3)

    tk_x = QL.dma(out=X.rearrange("p (tc d) -> p tc d", tc=TC),
                  in_=x_in.rearrange("(tc p) d -> p tc d", p=128))
    tk_cx = QL.dma(out=r0(0), in_=clipx_d.unsqueeze(0))
    tk_cd = QL.dma(out=r0(1), in_=clipd_d.unsqueeze(0))

    o("gpsimd", "memset", identf, 1.0)
    o("gpsimd", "memset", onesf, 1.0)
    dr("gpsimd")
    o("gpsimd", "affine_select", out=identf, in_=identf, pattern=[[1, 128]],
      compare_op=ALU.is_equal, fill=0.0, base=0, channel_multiplier=-1)
    dr("gpsimd")
    o("gpsimd", "tensor_copy", out=identb, in_=identf)
    o("gpsimd", "memset", cF(nmagic_o, 1), -MAGIC)
    o("gpsimd", "memset", cF(eps_o, 1), EPS)
    dr("gpsimd")
    o("gpsimd", "engine_nop", inc=GE)
    g_consts = GE.n
    wt("scalar", GE, g_consts)
    wt("vector", GE, g_consts)
    ctx.wait_toks("scalar", [tk_cx, tk_cd])
    o("scalar", "activation", out=r0(2, 2), in_=r0(0, 2), func=AF.Sigmoid, inc=SE)
    ev["sig"] = SE.n

    def quant3(src, dst_bf, rec_pp, tmp):
        o("vector", "tensor_scalar", out=tmp, in0=src, scalar1=rec_pp,
          scalar2=QHI, op0=ALU.mult, op1=ALU.min)
        dr("vector")
        o("vector", "tensor_scalar", out=tmp, in0=tmp, scalar1=QLO,
          scalar2=MAGIC, op0=ALU.max, op1=ALU.add, inc=VE)
        wt("scalar", VE, VE.n)
        o("scalar", "activation", out=dst_bf, in_=tmp, func=AF.Identity,
          bias=cF(nmagic_o, 1), inc=SE)
        return SE.n

    def rowquant(wld, n, sw_slot, t0, t1):
        o("vector", "reduce_max", out=t0, in_=wld, axis=AX.X,
          apply_absolute_value=True)
        dr("vector")
        o("vector", "reciprocal", out=t1, in_=t0)
        dr("vector")
        o("vector", "tensor_scalar_mul", out=t1, in0=t1, scalar1=127.0)
        o("vector", "tensor_scalar_mul", out=sw_slot, in0=t0, scalar1=1.0 / 127.0)
        dr("vector")

    def prep_gu(w_dram, scr, sw_off, tag):
        for ic in range(NI):
            par = ic % 2
            wld = wF(WLD_O + par * D, D)
            if ic >= 2:
                wt("gpsimd", VE, ev[(tag, "wq", ic - 2)])
            elif tag == "u":
                wt("gpsimd", VE, ev[("g", "wq", NI - 2 + ic)])
            tk = QG.dma(out=wld, in_=w_dram[ic * 128:(ic + 1) * 128, :])
            ctx.wait_tok("vector", tk)
            rowquant(wld, D, cF(sw_off + ic, 1), wF(WQT_O, 1), wF(WQT_O + 1, 1))
            wq = wF(WQT_O + 2 + par * (D + 2), D)
            if ic >= 2:
                wt("vector", SE, ev[(tag, "wqi", ic - 2)])
            elif tag == "u":
                wt("vector", SE, ev[("g", "wqi", NI - 2 + ic)])
            o("vector", "tensor_scalar", out=wq, in0=wld, scalar1=wF(WQT_O + 1, 1),
              scalar2=QHI, op0=ALU.mult, op1=ALU.min, inc=VE)
            ev[(tag, "wq", ic)] = VE.n
            dr("vector")
            o("vector", "tensor_scalar", out=wq, in0=wq, scalar1=QLO,
              scalar2=MAGIC, op0=ALU.max, op1=ALU.add, inc=VE)
            wt("scalar", VE, VE.n)
            if ic >= 2:
                wt("scalar", TE, ev[(tag, "tp", ic - 2)])
            wqi = wB(WQI_O + par * 512, D)
            o("scalar", "activation", out=wqi, in_=wq, func=AF.Identity,
              bias=cF(nmagic_o, 1), inc=SE)
            ev[(tag, "wqi", ic)] = SE.n
            ps = ps_tp0 if par == 0 else ps_tp1
            wt("tensor", SE, SE.n)
            if ic < 2:
                wt("tensor", VE, VE.n)
            if ic >= 2:
                wt("tensor", SE, ev[(tag, "ev", ic - 2)])
            for hc in range(8):
                o("tensor", "transpose", out=ps[:, hc * 128:(hc + 1) * 128],
                  in_=wqi[:, hc * 128:(hc + 1) * 128], identity=identb,
                  inc=(TE if hc == 7 else None))
            ev[(tag, "tp", ic)] = TE.n
            stg = wB(STG_O + (1024 if tag == "u" else 0) + par * 512, D)
            wt("scalar", TE, TE.n)
            if ic >= 2:
                ctx.wait_tok("scalar", ev[(tag, "st", ic - 2)])
            o("scalar", "activation", out=stg, in_=ps[:], func=AF.Copy, inc=SE)
            ev[(tag, "ev", ic)] = SE.n
            wt("scalar", SE, SE.n)
            ev[(tag, "st", ic)] = QS.dma(out=scr[ic], in_=stg)

    def prep_down():
        wdqi = wB(WDQ_O, 8 * IH)
        for hc in range(8):
            wld = wF(WQT_O, IH) if hc % 2 == 0 else wF(F_END, IH)
            if hc >= 2:
                wt("gpsimd", VE, ev[("d", "wq", hc - 2)])
                wt("gpsimd", SE, ev[("d", "wqi", hc - 2)])
            else:
                wt("gpsimd", VE, ev[("u", "wq", NI - 1)])
                wt("gpsimd", SE, ev[("u", "wqi", NI - 1)])
            tk = QG.dma(out=wld, in_=wd_d[hc * 128:(hc + 1) * 128, :])
            ctx.wait_tok("vector", tk)
            rowquant(wld, IH, cF(swd_o + hc, 1), wF(SILU_O, 1), wF(SILU_O + 1, 1))
            o("vector", "tensor_scalar", out=wld, in0=wld, scalar1=wF(SILU_O + 1, 1),
              scalar2=QHI, op0=ALU.mult, op1=ALU.min, inc=VE)
            ev[("d", "wq", hc)] = VE.n
            dr("vector")
            o("vector", "tensor_scalar", out=wld, in0=wld, scalar1=QLO,
              scalar2=MAGIC, op0=ALU.max, op1=ALU.add, inc=VE)
            wt("scalar", VE, VE.n)
            o("scalar", "activation", out=wdqi[:, hc * IH:(hc + 1) * IH], in_=wld,
              func=AF.Identity, bias=cF(nmagic_o, 1), inc=SE)
            ev[("d", "wqi", hc)] = SE.n
        ev["wdq"] = SE.n
        for ic in range(NI):
            par = ic % 2
            ps = ps_tp0 if par == 0 else ps_tp1
            wt("tensor", SE, ev["wdq"])
            if ic < 2:
                wt("tensor", VE, VE.n)
            if ic >= 2:
                wt("tensor", SE, ev[("d", "ev", ic - 2)])
            for hc in range(8):
                o("tensor", "transpose", out=ps[:, hc * 128:(hc + 1) * 128],
                  in_=wdqi[:, hc * IH + ic * 128:hc * IH + (ic + 1) * 128],
                  identity=identb, inc=(TE if hc == 7 else None))
            ev[("d", "tp", ic)] = TE.n
            stg = wB(WLD_O + (ic % 2) * 512, D)
            wt("scalar", TE, TE.n)
            if ic >= 2:
                ctx.wait_tok("scalar", ev[("d", "st", ic - 2)])
            o("scalar", "activation", out=stg, in_=ps[:], func=AF.Copy, inc=SE)
            ev[("d", "ev", ic)] = SE.n
            wt("scalar", SE, SE.n)
            ev[("d", "st", ic)] = QS.dma(out=scr_d[ic], in_=stg)
        ev[("d", "done")] = [ev[("d", "st", i)] for i in range(NI)]
        ev["prepd_T"] = TE.n

    def layernorm(ph, absmax, ybf_consumer=None):
        ctx.wait_tok("vector", tk_x)
        ctx.wait_tok("scalar", tk_x)
        mcols = cF(mr_o, TC)
        rcols = cF(mr_o + TC, TC)
        vcols = cF(lnv_o, TC)
        for tc in range(TC):
            o("vector", "reduce_sum", out=cF(mr_o + tc, 1),
              in_=X[:, tc * D:(tc + 1) * D], axis=AX.X)
        dr("vector")
        o("vector", "tensor_scalar_mul", out=mcols, in0=mcols, scalar1=1.0 / D,
          inc=VE)
        for tc in range(TC):
            par = tc % 2
            sq = cF(tmpa_o, D) if par == 0 else cF(ytile_o, D)
            if tc >= 2:
                wt("scalar", VE, ev[(ph, "lnsq", tc - 2)])
            o("scalar", "activation", out=sq, in_=X[:, tc * D:(tc + 1) * D],
              func=AF.Square, inc=SE)
            wt("vector", SE, SE.n)
            o("vector", "reduce_sum", out=cF(lnv_o + tc, 1), in_=sq, axis=AX.X,
              inc=VE)
            ev[(ph, "lnsq", tc)] = VE.n
        dr("vector")
        m2c = cF(lnm2_o, TC)
        wt("scalar", VE, VE.n)
        o("scalar", "activation", out=m2c, in_=mcols, func=AF.Square, inc=SE)
        wt("vector", SE, SE.n)
        o("vector", "scalar_tensor_tensor", out=vcols, in0=vcols, scalar=1.0 / D,
          in1=m2c, op0=ALU.mult, op1=ALU.subtract)
        dr("vector")
        o("vector", "tensor_scalar_add", out=vcols, in0=vcols,
          scalar1=cF(eps_o, 1), inc=VE)
        wt("scalar", VE, VE.n)
        o("scalar", "activation", out=vcols, in_=vcols, func=AF.Sqrt, inc=SE)
        wt("vector", SE, SE.n)
        o("vector", "reciprocal", out=rcols, in_=vcols)
        dr("vector")
        if absmax:
            for tc in range(TC):
                yt = cF(ytile_o, D) if tc % 2 else cF(tmpa_o, D)
                o("vector", "tensor_scalar", out=yt,
                  in0=X[:, tc * D:(tc + 1) * D], scalar1=cF(mr_o + tc, 1),
                  scalar2=cF(mr_o + TC + tc, 1), op0=ALU.subtract, op1=ALU.mult)
                dr("vector")
                o("vector", "reduce_max", out=cF(ymax_o + tc, 1), in_=yt,
                  axis=AX.X, apply_absolute_value=True, inc=VE)
            dr("vector")
        if ybf_consumer is not None:
            for tc in range(TC):
                if tc >= 1:
                    wt("vector", TE, ev[(ph, "ytp", tc - 1)])
                yt = cF(ytile_o, D)
                o("vector", "tensor_scalar", out=yt,
                  in0=X[:, tc * D:(tc + 1) * D], scalar1=cF(mr_o + tc, 1),
                  scalar2=cF(mr_o + TC + tc, 1), op0=ALU.subtract, op1=ALU.mult)
                dr("vector")
                ybf = cB(ytile_o, D)
                o("vector", "tensor_copy", out=ybf, in_=yt, inc=VE)
                ybf_consumer(tc, VE.n)
        ev[(ph, "ln")] = VE.n

    def global_scale(idx, max_off, ncols, sig_col, r127_slot, s_slot, s_mult):
        mx = cF(sc_o + 10, 1)
        dr("vector")
        o("vector", "reduce_max", out=mx, in_=cF(max_off, ncols), axis=AX.X, inc=VE)
        wt("tensor", VE, VE.n)
        wt("tensor", VE, ev.get("avb_free", 0))
        o("tensor", "transpose", out=ps_ava[0:1, 0:128], in_=mx, identity=identf,
          inc=TE)
        wt("vector", TE, TE.n)
        o("vector", "reduce_max", out=r0(4), in_=ps_ava[0:1, 0:128], axis=AX.X,
          inc=VE)
        wt("gpsimd", VE, VE.n)
        tk1 = QG.dma(out=cc_in[idx], in_=r0(4))
        ctx.wait_tok("gpsimd", tk1)
        o("gpsimd", "collective_compute", "AllGather", ALU.bypass,
          replica_groups=[list(range(n_cores))],
          ins=[cc_in[idx]], outs=[cc_out[idx]], inc=CCS)
        wt("gpsimd", CCS, CCS.n)
        tk2 = QG.dma(out=r0(8, n_cores), in_=cc_out[idx])
        ctx.wait_tok("vector", tk2)
        o("vector", "reduce_max", out=r0(5), in_=r0(8, n_cores), axis=AX.X)
        dr("vector")
        wt("vector", SE, ev["sig"])
        o("vector", "tensor_mul", out=r0(6), in0=r0(5), in1=r0(sig_col))
        dr("vector")
        o("vector", "reciprocal", out=r0(7), in_=r0(6))
        dr("vector")
        o("vector", "tensor_scalar_mul", out=r0(16), in0=r0(7), scalar1=127.0)
        o("vector", "tensor_scalar_mul", out=r0(17), in0=r0(6),
          scalar1=float(s_mult / 127.0), inc=VE)
        wt("tensor", VE, VE.n)
        o("tensor", "matmul", out=ps_ava[:, 128:129], lhsT=onesf[0:1, :],
          rhs=r0(16), start=True, stop=True)
        o("tensor", "matmul", out=ps_ava[:, 129:130], lhsT=onesf[0:1, :],
          rhs=r0(17), start=True, stop=True, inc=TE)
        wt("vector", TE, TE.n)
        o("vector", "tensor_copy", out=scv(r127_slot), in_=ps_ava[:, 128:129])
        o("vector", "tensor_copy", out=scv(s_slot), in_=ps_ava[:, 129:130], inc=VE)
        dr("vector")
        ev["avb_free"] = VE.n
        return VE.n

    def ffn_pre(p, cc_base):
        ph = f"f{p}"
        layernorm(ph, absmax=True)
        v_sc = global_scale(cc_base, ymax_o, TC, 2, SC_R127X, SC_SX, 1.0)



    def ffn_mm(p, cc_base):
        ph = f"f{p}"
        o("vector", "tensor_tensor", out=cF(combg_o, NI), in0=cF(swg_o, NI),
          in1=scv(SC_SX).broadcast_to([128, NI]), op=ALU.mult)
        o("vector", "tensor_tensor", out=cF(combu_o, NI), in0=cF(swu_o, NI),
          in1=scv(SC_SX).broadcast_to([128, NI]), op=ALU.mult, inc=VE)
        dr("vector")
        ev[(ph, "vcomb")] = VE.n
        xqT = wB(XQT_O, 8 * T)
        prev_sQ = 0
        for tc in range(TC):
            xt = X[:, tc * D:(tc + 1) * D]
            qt = cF(tmpa_o, D)
            if tc == 0:
                wt("vector", TE, TE.n)
            if tc >= 1:
                wt("vector", TE, ev[(ph, "ytp", tc - 1)])
                wt("vector", SE, prev_sQ)
            o("vector", "tensor_scalar", out=qt, in0=xt,
              scalar1=cF(mr_o + tc, 1), scalar2=cF(mr_o + TC + tc, 1),
              op0=ALU.subtract, op1=ALU.mult)
            dr("vector")
            sQ = quant3(qt, cB(ytile_o, D), scv(SC_R127X), qt)
            prev_sQ = sQ
            par = tc % 2
            ps = ps_tp0 if par == 0 else ps_tp1
            wt("tensor", SE, sQ)
            if tc == 0:
                wt("tensor", VE, VE.n)
            if tc >= 2:
                wt("tensor", VE, ev[(ph, "xqev", tc - 2)])
            for hc in range(8):
                o("tensor", "transpose", out=ps[:, hc * 128:(hc + 1) * 128],
                  in_=cB(ytile_o, D)[:, hc * 128:(hc + 1) * 128], identity=identb,
                  inc=(TE if hc == 7 else None))
            ev[(ph, "ytp", tc)] = TE.n
            wt("vector", TE, TE.n)
            out_ap = AP(tensor=W.bitcast(BF16).tensor,
                        offset=2 * XQT_O + tc * 128,
                        ap=[[2 * WORK_SZ, 128], [T, 8], [1, 128]])
            o("vector", "tensor_copy", out=out_ap,
              in_=ps[:].rearrange("p (h j) -> p h j", h=8), inc=VE)
            ev[(ph, "xqev", tc)] = VE.n
        ev[(ph, "xq")] = VE.n
        xqT = wB(XQT_O, 8 * T)

        def xq_ap(hc, t0, tw):
            return xqT.rearrange("p (h t) -> p h t", h=8)[:, hc, t0:t0 + tw]

        for ic in range(NI):
            par = ic % 2
            stg_g = wB(STG_O + par * 512, D)
            stg_u = wB(STG_O + 1024 + par * 512, D)
            ctx.wait_tok("sync", ev[("g", "st", ic)])
            ctx.wait_tok("sync", ev[("u", "st", ic)])
            if p == 1 and ic < 2:
                ctx.wait_tok("sync", ev[("g", "st", NI - 2 + ic)])
                ctx.wait_tok("sync", ev[("u", "st", NI - 2 + ic)])
            if ic >= 2:
                wt("sync", TE, ev[(ph, "gumm", ic - 2)])
            if p == 2 and ic < 2:
                wt("sync", TE, ev[("f1", "guload_ok", ic)])
            tk_sg = QL.dma(out=stg_g, in_=scr_g[ic])
            tk_su = QL.dma(out=stg_u, in_=scr_u[ic])
            ctx.wait_toks("tensor", [tk_sg, tk_su])
            for tt in range(NTT):
                u = ic * NTT + tt
                gslot = ps_big[:, (u % 2) * 512:(u % 2) * 512 + TTW]
                uslot = ps_big[:, 1024 + (u % 2) * 512:1024 + (u % 2) * 512 + TTW]
                if ic == 0:
                    wt("tensor", VE,
                       ev[(ph, "xqev", min(TC - 1, (tt + 1) * NSUB - 1))])
                if u >= 2:
                    pic, ptt = divmod(u - 2, NTT)
                    wt("tensor", VE, ev[(ph, "hout", pic, ptt)])
                for hc in range(8):
                    o("tensor", "matmul", out=gslot,
                      lhsT=stg_g[:, hc * 128:(hc + 1) * 128],
                      rhs=xq_ap(hc, tt * TTW, TTW), start=(hc == 0), stop=(hc == 7))
                for hc in range(8):
                    o("tensor", "matmul", out=uslot,
                      lhsT=stg_u[:, hc * 128:(hc + 1) * 128],
                      rhs=xq_ap(hc, tt * TTW, TTW), start=(hc == 0), stop=(hc == 7),
                      inc=(TE if hc == 7 else None))
                ev[(ph, "gumm", ic)] = TE.n
                par2 = u % 2
                silu = wF(SILU_O + par2 * 512, TTW)
                hout = wF(HOUT_O + par2 * 512, TTW)
                wt("scalar", TE, TE.n)
                wt("scalar", VE, ev[(ph, "vcomb")])
                if u >= 2:
                    pic, ptt = divmod(u - 2, NTT)
                    wt("scalar", VE, ev[(ph, "hout", pic, ptt)])
                o("scalar", "activation", out=silu, in_=gslot, func=AF.Silu,
                  scale=cF(combg_o + ic, 1), inc=SE)
                wt("vector", SE, SE.n)
                if u >= 2:
                    pic, ptt = divmod(u - 2, NTT)
                    ctx.wait_tok("vector", ev[(ph, "hdma", pic, ptt)])
                o("vector", "scalar_tensor_tensor", out=hout, in0=uslot,
                  scalar=cF(combu_o + ic, 1), in1=silu, op0=ALU.mult,
                  op1=ALU.mult, inc=VE)
                ev[(ph, "hout", ic, tt)] = VE.n
                dr("vector")
                o("vector", "reduce_max", out=cF(hmax_o + u, 1), in_=hout,
                  axis=AX.X, apply_absolute_value=True, inc=VE)
                wt("scalar", VE, VE.n)
                if p == 2 and u == 0:
                    ctx.wait_toks("scalar", QL.snapshot())
                ev[(ph, "hdma", ic, tt)] = QS.dma(
                    out=scr_h[ic, :, tt * TTW:(tt + 1) * TTW], in_=hout)
        ev[(ph, "h_all")] = QS.snapshot()
        if p == 1:
            ev[("f1", "guload_ok", 0)] = TE.n
            ev[("f1", "guload_ok", 1)] = TE.n

    def ffn_down(p, cc_base):
        ph = f"f{p}"
        ctx.wait_toks("sync", ev[("d", "done")])
        wt("sync", TE, ev["prepd_T"])
        wdt_toks = []
        for ic in range(NI):
            wdt_toks.append(QL.dma(out=wB(WDT_O + ic * 512, D), in_=scr_d[ic]))

        v_sc2 = global_scale(cc_base + 1, hmax_o, NI * NTT, 3, SC_R127H,
                             SC_SH05, 0.5)

        Ssel = wF(SSEL_O, 1024)
        ctx.wait_toks("gpsimd", ev[(ph, "h_all")])
        wt("gpsimd", VE, VE.n)
        o("gpsimd", "memset", Ssel, 1.0)
        dr("gpsimd")
        o("gpsimd", "affine_select", out=Ssel, in_=Ssel, pattern=[[1, 1024]],
          compare_op=ALU.is_ge, fill=0.0, base=0, channel_multiplier=-128)
        dr("gpsimd")
        o("gpsimd", "affine_select", out=Ssel, in_=Ssel, pattern=[[-1, 1024]],
          compare_op=ALU.is_ge, fill=0.0, base=127, channel_multiplier=128)
        dr("gpsimd")
        o("gpsimd", "engine_nop", inc=GE)
        g_sel = GE.n
        wt("tensor", VE, v_sc2)
        o("tensor", "transpose", out=ps_avb[0:8, 0:128], in_=cF(swd_o, 8),
          identity=identf, inc=TE)
        wt("vector", TE, TE.n)
        o("vector", "tensor_copy", out=cF(swdrow_o, 128)[0:8, :],
          in_=ps_avb[0:8, 0:128], inc=VE)
        dr("vector")
        rhs2 = wF(RHS2_O, 1024)
        wt("vector", GE, g_sel)
        o("vector", "tensor_tensor", out=rhs2[0:8, :].rearrange(
            "p (r j) -> p r j", r=8), in0=Ssel[0:8, :].rearrange(
            "p (r j) -> p r j", r=8),
          in1=cF(swdrow_o, 128)[0:8, :].unsqueeze(1).broadcast_to([8, 8, 128]),
          op=ALU.mult, inc=VE)
        dr("vector")
        wt("tensor", VE, VE.n)
        o("tensor", "matmul", out=ps_ava[:, 0:512],
          lhsT=onesf[0:8, :], rhs=rhs2[0:8, 0:512],
          start=True, stop=True)
        o("tensor", "matmul", out=ps_avb[:, 0:512],
          lhsT=onesf[0:8, :], rhs=rhs2[0:8, 512:1024],
          start=True, stop=True, inc=TE)
        t_swdb2 = TE.n
        wt("vector", TE, t_swdb2)
        o("vector", "tensor_scalar_mul", out=cF(swdb2_o, 512),
          in0=ps_ava[:, 0:512], scalar1=scv(SC_SH05))
        o("vector", "tensor_scalar_mul", out=cF(swdb2_o + 512, 512),
          in0=ps_avb[:, 0:512], scalar1=scv(SC_SH05), inc=VE)
        dr("vector")
        v_swdb2 = VE.n
        ev["avb_free"] = VE.n

        ngrp = (NI + 3) // 4
        ctx.wait_toks("scalar", ev[(ph, "h_all")])

        def emit_loadquant_group(tt, g):
            i0, i1 = g * 4, min(NI, g * 4 + 4)
            gu = tt * ngrp + g
            par = gu % 2
            hld = wF(HLD_O + par * 2048, (i1 - i0) * TTW)
            if gu >= 2:
                pt_, pg_ = divmod(gu - 2, ngrp)
                wt("sync", SE, ev[(ph, "hq", pt_, pg_)])
            ctx.wait_toks("sync", ev[(ph, "h_all")])
            if tt == 0 and g == 0:
                wt("sync", TE, ev[(ph, "gumm", NI - 1)])
                wt("sync", VE, VE.n)
                wt("sync", SE, SE.n)
                if p == 1:
                    wt("sync", VE, ev[("d", "wq", 7)])
                    wt("sync", SE, ev[("d", "wqi", 7)])
            tk_h = QL.dma(
                out=hld.rearrange("p (i t) -> p i t", i=i1 - i0),
                in_=scr_h[i0:i1, :, tt * TTW:(tt + 1) * TTW].transpose([1, 0, 2]))
            ctx.wait_tok("vector", tk_h)
            if g == 0:
                wt("vector", VE, v_sc2)
                if tt >= 2:
                    wt("scalar", TE, ev[(ph, "dmm", tt - 2)])
                    wt("vector", TE, ev[(ph, "dmm", tt - 2)])
            for ii in range(i0, i1):
                src_ = wF(HLD_O + par * 2048 + (ii - i0) * TTW, TTW)
                if (tt, g, ii) != (0, 0, 0):
                    wt("vector", SE, SE.n)
                quant3(src_, wB(HQ_O + (tt % 2) * NI * 256 + ii * 256, TTW),
                       scv(SC_R127H), cF(tmpa_o, TTW))
            ev[(ph, "hq", tt, g)] = SE.n

        def emit_loadquant(tt):
            for g in range(ngrp):
                emit_loadquant_group(tt, g)
            ev[(ph, "hqtt", tt)] = SE.n

        def emit_mms(tt):
            hq0 = HQ_O + (tt % 2) * NI * 256
            for tsub in range(NSUB):
                tc = tt * NSUB + tsub
                for hh in range(2):
                    k = tc * 2 + hh
                    slot = ps_big[:, (k % 3) * 512:(k % 3) * 512 + 512]
                    if k >= 3:
                        wt("tensor", VE, ev[(ph, "dres", (k - 3) // 2, (k - 3) % 2)])
                    wt("tensor", SE, ev[(ph, "hqtt", tt)])
                    if tt == 0 and k == 0:
                        ctx.wait_toks("tensor", wdt_toks)
                    for ii in range(NI):
                        o("tensor", "matmul", out=slot,
                          lhsT=wB(hq0 + ii * 256, TTW)[:, tsub * 128:(tsub + 1) * 128],
                          rhs=wB(WDT_O + ii * 512, D)[:, hh * 512:(hh + 1) * 512],
                          start=(ii == 0), stop=(ii == NI - 1),
                          inc=(TE if ii == NI - 1 else None))
                    ev[(ph, "dmm", tt)] = TE.n
                    par3 = k % 2
                    ob = wF(OUTB_O + par3 * 512, 512)
                    wt("vector", TE, TE.n)
                    wt("vector", VE, v_swdb2)
                    if p == 2 and k >= 2:
                        ctx.wait_tok("vector", ev[(ph, "odma", k - 2)])
                    o("vector", "tensor_tensor", out=ob, in0=slot,
                      in1=cF(swdb2_o + hh * 512, 512), op=ALU.mult)
                    dr("vector")
                    xsl = X[:, tc * D + hh * 512:tc * D + (hh + 1) * 512]
                    o("vector", "tensor_add", out=xsl, in0=xsl, in1=ob, inc=VE)
                    ev[(ph, "dres", tc, hh)] = VE.n
                    dr("vector")
                    if p == 2:
                        wt("scalar", VE, VE.n)
                        ev[(ph, "odma", k)] = QS.dma(
                            out=out_d[tc * 128:(tc + 1) * 128,
                                      hh * 512:(hh + 1) * 512],
                            in_=xsl)

        def emit_unit(tt, tsub, hh):
            hq0 = HQ_O + (tt % 2) * NI * 256
            tc = tt * NSUB + tsub
            k = tc * 2 + hh
            slot = ps_big[:, (k % 3) * 512:(k % 3) * 512 + 512]
            if k >= 3:
                wt("tensor", VE, ev[(ph, "dres", (k - 3) // 2, (k - 3) % 2)])
            wt("tensor", SE, ev[(ph, "hqtt", tt)])
            if tt == 0 and k == 0:
                ctx.wait_toks("tensor", wdt_toks)
            for ii in range(NI):
                o("tensor", "matmul", out=slot,
                  lhsT=wB(hq0 + ii * 256, TTW)[:, tsub * 128:(tsub + 1) * 128],
                  rhs=wB(WDT_O + ii * 512, D)[:, hh * 512:(hh + 1) * 512],
                  start=(ii == 0), stop=(ii == NI - 1),
                  inc=(TE if ii == NI - 1 else None))
            ev[(ph, "dmm", tt)] = TE.n
            par3 = k % 2
            ob = wF(OUTB_O + par3 * 512, 512)
            wt("vector", TE, TE.n)
            wt("vector", VE, v_swdb2)
            if p == 2 and k >= 2:
                ctx.wait_tok("vector", ev[(ph, "odma", k - 2)])
            o("vector", "tensor_tensor", out=ob, in0=slot,
              in1=cF(swdb2_o + hh * 512, 512), op=ALU.mult)
            dr("vector")
            xsl = X[:, tc * D + hh * 512:tc * D + (hh + 1) * 512]
            o("vector", "tensor_add", out=xsl, in0=xsl, in1=ob, inc=VE)
            ev[(ph, "dres", tc, hh)] = VE.n
            dr("vector")
            if p == 2:
                wt("scalar", VE, VE.n)
                ev[(ph, "odma", k)] = QS.dma(
                    out=out_d[tc * 128:(tc + 1) * 128,
                              hh * 512:(hh + 1) * 512],
                    in_=xsl)

        units = [(tsub, hh) for tsub in range(NSUB) for hh in range(2)]
        emit_loadquant(0)
        for tt in range(NTT):
            if tt + 1 < NTT:
                nu = len(units)
                for gi in range(ngrp):
                    emit_loadquant_group(tt + 1, gi)
                    for (tsub, hh) in units[gi * nu // ngrp:(gi + 1) * nu // ngrp]:
                        emit_unit(tt, tsub, hh)
                ev[(ph, "hqtt", tt + 1)] = SE.n
            else:
                for (tsub, hh) in units:
                    emit_unit(tt, tsub, hh)
        ev[(ph, "down_v")] = VE.n
        ev[(ph, "down_T")] = TE.n
        if p == 2:
            ev["outdma"] = QS.snapshot()

    def attn_block():
        ph = "a"
        y2T = wB(Y2T_O, 8 * T)
        kX = wB(KX_O, 8 * T)
        EQA = wB(TBL_O, T)
        EQB = wB(TBL_O + T // 2, T)
        I16 = W.bitcast(mybir.dt.int16)
        MB0 = I16[:, 2 * MSK_O:2 * MSK_O + QT]
        MB1 = I16[:, 2 * MSK_O + QT:2 * MSK_O + 2 * QT]
        kmix = wB(KMIX_O, 8 * 256)
        scm = wB(SCM_O, 4 * QT)
        wbuf = wB(WBUF_O, 8 * D)
        exps = wB(EXPS_O, 2 * TC * QT)

        dtmp = wF(WBUF_O, T)
        wt("gpsimd", TE, ev[("f1", "down_T")])
        wt("gpsimd", VE, ev[("f1", "down_v")])
        o("gpsimd", "iota", dtmp, pattern=[[1, T]], base=0,
          channel_multiplier=0, allow_small_or_imprecise_dtypes=True)
        o("gpsimd", "memset", MB0, 1)
        o("gpsimd", "memset", MB1, 1)
        dr("gpsimd")
        o("gpsimd", "affine_select", out=MB0, in_=MB0, pattern=[[1, QT]],
          compare_op=ALU.is_ge, fill=0.0, base=-1, channel_multiplier=-1)
        o("gpsimd", "affine_select", out=MB1, in_=MB1, pattern=[[1, QT]],
          compare_op=ALU.is_ge, fill=0.0, base=-129, channel_multiplier=-1)
        dr("gpsimd")
        o("gpsimd", "engine_nop", inc=GE)
        wt("scalar", GE, GE.n)
        o("scalar", "activation", out=EQA, in_=dtmp, func=AF.Exp, scale=LAM)
        o("scalar", "activation", out=EQB, in_=dtmp, func=AF.Exp, scale=-LAM,
          inc=SE)
        ev["tbl"] = SE.n

        def y2_consumer(tc, vtag):
            par = tc % 2
            ps = ps_tp0 if par == 0 else ps_tp1
            wt("tensor", VE, vtag)
            if tc == 0:
                wt("tensor", VE, VE.n)
            if tc >= 2:
                wt("tensor", VE, ev[(ph, "y2ev", tc - 2)])
            for hc in range(8):
                o("tensor", "transpose", out=ps[:, hc * 128:(hc + 1) * 128],
                  in_=cB(ytile_o, D)[:, hc * 128:(hc + 1) * 128], identity=identb,
                  inc=(TE if hc == 7 else None))
            ev[(ph, "ytp", tc)] = TE.n
            wt("vector", TE, TE.n)
            out_ap = AP(tensor=W.bitcast(BF16).tensor,
                        offset=2 * Y2T_O + tc * 128,
                        ap=[[2 * WORK_SZ, 128], [T, 8], [1, 128]])
            o("vector", "tensor_copy", out=out_ap,
              in_=ps[:].rearrange("p (h j) -> p h j", h=8), inc=VE)
            ev[(ph, "y2ev", tc)] = VE.n

        layernorm(ph, absmax=False, ybf_consumer=y2_consumer)
        ev["y2T"] = VE.n

        def y2_ap(kc, t0, tw):
            return y2T.rearrange("p (k t) -> p k t", k=8)[:, kc, t0:t0 + tw]

        wt("gpsimd", VE, ev["y2T"])
        wt("gpsimd", SE, ev["tbl"])
        tk_wq = QG.dma(out=wbuf.rearrange("p (k d) -> p k d", k=8),
                       in_=Wq_d.rearrange("(k p) d -> p k d", p=128))
        ctx.wait_tok("tensor", tk_wq)
        wt("tensor", VE, ev["y2T"])
        QS0 = KMIX_O
        for dc in range(8):
            for tt in range(NTT):
                u = dc * NTT + tt
                slot = ps_big[:, (u % 4) * 512:(u % 4) * 512 + TTW]
                if u >= 4:
                    wt("tensor", SE, ev[(ph, "qev", u - 4)])
                for kc in range(8):
                    o("tensor", "matmul", out=slot,
                      lhsT=wbuf[:, kc * D + dc * 128:kc * D + (dc + 1) * 128],
                      rhs=y2_ap(kc, tt * TTW, TTW), start=(kc == 0), stop=(kc == 7),
                      inc=(TE if kc == 7 else None))
                wt("scalar", TE, TE.n)
                if u >= 2:
                    ctx.wait_tok("scalar", ev[(ph, "qdma", u - 2)])
                qstg = wB(QS0 + (u % 2) * 256, TTW)
                o("scalar", "activation", out=qstg, in_=slot, func=AF.Copy,
                  scale=SCALE_Q, inc=SE)
                ev[(ph, "qev", u)] = SE.n
                wt("scalar", SE, SE.n)
                ev[(ph, "qdma", u)] = QS.dma(
                    out=scr_q[dc, :, tt * TTW:(tt + 1) * TTW], in_=qstg)
        ev["qdma"] = QS.snapshot()
        ev["q_T"] = TE.n

        wt("gpsimd", TE, ev["q_T"])
        tk_wk = QG.dma(out=wbuf.rearrange("p (k d) -> p k d", k=8),
                       in_=Wk_d.rearrange("(k p) d -> p k d", p=128))
        ctx.wait_tok("vector", tk_wk)
        wvc = wB(QS0 + 512, 1024)
        ctx.wait_toks("gpsimd", ev["qdma"])
        last_tt = 0
        for kc8 in range(8):
            if kc8 > 0:
                wt("gpsimd", VE, last_tt)
            tk_wv = QG.dma(out=wvc, in_=Wv_d[kc8 * 128:(kc8 + 1) * 128, :])
            ctx.wait_tok("vector", tk_wv)
            o("vector", "tensor_add", out=wbuf[:, kc8 * D:(kc8 + 1) * D],
              in0=wbuf[:, kc8 * D:(kc8 + 1) * D], in1=wvc, inc=VE)
            dr("vector")
            last_tt = VE.n
        v_wsum = VE.n
        wdkv = wB(VTOK_O, 2048)
        wt("gpsimd", SE, ev[(ph, "qev", 8 * NTT - 1)])
        tk_wdkv = QG.dma(out=wdkv.rearrange("p (k l) -> p k l", k=8),
                         in_=Wdkv_d.rearrange("(k p) l -> p k l", p=128))

        latT = wB(LATT_O, 2 * T)
        kvbf = wB(QSTR_O, 8 * 512)
        for tt in range(NTT):
            for dc in range(8):
                slot = ps_big[:, (dc % 2) * 512:(dc % 2) * 512 + TTW]
                u = tt * 8 + dc
                if u >= 2:
                    pu = u - 2
                    wt("tensor", VE, ev[(ph, "kvev", pu // 8, pu % 8)])
                if u == 0:
                    wt("tensor", VE, v_wsum)
                for kc in range(8):
                    o("tensor", "matmul", out=slot,
                      lhsT=wbuf[:, kc * D + dc * 128:kc * D + (dc + 1) * 128],
                      rhs=y2_ap(kc, tt * TTW, TTW), start=(kc == 0), stop=(kc == 7),
                      inc=(TE if kc == 7 else None))
                wt("vector", TE, TE.n)
                if tt > 0 and dc == 0:
                    wt("vector", TE, ev[(ph, "latmm", tt - 1)])
                o("vector", "tensor_copy", out=kvbf[:, dc * TTW:(dc + 1) * TTW],
                  in_=slot, inc=VE)
                ev[(ph, "kvev", tt, dc)] = VE.n
            wt("tensor", VE, ev[(ph, "kvev", tt, 7)])
            if tt == 0:
                ctx.wait_tok("tensor", tk_wdkv)
            for lc in range(2):
                slot = ps_big[:, 1024 + lc * 512:1024 + lc * 512 + TTW]
                for kc in range(8):
                    o("tensor", "matmul", out=slot,
                      lhsT=wdkv[:, kc * 256 + lc * 128:kc * 256 + (lc + 1) * 128],
                      rhs=kvbf[:, kc * TTW:(kc + 1) * TTW], start=(kc == 0),
                      stop=(kc == 7), inc=(TE if kc == 7 else None))
                ev[(ph, "latmm", tt)] = TE.n
                wt("vector", TE, TE.n)
                o("vector", "tensor_copy",
                  out=latT.rearrange("p (l t) -> p l t", l=2)[
                      :, lc, tt * TTW:(tt + 1) * TTW], in_=slot, inc=VE)
            ev[(ph, "latev", tt)] = VE.n
        ev["lat"] = VE.n

        def lat_ap(lc, t0, tw):
            return latT.rearrange("p (l t) -> p l t", l=2)[:, lc, t0:t0 + tw]

        wupk = wB(WBUF_O, 2 * D)
        wt("gpsimd", VE, ev["lat"])
        tk_upk = QG.dma(out=wupk.rearrange("p (l d) -> p l d", l=2),
                        in_=Wupk_d.rearrange("(l p) d -> p l d", p=128))
        ctx.wait_tok("tensor", tk_upk)
        wt("tensor", VE, ev["lat"])
        for dc in range(8):
            for tt in range(NTT):
                u = dc * NTT + tt
                slot = ps_big[:, (u % 4) * 512:(u % 4) * 512 + TTW]
                if u >= 4:
                    wt("tensor", VE, ev[(ph, "krev", u - 4)])
                for lc in range(2):
                    o("tensor", "matmul", out=slot,
                      lhsT=wupk[:, lc * D + dc * 128:lc * D + (dc + 1) * 128],
                      rhs=lat_ap(lc, tt * TTW, TTW), start=(lc == 0), stop=(lc == 1),
                      inc=(TE if lc == 1 else None))
                wt("vector", TE, TE.n)
                if u == 0:
                    wt("vector", SE, ev["tbl"])
                o("vector", "tensor_tensor",
                  out=kX.rearrange("p (d t) -> p d t", d=8)[
                      :, dc, tt * TTW:(tt + 1) * TTW],
                  in0=slot, in1=EQB[:, tt * TTW:(tt + 1) * TTW],
                  op=ALU.mult, inc=VE)
                ev[(ph, "krev", u)] = VE.n
        ev["krec"] = VE.n
        ev["krec_T"] = TE.n

        wupv = wB(WBUF_O, 2 * D)
        wt("gpsimd", TE, ev["krec_T"])
        tk_upv = QG.dma(out=wupv.rearrange("p (l d) -> p l d", l=2),
                        in_=Wupv_d.rearrange("(l p) d -> p l d", p=128))
        vtok = wB(VTOK_O, NH * TC * 65)
        wt("vector", TE, ev[(ph, "latmm", NTT - 1)])
        o("vector", "memset",
          vtok.rearrange("p (n c) -> p n c", c=65)[:, :, 64:65], 1.0, inc=VE)
        dr("vector")
        v_ones = VE.n
        ctx.wait_tok("tensor", tk_upv)
        wt("tensor", VE, v_ones)
        for tc in range(TC):
            for dh in range(2):
                u = tc * 2 + dh
                slot = ps_big[:, (u % 4) * 512:(u % 4) * 512 + 512]
                if u >= 4:
                    wt("tensor", VE, ev[(ph, "vtev", u - 4)])
                for lc in range(2):
                    o("tensor", "matmul", out=slot,
                      lhsT=lat_ap(lc, tc * 128, 128),
                      rhs=wupv[:, lc * D + dh * 512:lc * D + (dh + 1) * 512],
                      start=(lc == 0), stop=(lc == 1),
                      inc=(TE if lc == 1 else None))
                wt("vector", TE, TE.n)
                out_ap = AP(tensor=W.bitcast(BF16).tensor,
                            offset=2 * VTOK_O + (dh * 8 * TC + tc) * 65,
                            ap=[[2 * WORK_SZ, 128], [65 * TC, 8], [1, 64]])
                o("vector", "tensor_copy", out=out_ap,
                  in_=slot.rearrange("p (h e) -> p h e", h=8), inc=VE)
                ev[(ph, "vtev", u)] = VE.n
        ev["vtok"] = VE.n

        wo = wB(WBUF_O, 8 * D)
        wt("gpsimd", TE, TE.n)
        tk_wo = QG.dma(out=wo.rearrange("p (k d) -> p k d", k=8),
                       in_=Wo_d.rearrange("(k p) d -> p k d", p=128))

        qA = wB(QSTR_O, 8 * QT)
        qB = wB(QSTR_O + 1024, 8 * QT)
        exp_hist = []
        lkx = kX.rearrange("p (d t) -> p d t", d=8)
        NG2 = TC // 2

        def eqslice(tbl, qt):
            return AP(tensor=W.bitcast(BF16).tensor,
                      offset=2 * TBL_O + (0 if tbl is EQA else
                                          (T if tbl is EQB else 2 * T)) + qt * QT,
                      ap=[[2 * WORK_SZ, 128], [0, 8], [1, QT]])

        def a3_emit(qt):
            wt("tensor", VE, ev[(ph, "avTT", qt, NH // 2 - 1)])
            if qt == 0:
                ctx.wait_tok("tensor", tk_wo)
            outT = wB(OUTT_O + (qt % 2) * 1024, 8 * QT)
            loT = outT.rearrange("p (c t) -> p c t", c=8)
            for tsub in range(QT // 128):
                tc = (qt * QT) // 128 + tsub
                for dh in range(2):
                    u = qt * (QT // 128) * 2 + tsub * 2 + dh
                    slot = (ps_ava if u % 2 == 0 else ps_avb)[:, 0:512]
                    if u >= 2:
                        wt("tensor", VE, ev[(ph, "xres", u - 2)])
                    for kc in range(8):
                        o("tensor", "matmul", out=slot,
                          lhsT=loT[:, kc, tsub * 128:(tsub + 1) * 128],
                          rhs=wo[:, kc * D + dh * 512:kc * D + (dh + 1) * 512],
                          start=(kc == 0), stop=(kc == 7),
                          inc=(TE if kc == 7 else None))
                    wt("vector", TE, TE.n)
                    xsl = X[:, tc * D + dh * 512:tc * D + (dh + 1) * 512]
                    o("vector", "tensor_add", out=xsl, in0=xsl, in1=slot, inc=VE)
                    ev[(ph, "xres", u)] = VE.n
                    dr("vector")

        def do_av_mm(h, qt):
            hp, hi = h // 2, h % 2
            avps = ps_ava if hi == 0 else ps_avb
            avslot = avps[0:65, 0:QT]
            if hi == 0 and (qt, hp) > (0, 0):
                ph_, hp_ = (qt, hp - 1) if hp >= 1 else (qt - 1, NH // 2 - 1)
                wt("tensor", VE, ev[(ph, "avTT", ph_, hp_)])
            if hi == 0 and hp == 0 and qt >= 1:
                wt("tensor", VE,
                   ev[(ph, "xres", qt * (QT // 128) * 2 - 1)])
            estrip = hi * TC * QT
            for g in range(NG2):
                wt("tensor", SE, ev[(ph, "exp", qt, hp, g)])
                if g == qt:
                    wt("tensor", VE, ev[(ph, "sel", qt, hp)])
                for j in range(2):
                    kc = g * 2 + j
                    o("tensor", "matmul", out=avslot,
                      lhsT=vtok[:, (h * TC + kc) * 65:(h * TC + kc) * 65 + 65],
                      rhs=exps[:, estrip + kc * QT:estrip + (kc + 1) * QT],
                      start=(kc == 0), stop=(kc == TC - 1),
                      inc=(TE if kc == TC - 1 else None))
            ev[(ph, "avmm", qt, h)] = TE.n

        def do_pair_norm(hp, qt):
            den0 = C[64:65, rec_o:rec_o + QT]
            den1 = C[64:65, rec_o + QT:rec_o + 2 * QT]
            wt("vector", TE, ev[(ph, "avmm", qt, 2 * hp)])
            o("vector", "tensor_copy", out=den0, in_=ps_ava[64:65, 0:QT],
              inc=VE)
            v_cp0 = VE.n
            wt("vector", TE, ev[(ph, "avmm", qt, 2 * hp + 1)])
            o("vector", "tensor_copy", out=den1, in_=ps_avb[64:65, 0:QT],
              inc=VE)
            v_cp1 = VE.n
            wt("tensor", VE, v_cp0)
            o("tensor", "matmul", out=ps_ava[0:64, 256:256 + QT],
              lhsT=onesf[64:65, 0:64], rhs=den0, start=True, stop=True)
            wt("tensor", VE, v_cp1)
            o("tensor", "matmul", out=ps_ava[64:128, 256:256 + QT],
              lhsT=onesf[64:65, 0:64], rhs=den1, start=True, stop=True, inc=TE)
            t_bc = TE.n
            rec128 = C[:, avtmp_o:avtmp_o + QT]
            wt("vector", TE, t_bc)
            o("vector", "reciprocal", out=rec128, in_=ps_ava[:, 256:256 + QT],
              inc=VE)
            dr("vector")
            outT = wB(OUTT_O + (qt % 2) * 1024, 8 * QT)
            oT = outT.rearrange("p (c t) -> p c t", c=8)
            o("vector", "tensor_tensor", out=oT[0:64, hp, :],
              in0=ps_ava[0:64, 0:QT], in1=rec128[0:64, :], op=ALU.mult)
            o("vector", "tensor_tensor", out=oT[64:128, hp, :],
              in0=ps_avb[0:64, 0:QT], in1=rec128[64:128, :], op=ALU.mult,
              inc=VE)
            ev[(ph, "avTT", qt, hp)] = VE.n

        for qt in range(NQT):
            ctx.wait_toks("sync", ev["qdma"])
            if qt >= 1:
                wt("sync", TE, ev[(ph, "qused", qt - 1)])
            else:
                wt("sync", TE, ev[(ph, "latmm", NTT - 1)])
                wt("sync", VE, ev["lat"])
            tk_q = QL.dma(
                out=qA.rearrange("p (d t) -> p d t", d=8),
                in_=scr_q[:, :, qt * QT:(qt + 1) * QT].transpose([1, 0, 2]))
            ctx.wait_tok("vector", tk_q)
            if qt >= 1:
                wt("vector", TE, ev[(ph, "qused", qt - 1)])
            o("vector", "tensor_tensor",
              out=qB.rearrange("p (d t) -> p d t", d=8),
              in0=qA.rearrange("p (d t) -> p d t", d=8),
              in1=eqslice(EQB, qt), op=ALU.mult)
            dr("vector")
            o("vector", "tensor_tensor",
              out=qA.rearrange("p (d t) -> p d t", d=8),
              in0=qA.rearrange("p (d t) -> p d t", d=8),
              in1=eqslice(EQA, qt), op=ALU.mult, inc=VE)
            v_qk = VE.n
            if qt >= 1:
                wt("vector", VE, ev[(ph, "kcv", qt - 1)])
            eqa_s = AP(tensor=W.bitcast(BF16).tensor,
                       offset=2 * TBL_O + 2 * qt * 128,
                       ap=[[2 * WORK_SZ, 128], [0, 8], [1, 256]])
            o("vector", "tensor_tensor",
              out=kmix.rearrange("p (d t) -> p d t", d=8),
              in0=AP(tensor=W.bitcast(BF16).tensor, offset=2 * KX_O + 2 * qt * 128,
                     ap=[[2 * WORK_SZ, 128], [T, 8], [1, 256]]),
              in1=eqa_s, op=ALU.mult)
            dr("vector")
            o("vector", "tensor_tensor",
              out=kmix.rearrange("p (d t) -> p d t", d=8),
              in0=kmix.rearrange("p (d t) -> p d t", d=8),
              in1=eqa_s, op=ALU.mult, inc=VE)
            v_km = VE.n
            for hp in range(NH // 2):
                hc = hp
                for g in range(NG2):
                    u = hp * NG2 + g
                    sslot = ps_big[:, (u % 2) * 1024:(u % 2) * 1024 + 4 * QT]
                    if len(exp_hist) >= 2:
                        wt("tensor", SE, exp_hist[-2])
                    if hp == 0 and g == 0:
                        ctx.wait_tok("tensor", tk_q)
                        wt("tensor", VE, v_qk)
                        wt("tensor", VE, ev["krec"])
                        wt("tensor", VE, ev["vtok"])
                    for j in range(2):
                        kc = g * 2 + j
                        if kc >= 2 * qt + 2:
                            rhs_q = qA
                        elif kc <= 2 * qt - 1:
                            rhs_q = qB
                        else:
                            rhs_q = qA
                        rq = rhs_q.rearrange("p (d t) -> p d t", d=8)
                        for hi in range(2):
                            hr = hi * 64
                            o("tensor", "matmul",
                              out=sslot[:, hi * 512 + j * QT:hi * 512 + (j + 1) * QT],
                              lhsT=lkx[hr:hr + 64, hc, kc * 128:(kc + 1) * 128],
                              rhs=rq[hr:hr + 64, hc, :],
                              start=True, stop=True,
                              inc=(TE if (j == 1 and hi == 1) else None))
                    t_sc = TE.n
                    if g == qt:
                        if hp == 0:
                            wt("tensor", VE, v_km)
                        if hp >= 1:
                            wt("tensor", SE, ev[(ph, "expB", qt, hp - 1)])
                        elif qt >= 1:
                            wt("tensor", SE, ev[(ph, "expB", qt - 1, NH // 2 - 1)])
                        km = kmix.rearrange("p (d t) -> p d t", d=8)
                        qb = qB.rearrange("p (d t) -> p d t", d=8)
                        for j2 in range(2):
                            for hi in range(2):
                                hr = hi * 64
                                tp = ps_tp0f if hi == 0 else ps_tp1f
                                o("tensor", "matmul",
                                  out=tp[:, j2 * QT:(j2 + 1) * QT],
                                  lhsT=km[hr:hr + 64, hc, j2 * 128:(j2 + 1) * 128],
                                  rhs=qb[hr:hr + 64, hc, :],
                                  start=True, stop=True,
                                  inc=(TE if (j2 == 1 and hi == 1) else None))
                        t_b = TE.n
                        wt("scalar", TE, t_b)
                        if hp >= 1:
                            wt("scalar", VE, ev[(ph, "sel", qt, hp - 1)])
                        elif qt >= 1:
                            wt("scalar", VE, ev[(ph, "sel", qt - 1, NH // 2 - 1)])
                        o("scalar", "activation", out=scm[:, 0:2 * QT],
                          in_=ps_tp0f[:, 0:2 * QT], func=AF.Exp)
                        o("scalar", "activation", out=scm[:, 2 * QT:4 * QT],
                          in_=ps_tp1f[:, 0:2 * QT], func=AF.Exp, inc=SE)
                        ev[(ph, "expB", qt, hp)] = SE.n
                    wt("scalar", TE, t_sc)
                    if g == 0:
                        if hp >= 1:
                            wt("scalar", TE, ev[(ph, "avmm", qt, 2 * hp - 1)])
                        elif qt >= 1:
                            wt("scalar", TE, ev[(ph, "avmm", qt - 1, NH - 1)])
                    exp_out = AP(tensor=W.bitcast(BF16).tensor,
                                 offset=2 * EXPS_O + g * 2 * QT,
                                 ap=[[2 * WORK_SZ, 128], [TC * QT, 2], [1, 2 * QT]])
                    o("scalar", "activation", out=exp_out,
                      in_=sslot, func=AF.Exp, inc=SE)
                    ev[(ph, "exp", qt, hp, g)] = SE.n
                    exp_hist.append(SE.n)
                    if g == qt:
                        wt("scalar", VE, VE.n)
                        wt("vector", SE, SE.n)
                        for hi in range(2):
                            estrip = hi * TC * QT
                            for j2 in range(2):
                                kc = 2 * qt + j2
                                o("vector", "copy_predicated",
                                  out=exps[:, estrip + kc * QT:
                                           estrip + (kc + 1) * QT],
                                  mask=(MB0 if j2 == 0 else MB1),
                                  data=scm[:, (hi * 2 + j2) * QT:
                                           (hi * 2 + j2 + 1) * QT], inc=(
                                      VE if (hi == 1 and j2 == 1) else None))
                        ev[(ph, "sel", qt, hp)] = VE.n
                if hp == 0 and qt >= 1:
                    a3_emit(qt - 1)
                do_av_mm(2 * hp, qt)
                do_av_mm(2 * hp + 1, qt)
                do_pair_norm(hp, qt)
            if qt == NQT - 1:
                ev["kx_done"] = TE.n
            ev[(ph, "qused", qt)] = TE.n
            wt("vector", TE, ev[(ph, "qused", qt)])
            o("vector", "tensor_copy",
              out=AP(tensor=W.bitcast(BF16).tensor, offset=2 * KX_O + 2 * qt * 128,
                     ap=[[2 * WORK_SZ, 128], [T, 8], [1, 256]]),
              in_=kmix.rearrange("p (d t) -> p d t", d=8), inc=VE)
            dr("vector")
            ev[(ph, "kcv", qt)] = VE.n
        a3_emit(NQT - 1)
        ev["attn_v"] = VE.n
        ev["attn_T"] = TE.n

    ffn_pre(1, 0)
    prep_gu(wg_d, scr_g, swg_o, "g")
    prep_gu(wu_d, scr_u, swu_o, "u")
    ffn_mm(1, 0)
    prep_down()
    ffn_down(1, 0)
    attn_block()
    ffn_pre(2, 2)
    wt("vector", TE, ev["kx_done"])
    wt("scalar", TE, ev["kx_done"])
    wt("sync", TE, ev["attn_T"])
    wt("sync", VE, ev["attn_v"])
    wt("sync", SE, SE.n)
    wt("gpsimd", TE, ev["attn_T"])
    wt("gpsimd", VE, ev["attn_v"])
    ffn_mm(2, 2)
    ffn_down(2, 2)
    ctx.wait_toks("sync", ev["outdma"])

    with nc.Block() as block:
        @block.sync
        def _(eng):
            for fn in ctx.streams["sync"]:
                fn(eng)

        @block.gpsimd
        def _(eng):
            for fn in ctx.streams["gpsimd"]:
                fn(eng)

        @block.vector
        def _(eng):
            for fn in ctx.streams["vector"]:
                fn(eng)

        @block.scalar
        def _(eng):
            for fn in ctx.streams["scalar"]:
                fn(eng)

        @block.tensor
        def _(eng):
            for fn in ctx.streams["tensor"]:
                fn(eng)

    es.close()
    nc._dbg_sems = {"CCS": CCS.n, "TE": TE.n, "VE": VE.n, "SE": SE.n,
                    "GE": GE.n}
    return nc


def kernel(**inputs):
    n = 8
    x = np.ascontiguousarray(np.asarray(inputs["L0_attn_x"], dtype=np.float32))
    T = x.shape[1]
    nc = build(T=T, n_cores=n)
    wk = {k: np.ascontiguousarray(np.asarray(v, dtype=np.float32))
          for k, v in inputs.items() if k != "L0_attn_x"}
    in_maps = [dict(wk, L0_attn_x=np.ascontiguousarray(x[c])) for c in range(n)]
    res = run_bass_kernel_spmd(nc, in_maps, core_ids=list(range(n)))
    return np.stack([res.results[c]["out"] for c in range(n)], axis=0)


# version 21
# speedup vs baseline: 1.0635x; 1.0084x over previous
import numpy as np
import concourse.bass as bass
import concourse.mybir as mybir
from concourse.bass_utils import run_bass_kernel_spmd

F32 = mybir.dt.float32
BF16 = mybir.dt.bfloat16
AF = mybir.ActivationFunctionType
ALU = mybir.AluOpType
AX = mybir.AxisListType
AP = bass.AP

MAGIC = 12582912.0
QHI, QLO = 127.49, -128.49
D = 1024
NH, HD, DL = 16, 64, 256
NI, IH = 22, 2816
SCALE_Q = float(1.0 / (np.float64(np.sqrt(np.float32(1024.0))) + 1e-8))
LAM = 1e-4
EPS = 1e-5
QT = 256


class Sem:
    def __init__(self, h, step=1):
        self.h, self.n, self.step = h, 0, step


class Ctx:
    ENGS = ("sync", "gpsimd", "vector", "scalar", "tensor")

    def __init__(self):
        self.streams = {e: [] for e in self.ENGS}

    def op(self, eng, method, *args, inc=None, **kwargs):
        incs = []
        if inc is not None:
            for s in (inc if isinstance(inc, list) else [inc]):
                s.n += s.step
                incs.append((s.h, s.step))

        import traceback
        tb = traceback.extract_stack()[-2]

        def fn(e, _m=method, _a=args, _k=kwargs, _i=incs, _tb=tb):
            try:
                inst = getattr(e, _m)(*_a, **_k)
            except Exception:
                print(f"FAILING OP: {_m} emitted at {_tb.filename}:{_tb.lineno}")
                raise
            for (h, k) in _i:
                inst = inst.then_inc(h, k)
        self.streams[eng].append(fn)

    def wait(self, eng, sem, val=None):
        v = sem.n if val is None else val
        if v <= 0:
            return
        self.streams[eng].append(lambda e, _h=sem.h, _v=v: e.wait_ge(_h, _v))

    def drain(self, eng):
        self.streams[eng].append(lambda e: e.drain())

    def wait_tok(self, eng, tok):
        if tok is None:
            return
        s, v = tok
        self.wait(eng, s, v)

    def wait_toks(self, eng, toks):
        best = {}
        for t in toks:
            if t is None:
                continue
            s, v = t
            if v > best.get(id(s), (s, 0))[1]:
                best[id(s)] = (s, v)
        for s, v in best.values():
            self.wait(eng, s, v)


class DmaQ:

    def __init__(self, ctx, es, nc, name, eng, nlanes):
        self.ctx, self.eng = ctx, eng
        self.lanes = [Sem(es.enter_context(nc.semaphore(name + str(i))), 16)
                      for i in range(nlanes)]
        self.rr = 0

    def dma(self, out, in_, eng=None):
        e = eng or self.eng
        ln = self.lanes[self.rr]
        self.rr = (self.rr + 1) % len(self.lanes)
        if ln.n > 0:
            self.ctx.wait(e, ln, ln.n)
        self.ctx.op(e, "dma_start", out=out, in_=in_, inc=ln)
        return (ln, ln.n)

    def snapshot(self):
        return [(l, l.n) for l in self.lanes if l.n > 0]


def build(T=2048, n_cores=8):
    TC = T // 128
    NTT = max(1, T // 512)
    TTW = min(512, T)
    NQT = T // QT
    NSUB = TTW // 128

    nc = bass.Bass("TRN2", target_bir_lowering=False)

    x_in = nc.dram_tensor("L0_attn_x", [T, D], F32, kind="ExternalInput").ap()
    Wq_d = nc.dram_tensor("L0_attn_Wq", [D, D], F32, kind="ExternalInput").ap()
    Wk_d = nc.dram_tensor("L0_attn_Wk", [D, D], F32, kind="ExternalInput").ap()
    Wv_d = nc.dram_tensor("L0_attn_Wv", [D, D], F32, kind="ExternalInput").ap()
    Wo_d = nc.dram_tensor("L0_attn_Wo", [D, D], F32, kind="ExternalInput").ap()
    Wdkv_d = nc.dram_tensor("L0_attn_W_down_kv", [D, DL], F32, kind="ExternalInput").ap()
    Wupk_d = nc.dram_tensor("L0_attn_W_up_k", [DL, D], F32, kind="ExternalInput").ap()
    Wupv_d = nc.dram_tensor("L0_attn_W_up_v", [DL, D], F32, kind="ExternalInput").ap()
    clipx_d = nc.dram_tensor("L0_ffn_clip_x", [1], F32, kind="ExternalInput").ap()
    clipd_d = nc.dram_tensor("L0_ffn_clip_down", [1], F32, kind="ExternalInput").ap()
    wg_d = nc.dram_tensor("L0_ffn_w_gate", [IH, D], F32, kind="ExternalInput").ap()
    wu_d = nc.dram_tensor("L0_ffn_w_up", [IH, D], F32, kind="ExternalInput").ap()
    wd_d = nc.dram_tensor("L0_ffn_w_down", [D, IH], F32, kind="ExternalInput").ap()
    out_d = nc.dram_tensor("out", [T, D], F32, kind="ExternalOutput").ap()

    scr_g = nc.dram_tensor("scr_g", [NI, 128, D], BF16).ap()
    scr_u = nc.dram_tensor("scr_u", [NI, 128, D], BF16).ap()
    scr_d = nc.dram_tensor("scr_d", [NI, 128, D], BF16).ap()
    scr_h = nc.dram_tensor("scr_h", [NI, 128, T], F32).ap()
    scr_q = nc.dram_tensor("scr_q", [8, 128, T], BF16).ap()
    cc_in = [nc.dram_tensor(f"cc_in{i}", [1, 1], F32).ap() for i in range(4)]
    cc_out = [nc.dram_tensor(f"cc_out{i}", [1, n_cores], F32).ap() for i in range(4)]

    from contextlib import ExitStack
    es = ExitStack()

    X = es.enter_context(nc.sbuf_tensor("x_arena", [128, TC * D], F32)).ap()

    CONST_SZ = 3700
    C = es.enter_context(nc.sbuf_tensor("c_arena", [128, CONST_SZ], F32)).ap()
    _co = [0]

    def calloc(n):
        o_ = _co[0]
        _co[0] += n
        assert _co[0] <= CONST_SZ, _co[0]
        return o_

    identb_o = calloc(64)
    identf_o = calloc(128)
    onesf_o = calloc(128)
    swg_o = calloc(NI)
    swu_o = calloc(NI)
    swd_o = calloc(8)
    combg_o = calloc(NI)
    combu_o = calloc(NI)
    mr_o = calloc(2 * TC)
    ymax_o = calloc(TC)
    hmax_o = calloc(NI * NTT)
    sc_o = calloc(16)
    tmpa_o = calloc(D)
    ytile_o = calloc(D)
    swdb2_o = ytile_o
    row_o = calloc(48)
    nmagic_o = calloc(1)
    eps_o = calloc(1)
    lnv_o = calloc(TC)
    lnm2_o = calloc(TC)
    swdrow_o = calloc(128)
    rec_o = calloc(2 * QT)
    avtmp_o = calloc(QT)

    def cF(off, n):
        return C[:, off:off + n]

    def cB(off, nbf):
        return C.bitcast(BF16)[:, 2 * off:2 * off + nbf]

    identb = cB(identb_o, 128)
    identf = cF(identf_o, 128)
    onesf = cF(onesf_o, 128)

    SC_R127X, SC_SX, SC_R127H, SC_SH05 = 0, 1, 2, 3

    def scv(i):
        return cF(sc_o + i, 1)

    def r0(col, n=1):
        return C[0:1, row_o + col:row_o + col + n]

    WORK_SZ = 32900
    W = es.enter_context(nc.sbuf_tensor("w_arena", [128, WORK_SZ], F32)).ap()

    def wF(off, n):
        return W[:, off:off + n]

    def wB(off, nbf):
        return W.bitcast(BF16)[:, 2 * off:2 * off + nbf]

    XQT_O = 0
    STG_O = XQT_O + 4 * T
    WLD_O = STG_O + 2048
    WQT_O = WLD_O + 2048
    WQI_O = WQT_O + IH
    SILU_O = WQI_O + IH // 2
    HOUT_O = SILU_O + 1024
    F_END = HOUT_O + 1024
    assert F_END <= WORK_SZ
    HLD_O = 0
    HQ_O = HLD_O + 4096
    OUTB_O = HQ_O + 2 * NI * 256
    SSEL_O = OUTB_O + 1024
    RHS2_O = SSEL_O + 1024
    WDT_O = max(F_END, RHS2_O + 1024)
    assert WDT_O + NI * 512 <= WORK_SZ
    WDQ_O = WORK_SZ - 4 * IH
    assert WDQ_O >= F_END + IH

    Y2T_O = 0
    KX_O = Y2T_O
    VTOK_O = Y2T_O + 4 * T
    VTOK_SZ = (NH * TC * 65 + 1) // 2
    EXPS_O = VTOK_O + VTOK_SZ
    EXPS_SZ = TC * QT
    WBUF_O = EXPS_O + EXPS_SZ
    LATT_O = WBUF_O + 4096
    OUTT_O = LATT_O
    QSTR_O = LATT_O + max(T, 8 * QT)
    TBL_O = QSTR_O + 2048
    KMIX_O = TBL_O + 2 * (T // 2)
    MSK_O = KMIX_O + 1024
    SCM_O = MSK_O + QT
    A_END = SCM_O + 2 * QT
    assert A_END <= WORK_SZ, A_END

    ps_big = es.enter_context(nc.psum_tensor("ps_big", [128, 2048], F32)).ap()
    ps_ava = es.enter_context(nc.psum_tensor("ps_ava", [128, 512], F32)).ap()
    ps_avb = es.enter_context(nc.psum_tensor("ps_avb", [128, 512], F32)).ap()
    ps_tp0f = es.enter_context(nc.psum_tensor("ps_tp0", [128, 512], F32)).ap()
    ps_tp1f = es.enter_context(nc.psum_tensor("ps_tp1", [128, 512], F32)).ap()
    ps_tp0 = ps_tp0f.bitcast(BF16)
    ps_tp1 = ps_tp1f.bitcast(BF16)

    CCS = Sem(es.enter_context(nc.semaphore("CCS")))
    TE = Sem(es.enter_context(nc.semaphore("TE")))
    VE = Sem(es.enter_context(nc.semaphore("VE")))
    SE = Sem(es.enter_context(nc.semaphore("SE")))
    GE = Sem(es.enter_context(nc.semaphore("GE")))

    ctx = Ctx()
    o, wt, dr = ctx.op, ctx.wait, ctx.drain
    ev = {}
    QL = DmaQ(ctx, es, nc, "QL", "sync", 6)
    QS = DmaQ(ctx, es, nc, "QS", "scalar", 5)
    QG = DmaQ(ctx, es, nc, "QG", "gpsimd", 3)

    tk_x = QL.dma(out=X.rearrange("p (tc d) -> p tc d", tc=TC),
                  in_=x_in.rearrange("(tc p) d -> p tc d", p=128))
    tk_cx = QL.dma(out=r0(0), in_=clipx_d.unsqueeze(0))
    tk_cd = QL.dma(out=r0(1), in_=clipd_d.unsqueeze(0))

    o("gpsimd", "memset", identf, 1.0)
    o("gpsimd", "memset", onesf, 1.0)
    dr("gpsimd")
    o("gpsimd", "affine_select", out=identf, in_=identf, pattern=[[1, 128]],
      compare_op=ALU.is_equal, fill=0.0, base=0, channel_multiplier=-1)
    dr("gpsimd")
    o("gpsimd", "tensor_copy", out=identb, in_=identf)
    o("gpsimd", "memset", cF(nmagic_o, 1), -MAGIC)
    o("gpsimd", "memset", cF(eps_o, 1), EPS)
    dr("gpsimd")
    o("gpsimd", "engine_nop", inc=GE)
    g_consts = GE.n
    wt("scalar", GE, g_consts)
    wt("vector", GE, g_consts)
    ctx.wait_toks("scalar", [tk_cx, tk_cd])
    o("scalar", "activation", out=r0(2, 2), in_=r0(0, 2), func=AF.Sigmoid, inc=SE)
    ev["sig"] = SE.n

    def quant3(src, dst_bf, rec_pp, tmp):
        o("vector", "tensor_scalar", out=tmp, in0=src, scalar1=rec_pp,
          scalar2=QHI, op0=ALU.mult, op1=ALU.min)
        dr("vector")
        o("vector", "tensor_scalar", out=tmp, in0=tmp, scalar1=QLO,
          scalar2=MAGIC, op0=ALU.max, op1=ALU.add, inc=VE)
        wt("scalar", VE, VE.n)
        o("scalar", "activation", out=dst_bf, in_=tmp, func=AF.Identity,
          bias=cF(nmagic_o, 1), inc=SE)
        return SE.n

    def rowquant(wld, n, sw_slot, t0, t1):
        o("vector", "reduce_max", out=t0, in_=wld, axis=AX.X,
          apply_absolute_value=True)
        dr("vector")
        o("vector", "reciprocal", out=t1, in_=t0)
        dr("vector")
        o("vector", "tensor_scalar_mul", out=t1, in0=t1, scalar1=127.0)
        o("vector", "tensor_scalar_mul", out=sw_slot, in0=t0, scalar1=1.0 / 127.0)
        dr("vector")

    def prep_gu(w_dram, scr, sw_off, tag):
        for ic in range(NI):
            par = ic % 2
            wld = wF(WLD_O + par * D, D)
            if ic >= 2:
                wt("gpsimd", VE, ev[(tag, "wq", ic - 2)])
            elif tag == "u":
                wt("gpsimd", VE, ev[("g", "wq", NI - 2 + ic)])
            tk = QG.dma(out=wld, in_=w_dram[ic * 128:(ic + 1) * 128, :])
            ctx.wait_tok("vector", tk)
            rowquant(wld, D, cF(sw_off + ic, 1), wF(WQT_O, 1), wF(WQT_O + 1, 1))
            wq = wF(WQT_O + 2 + par * (D + 2), D)
            if ic >= 2:
                wt("vector", SE, ev[(tag, "wqi", ic - 2)])
            elif tag == "u":
                wt("vector", SE, ev[("g", "wqi", NI - 2 + ic)])
            o("vector", "tensor_scalar", out=wq, in0=wld, scalar1=wF(WQT_O + 1, 1),
              scalar2=QHI, op0=ALU.mult, op1=ALU.min, inc=VE)
            ev[(tag, "wq", ic)] = VE.n
            dr("vector")
            o("vector", "tensor_scalar", out=wq, in0=wq, scalar1=QLO,
              scalar2=MAGIC, op0=ALU.max, op1=ALU.add, inc=VE)
            wt("scalar", VE, VE.n)
            if ic >= 2:
                wt("scalar", TE, ev[(tag, "tp", ic - 2)])
            wqi = wB(WQI_O + par * 512, D)
            o("scalar", "activation", out=wqi, in_=wq, func=AF.Identity,
              bias=cF(nmagic_o, 1), inc=SE)
            ev[(tag, "wqi", ic)] = SE.n
            ps = ps_tp0 if par == 0 else ps_tp1
            wt("tensor", SE, SE.n)
            if ic < 2:
                wt("tensor", VE, VE.n)
            if ic >= 2:
                wt("tensor", SE, ev[(tag, "ev", ic - 2)])
            for hc in range(8):
                o("tensor", "transpose", out=ps[:, hc * 128:(hc + 1) * 128],
                  in_=wqi[:, hc * 128:(hc + 1) * 128], identity=identb,
                  inc=(TE if hc == 7 else None))
            ev[(tag, "tp", ic)] = TE.n
            stg = wB(STG_O + (1024 if tag == "u" else 0) + par * 512, D)
            wt("scalar", TE, TE.n)
            if ic >= 2:
                ctx.wait_tok("scalar", ev[(tag, "st", ic - 2)])
            o("scalar", "activation", out=stg, in_=ps[:], func=AF.Copy, inc=SE)
            ev[(tag, "ev", ic)] = SE.n
            wt("scalar", SE, SE.n)
            ev[(tag, "st", ic)] = QS.dma(out=scr[ic], in_=stg)

    def prep_down():
        wdqi = wB(WDQ_O, 8 * IH)
        for hc in range(8):
            wld = wF(WQT_O, IH) if hc % 2 == 0 else wF(F_END, IH)
            if hc >= 2:
                wt("gpsimd", VE, ev[("d", "wq", hc - 2)])
                wt("gpsimd", SE, ev[("d", "wqi", hc - 2)])
            else:
                wt("gpsimd", VE, ev[("u", "wq", NI - 1)])
                wt("gpsimd", SE, ev[("u", "wqi", NI - 1)])
            tk = QG.dma(out=wld, in_=wd_d[hc * 128:(hc + 1) * 128, :])
            ctx.wait_tok("vector", tk)
            rowquant(wld, IH, cF(swd_o + hc, 1), wF(SILU_O, 1), wF(SILU_O + 1, 1))
            o("vector", "tensor_scalar", out=wld, in0=wld, scalar1=wF(SILU_O + 1, 1),
              scalar2=QHI, op0=ALU.mult, op1=ALU.min, inc=VE)
            ev[("d", "wq", hc)] = VE.n
            dr("vector")
            o("vector", "tensor_scalar", out=wld, in0=wld, scalar1=QLO,
              scalar2=MAGIC, op0=ALU.max, op1=ALU.add, inc=VE)
            wt("scalar", VE, VE.n)
            o("scalar", "activation", out=wdqi[:, hc * IH:(hc + 1) * IH], in_=wld,
              func=AF.Identity, bias=cF(nmagic_o, 1), inc=SE)
            ev[("d", "wqi", hc)] = SE.n
        ev["wdq"] = SE.n
        for ic in range(NI):
            par = ic % 2
            ps = ps_tp0 if par == 0 else ps_tp1
            wt("tensor", SE, ev["wdq"])
            if ic < 2:
                wt("tensor", VE, VE.n)
            if ic >= 2:
                wt("tensor", SE, ev[("d", "ev", ic - 2)])
            for hc in range(8):
                o("tensor", "transpose", out=ps[:, hc * 128:(hc + 1) * 128],
                  in_=wdqi[:, hc * IH + ic * 128:hc * IH + (ic + 1) * 128],
                  identity=identb, inc=(TE if hc == 7 else None))
            ev[("d", "tp", ic)] = TE.n
            stg = wB(WLD_O + (ic % 2) * 512, D)
            wt("scalar", TE, TE.n)
            if ic >= 2:
                ctx.wait_tok("scalar", ev[("d", "st", ic - 2)])
            o("scalar", "activation", out=stg, in_=ps[:], func=AF.Copy, inc=SE)
            ev[("d", "ev", ic)] = SE.n
            wt("scalar", SE, SE.n)
            ev[("d", "st", ic)] = QS.dma(out=scr_d[ic], in_=stg)
        ev[("d", "done")] = [ev[("d", "st", i)] for i in range(NI)]
        ev["prepd_T"] = TE.n

    def layernorm(ph, absmax, ybf_consumer=None):
        ctx.wait_tok("vector", tk_x)
        ctx.wait_tok("scalar", tk_x)
        mcols = cF(mr_o, TC)
        rcols = cF(mr_o + TC, TC)
        vcols = cF(lnv_o, TC)
        for tc in range(TC):
            o("vector", "reduce_sum", out=cF(mr_o + tc, 1),
              in_=X[:, tc * D:(tc + 1) * D], axis=AX.X)
        dr("vector")
        o("vector", "tensor_scalar_mul", out=mcols, in0=mcols, scalar1=1.0 / D,
          inc=VE)
        for tc in range(TC):
            par = tc % 2
            sq = cF(tmpa_o, D) if par == 0 else cF(ytile_o, D)
            if tc >= 2:
                wt("scalar", VE, ev[(ph, "lnsq", tc - 2)])
            o("scalar", "activation", out=sq, in_=X[:, tc * D:(tc + 1) * D],
              func=AF.Square, inc=SE)
            wt("vector", SE, SE.n)
            o("vector", "reduce_sum", out=cF(lnv_o + tc, 1), in_=sq, axis=AX.X,
              inc=VE)
            ev[(ph, "lnsq", tc)] = VE.n
        dr("vector")
        m2c = cF(lnm2_o, TC)
        wt("scalar", VE, VE.n)
        o("scalar", "activation", out=m2c, in_=mcols, func=AF.Square, inc=SE)
        wt("vector", SE, SE.n)
        o("vector", "scalar_tensor_tensor", out=vcols, in0=vcols, scalar=1.0 / D,
          in1=m2c, op0=ALU.mult, op1=ALU.subtract)
        dr("vector")
        o("vector", "tensor_scalar_add", out=vcols, in0=vcols,
          scalar1=cF(eps_o, 1), inc=VE)
        wt("scalar", VE, VE.n)
        o("scalar", "activation", out=vcols, in_=vcols, func=AF.Sqrt, inc=SE)
        wt("vector", SE, SE.n)
        o("vector", "reciprocal", out=rcols, in_=vcols)
        dr("vector")
        if absmax:
            for tc in range(TC):
                yt = cF(ytile_o, D) if tc % 2 else cF(tmpa_o, D)
                o("vector", "tensor_scalar", out=yt,
                  in0=X[:, tc * D:(tc + 1) * D], scalar1=cF(mr_o + tc, 1),
                  scalar2=cF(mr_o + TC + tc, 1), op0=ALU.subtract, op1=ALU.mult)
                dr("vector")
                o("vector", "reduce_max", out=cF(ymax_o + tc, 1), in_=yt,
                  axis=AX.X, apply_absolute_value=True, inc=VE)
            dr("vector")
        if ybf_consumer is not None:
            for tc in range(TC):
                if tc >= 1:
                    wt("vector", TE, ev[(ph, "ytp", tc - 1)])
                yt = cF(ytile_o, D)
                o("vector", "tensor_scalar", out=yt,
                  in0=X[:, tc * D:(tc + 1) * D], scalar1=cF(mr_o + tc, 1),
                  scalar2=cF(mr_o + TC + tc, 1), op0=ALU.subtract, op1=ALU.mult)
                dr("vector")
                ybf = cB(ytile_o, D)
                o("vector", "tensor_copy", out=ybf, in_=yt, inc=VE)
                ybf_consumer(tc, VE.n)
        ev[(ph, "ln")] = VE.n

    def global_scale(idx, max_off, ncols, sig_col, r127_slot, s_slot, s_mult):
        mx = cF(sc_o + 10, 1)
        dr("vector")
        o("vector", "reduce_max", out=mx, in_=cF(max_off, ncols), axis=AX.X, inc=VE)
        wt("tensor", VE, VE.n)
        wt("tensor", VE, ev.get("avb_free", 0))
        o("tensor", "transpose", out=ps_ava[0:1, 0:128], in_=mx, identity=identf,
          inc=TE)
        wt("vector", TE, TE.n)
        o("vector", "reduce_max", out=r0(4), in_=ps_ava[0:1, 0:128], axis=AX.X,
          inc=VE)
        wt("gpsimd", VE, VE.n)
        tk1 = QG.dma(out=cc_in[idx], in_=r0(4))
        ctx.wait_tok("gpsimd", tk1)
        o("gpsimd", "collective_compute", "AllGather", ALU.bypass,
          replica_groups=[list(range(n_cores))],
          ins=[cc_in[idx]], outs=[cc_out[idx]], inc=CCS)
        wt("gpsimd", CCS, CCS.n)
        tk2 = QG.dma(out=r0(8, n_cores), in_=cc_out[idx])
        ctx.wait_tok("vector", tk2)
        o("vector", "reduce_max", out=r0(5), in_=r0(8, n_cores), axis=AX.X)
        dr("vector")
        wt("vector", SE, ev["sig"])
        o("vector", "tensor_mul", out=r0(6), in0=r0(5), in1=r0(sig_col))
        dr("vector")
        o("vector", "reciprocal", out=r0(7), in_=r0(6))
        dr("vector")
        o("vector", "tensor_scalar_mul", out=r0(16), in0=r0(7), scalar1=127.0)
        o("vector", "tensor_scalar_mul", out=r0(17), in0=r0(6),
          scalar1=float(s_mult / 127.0), inc=VE)
        wt("tensor", VE, VE.n)
        o("tensor", "matmul", out=ps_ava[:, 128:129], lhsT=onesf[0:1, :],
          rhs=r0(16), start=True, stop=True)
        o("tensor", "matmul", out=ps_ava[:, 129:130], lhsT=onesf[0:1, :],
          rhs=r0(17), start=True, stop=True, inc=TE)
        wt("vector", TE, TE.n)
        o("vector", "tensor_copy", out=scv(r127_slot), in_=ps_ava[:, 128:129])
        o("vector", "tensor_copy", out=scv(s_slot), in_=ps_ava[:, 129:130], inc=VE)
        dr("vector")
        ev["avb_free"] = VE.n
        return VE.n

    def ffn_pre(p, cc_base):
        ph = f"f{p}"
        layernorm(ph, absmax=True)
        v_sc = global_scale(cc_base, ymax_o, TC, 2, SC_R127X, SC_SX, 1.0)



    def ffn_mm(p, cc_base):
        ph = f"f{p}"
        o("vector", "tensor_tensor", out=cF(combg_o, NI), in0=cF(swg_o, NI),
          in1=scv(SC_SX).broadcast_to([128, NI]), op=ALU.mult)
        o("vector", "tensor_tensor", out=cF(combu_o, NI), in0=cF(swu_o, NI),
          in1=scv(SC_SX).broadcast_to([128, NI]), op=ALU.mult, inc=VE)
        dr("vector")
        ev[(ph, "vcomb")] = VE.n
        xqT = wB(XQT_O, 8 * T)
        prev_sQ = 0
        for tc in range(TC):
            xt = X[:, tc * D:(tc + 1) * D]
            qt = cF(tmpa_o, D)
            if tc == 0:
                wt("vector", TE, TE.n)
            if tc >= 1:
                wt("vector", TE, ev[(ph, "ytp", tc - 1)])
                wt("vector", SE, prev_sQ)
            o("vector", "tensor_scalar", out=qt, in0=xt,
              scalar1=cF(mr_o + tc, 1), scalar2=cF(mr_o + TC + tc, 1),
              op0=ALU.subtract, op1=ALU.mult)
            dr("vector")
            sQ = quant3(qt, cB(ytile_o, D), scv(SC_R127X), qt)
            prev_sQ = sQ
            par = tc % 2
            ps = ps_tp0 if par == 0 else ps_tp1
            wt("tensor", SE, sQ)
            if tc == 0:
                wt("tensor", VE, VE.n)
            if tc >= 2:
                wt("tensor", VE, ev[(ph, "xqev", tc - 2)])
            for hc in range(8):
                o("tensor", "transpose", out=ps[:, hc * 128:(hc + 1) * 128],
                  in_=cB(ytile_o, D)[:, hc * 128:(hc + 1) * 128], identity=identb,
                  inc=(TE if hc == 7 else None))
            ev[(ph, "ytp", tc)] = TE.n
            wt("vector", TE, TE.n)
            out_ap = AP(tensor=W.bitcast(BF16).tensor,
                        offset=2 * XQT_O + tc * 128,
                        ap=[[2 * WORK_SZ, 128], [T, 8], [1, 128]])
            o("vector", "tensor_copy", out=out_ap,
              in_=ps[:].rearrange("p (h j) -> p h j", h=8), inc=VE)
            ev[(ph, "xqev", tc)] = VE.n
        ev[(ph, "xq")] = VE.n
        xqT = wB(XQT_O, 8 * T)

        def xq_ap(hc, t0, tw):
            return xqT.rearrange("p (h t) -> p h t", h=8)[:, hc, t0:t0 + tw]

        for ic in range(NI):
            par = ic % 2
            stg_g = wB(STG_O + par * 512, D)
            stg_u = wB(STG_O + 1024 + par * 512, D)
            ctx.wait_tok("sync", ev[("g", "st", ic)])
            ctx.wait_tok("sync", ev[("u", "st", ic)])
            if p == 1 and ic < 2:
                ctx.wait_tok("sync", ev[("g", "st", NI - 2 + ic)])
                ctx.wait_tok("sync", ev[("u", "st", NI - 2 + ic)])
            if ic >= 2:
                wt("sync", TE, ev[(ph, "gumm", ic - 2)])
            if p == 2 and ic < 2:
                wt("sync", TE, ev[("f1", "guload_ok", ic)])
            tk_sg = QL.dma(out=stg_g, in_=scr_g[ic])
            tk_su = QL.dma(out=stg_u, in_=scr_u[ic])
            ctx.wait_toks("tensor", [tk_sg, tk_su])
            for tt in range(NTT):
                u = ic * NTT + tt
                gslot = ps_big[:, (u % 2) * 512:(u % 2) * 512 + TTW]
                uslot = ps_big[:, 1024 + (u % 2) * 512:1024 + (u % 2) * 512 + TTW]
                if ic == 0:
                    wt("tensor", VE,
                       ev[(ph, "xqev", min(TC - 1, (tt + 1) * NSUB - 1))])
                if u >= 2:
                    pic, ptt = divmod(u - 2, NTT)
                    wt("tensor", VE, ev[(ph, "hout", pic, ptt)])
                for hc in range(8):
                    o("tensor", "matmul", out=gslot,
                      lhsT=stg_g[:, hc * 128:(hc + 1) * 128],
                      rhs=xq_ap(hc, tt * TTW, TTW), start=(hc == 0), stop=(hc == 7))
                for hc in range(8):
                    o("tensor", "matmul", out=uslot,
                      lhsT=stg_u[:, hc * 128:(hc + 1) * 128],
                      rhs=xq_ap(hc, tt * TTW, TTW), start=(hc == 0), stop=(hc == 7),
                      inc=(TE if hc == 7 else None))
                ev[(ph, "gumm", ic)] = TE.n
                par2 = u % 2
                silu = wF(SILU_O + par2 * 512, TTW)
                hout = wF(HOUT_O + par2 * 512, TTW)
                wt("scalar", TE, TE.n)
                wt("scalar", VE, ev[(ph, "vcomb")])
                if u >= 2:
                    pic, ptt = divmod(u - 2, NTT)
                    wt("scalar", VE, ev[(ph, "hout", pic, ptt)])
                o("scalar", "activation", out=silu, in_=gslot, func=AF.Silu,
                  scale=cF(combg_o + ic, 1), inc=SE)
                wt("vector", SE, SE.n)
                if u >= 2:
                    pic, ptt = divmod(u - 2, NTT)
                    ctx.wait_tok("vector", ev[(ph, "hdma", pic, ptt)])
                o("vector", "scalar_tensor_tensor", out=hout, in0=uslot,
                  scalar=cF(combu_o + ic, 1), in1=silu, op0=ALU.mult,
                  op1=ALU.mult, inc=VE)
                ev[(ph, "hout", ic, tt)] = VE.n
                dr("vector")
                o("vector", "reduce_max", out=cF(hmax_o + u, 1), in_=hout,
                  axis=AX.X, apply_absolute_value=True, inc=VE)
                wt("scalar", VE, VE.n)
                if p == 2 and u == 0:
                    ctx.wait_toks("scalar", QL.snapshot())
                ev[(ph, "hdma", ic, tt)] = QS.dma(
                    out=scr_h[ic, :, tt * TTW:(tt + 1) * TTW], in_=hout)
        ev[(ph, "h_all")] = QS.snapshot()
        if p == 1:
            ev[("f1", "guload_ok", 0)] = TE.n
            ev[("f1", "guload_ok", 1)] = TE.n

    def ffn_down(p, cc_base):
        ph = f"f{p}"
        ctx.wait_toks("sync", ev[("d", "done")])
        wt("sync", TE, ev["prepd_T"])
        wdt_toks = []
        for ic in range(NI):
            wdt_toks.append(QL.dma(out=wB(WDT_O + ic * 512, D), in_=scr_d[ic]))

        v_sc2 = global_scale(cc_base + 1, hmax_o, NI * NTT, 3, SC_R127H,
                             SC_SH05, 0.5)

        Ssel = wF(SSEL_O, 1024)
        ctx.wait_toks("gpsimd", ev[(ph, "h_all")])
        wt("gpsimd", VE, VE.n)
        o("gpsimd", "memset", Ssel, 1.0)
        dr("gpsimd")
        o("gpsimd", "affine_select", out=Ssel, in_=Ssel, pattern=[[1, 1024]],
          compare_op=ALU.is_ge, fill=0.0, base=0, channel_multiplier=-128)
        dr("gpsimd")
        o("gpsimd", "affine_select", out=Ssel, in_=Ssel, pattern=[[-1, 1024]],
          compare_op=ALU.is_ge, fill=0.0, base=127, channel_multiplier=128)
        dr("gpsimd")
        o("gpsimd", "engine_nop", inc=GE)
        g_sel = GE.n
        wt("tensor", VE, v_sc2)
        o("tensor", "transpose", out=ps_avb[0:8, 0:128], in_=cF(swd_o, 8),
          identity=identf, inc=TE)
        wt("vector", TE, TE.n)
        o("vector", "tensor_copy", out=cF(swdrow_o, 128)[0:8, :],
          in_=ps_avb[0:8, 0:128], inc=VE)
        dr("vector")
        rhs2 = wF(RHS2_O, 1024)
        wt("vector", GE, g_sel)
        o("vector", "tensor_tensor", out=rhs2[0:8, :].rearrange(
            "p (r j) -> p r j", r=8), in0=Ssel[0:8, :].rearrange(
            "p (r j) -> p r j", r=8),
          in1=cF(swdrow_o, 128)[0:8, :].unsqueeze(1).broadcast_to([8, 8, 128]),
          op=ALU.mult, inc=VE)
        dr("vector")
        wt("tensor", VE, VE.n)
        o("tensor", "matmul", out=ps_ava[:, 0:512],
          lhsT=onesf[0:8, :], rhs=rhs2[0:8, 0:512],
          start=True, stop=True)
        o("tensor", "matmul", out=ps_avb[:, 0:512],
          lhsT=onesf[0:8, :], rhs=rhs2[0:8, 512:1024],
          start=True, stop=True, inc=TE)
        t_swdb2 = TE.n
        wt("vector", TE, t_swdb2)
        o("vector", "tensor_scalar_mul", out=cF(swdb2_o, 512),
          in0=ps_ava[:, 0:512], scalar1=scv(SC_SH05))
        o("vector", "tensor_scalar_mul", out=cF(swdb2_o + 512, 512),
          in0=ps_avb[:, 0:512], scalar1=scv(SC_SH05), inc=VE)
        dr("vector")
        v_swdb2 = VE.n
        ev["avb_free"] = VE.n

        ngrp = (NI + 3) // 4
        ctx.wait_toks("scalar", ev[(ph, "h_all")])

        def emit_loadquant_group(tt, g):
            i0, i1 = g * 4, min(NI, g * 4 + 4)
            gu = tt * ngrp + g
            par = gu % 2
            hld = wF(HLD_O + par * 2048, (i1 - i0) * TTW)
            if gu >= 2:
                pt_, pg_ = divmod(gu - 2, ngrp)
                wt("sync", SE, ev[(ph, "hq", pt_, pg_)])
            ctx.wait_toks("sync", ev[(ph, "h_all")])
            if tt == 0 and g == 0:
                wt("sync", TE, ev[(ph, "gumm", NI - 1)])
                wt("sync", VE, VE.n)
                wt("sync", SE, SE.n)
                if p == 1:
                    wt("sync", VE, ev[("d", "wq", 7)])
                    wt("sync", SE, ev[("d", "wqi", 7)])
            tk_h = QL.dma(
                out=hld.rearrange("p (i t) -> p i t", i=i1 - i0),
                in_=scr_h[i0:i1, :, tt * TTW:(tt + 1) * TTW].transpose([1, 0, 2]))
            ctx.wait_tok("vector", tk_h)
            if g == 0:
                wt("vector", VE, v_sc2)
                if tt >= 2:
                    wt("scalar", TE, ev[(ph, "dmm", tt - 2)])
                    wt("vector", TE, ev[(ph, "dmm", tt - 2)])
            for ii in range(i0, i1):
                src_ = wF(HLD_O + par * 2048 + (ii - i0) * TTW, TTW)
                if (tt, g, ii) != (0, 0, 0):
                    wt("vector", SE, SE.n)
                quant3(src_, wB(HQ_O + (tt % 2) * NI * 256 + ii * 256, TTW),
                       scv(SC_R127H), cF(tmpa_o, TTW))
            ev[(ph, "hq", tt, g)] = SE.n

        def emit_loadquant(tt):
            for g in range(ngrp):
                emit_loadquant_group(tt, g)
            ev[(ph, "hqtt", tt)] = SE.n

        def emit_mms(tt):
            hq0 = HQ_O + (tt % 2) * NI * 256
            for tsub in range(NSUB):
                tc = tt * NSUB + tsub
                for hh in range(2):
                    k = tc * 2 + hh
                    slot = ps_big[:, (k % 3) * 512:(k % 3) * 512 + 512]
                    if k >= 3:
                        wt("tensor", VE, ev[(ph, "dres", (k - 3) // 2, (k - 3) % 2)])
                    wt("tensor", SE, ev[(ph, "hqtt", tt)])
                    if tt == 0 and k == 0:
                        ctx.wait_toks("tensor", wdt_toks)
                    for ii in range(NI):
                        o("tensor", "matmul", out=slot,
                          lhsT=wB(hq0 + ii * 256, TTW)[:, tsub * 128:(tsub + 1) * 128],
                          rhs=wB(WDT_O + ii * 512, D)[:, hh * 512:(hh + 1) * 512],
                          start=(ii == 0), stop=(ii == NI - 1),
                          inc=(TE if ii == NI - 1 else None))
                    ev[(ph, "dmm", tt)] = TE.n
                    par3 = k % 2
                    ob = wF(OUTB_O + par3 * 512, 512)
                    wt("vector", TE, TE.n)
                    wt("vector", VE, v_swdb2)
                    if p == 2 and k >= 2:
                        ctx.wait_tok("vector", ev[(ph, "odma", k - 2)])
                    o("vector", "tensor_tensor", out=ob, in0=slot,
                      in1=cF(swdb2_o + hh * 512, 512), op=ALU.mult)
                    dr("vector")
                    xsl = X[:, tc * D + hh * 512:tc * D + (hh + 1) * 512]
                    o("vector", "tensor_add", out=xsl, in0=xsl, in1=ob, inc=VE)
                    ev[(ph, "dres", tc, hh)] = VE.n
                    dr("vector")
                    if p == 2:
                        wt("scalar", VE, VE.n)
                        ev[(ph, "odma", k)] = QS.dma(
                            out=out_d[tc * 128:(tc + 1) * 128,
                                      hh * 512:(hh + 1) * 512],
                            in_=xsl)

        def emit_unit(tt, tsub, hh):
            hq0 = HQ_O + (tt % 2) * NI * 256
            tc = tt * NSUB + tsub
            k = tc * 2 + hh
            slot = ps_big[:, (k % 3) * 512:(k % 3) * 512 + 512]
            if k >= 3:
                wt("tensor", VE, ev[(ph, "dres", (k - 3) // 2, (k - 3) % 2)])
            wt("tensor", SE, ev[(ph, "hqtt", tt)])
            if tt == 0 and k == 0:
                ctx.wait_toks("tensor", wdt_toks)
            for ii in range(NI):
                o("tensor", "matmul", out=slot,
                  lhsT=wB(hq0 + ii * 256, TTW)[:, tsub * 128:(tsub + 1) * 128],
                  rhs=wB(WDT_O + ii * 512, D)[:, hh * 512:(hh + 1) * 512],
                  start=(ii == 0), stop=(ii == NI - 1),
                  inc=(TE if ii == NI - 1 else None))
            ev[(ph, "dmm", tt)] = TE.n
            par3 = k % 2
            ob = wF(OUTB_O + par3 * 512, 512)
            wt("vector", TE, TE.n)
            wt("vector", VE, v_swdb2)
            if p == 2 and k >= 2:
                ctx.wait_tok("vector", ev[(ph, "odma", k - 2)])
            o("vector", "tensor_tensor", out=ob, in0=slot,
              in1=cF(swdb2_o + hh * 512, 512), op=ALU.mult)
            dr("vector")
            xsl = X[:, tc * D + hh * 512:tc * D + (hh + 1) * 512]
            o("vector", "tensor_add", out=xsl, in0=xsl, in1=ob, inc=VE)
            ev[(ph, "dres", tc, hh)] = VE.n
            dr("vector")
            if p == 2:
                wt("scalar", VE, VE.n)
                ev[(ph, "odma", k)] = QS.dma(
                    out=out_d[tc * 128:(tc + 1) * 128,
                              hh * 512:(hh + 1) * 512],
                    in_=xsl)

        units = [(tsub, hh) for tsub in range(NSUB) for hh in range(2)]
        emit_loadquant(0)
        for tt in range(NTT):
            if tt + 1 < NTT:
                nu = len(units)
                for gi in range(ngrp):
                    emit_loadquant_group(tt + 1, gi)
                    for (tsub, hh) in units[gi * nu // ngrp:(gi + 1) * nu // ngrp]:
                        emit_unit(tt, tsub, hh)
                ev[(ph, "hqtt", tt + 1)] = SE.n
            else:
                for (tsub, hh) in units:
                    emit_unit(tt, tsub, hh)
        ev[(ph, "down_v")] = VE.n
        ev[(ph, "down_T")] = TE.n
        if p == 2:
            ev["outdma"] = QS.snapshot()

    def attn_block():
        ph = "a"
        y2T = wB(Y2T_O, 8 * T)
        kX = wB(KX_O, 8 * T)
        EQA = wB(TBL_O, T)
        EQB = wB(TBL_O + T // 2, T)
        I16 = W.bitcast(mybir.dt.int16)
        MB0 = I16[:, 2 * MSK_O:2 * MSK_O + QT]
        MB1 = I16[:, 2 * MSK_O + QT:2 * MSK_O + 2 * QT]
        kmix = wB(KMIX_O, 8 * 256)
        scm = wB(SCM_O, 4 * QT)
        wbuf = wB(WBUF_O, 8 * D)
        exps = wB(EXPS_O, 2 * TC * QT)

        dtmp = wF(WBUF_O, T)
        wt("gpsimd", TE, ev[("f1", "down_T")])
        wt("gpsimd", VE, ev[("f1", "down_v")])
        o("gpsimd", "iota", dtmp, pattern=[[1, T]], base=0,
          channel_multiplier=0, allow_small_or_imprecise_dtypes=True)
        o("gpsimd", "memset", MB0, 1)
        o("gpsimd", "memset", MB1, 1)
        dr("gpsimd")
        o("gpsimd", "affine_select", out=MB0, in_=MB0, pattern=[[1, QT]],
          compare_op=ALU.is_ge, fill=0.0, base=-1, channel_multiplier=-1)
        o("gpsimd", "affine_select", out=MB1, in_=MB1, pattern=[[1, QT]],
          compare_op=ALU.is_ge, fill=0.0, base=-129, channel_multiplier=-1)
        dr("gpsimd")
        o("gpsimd", "engine_nop", inc=GE)
        wt("scalar", GE, GE.n)
        o("scalar", "activation", out=EQA, in_=dtmp, func=AF.Exp, scale=LAM)
        o("scalar", "activation", out=EQB, in_=dtmp, func=AF.Exp, scale=-LAM,
          inc=SE)
        ev["tbl"] = SE.n

        def y2_consumer(tc, vtag):
            par = tc % 2
            ps = ps_tp0 if par == 0 else ps_tp1
            wt("tensor", VE, vtag)
            if tc == 0:
                wt("tensor", VE, VE.n)
            if tc >= 2:
                wt("tensor", VE, ev[(ph, "y2ev", tc - 2)])
            for hc in range(8):
                o("tensor", "transpose", out=ps[:, hc * 128:(hc + 1) * 128],
                  in_=cB(ytile_o, D)[:, hc * 128:(hc + 1) * 128], identity=identb,
                  inc=(TE if hc == 7 else None))
            ev[(ph, "ytp", tc)] = TE.n
            wt("vector", TE, TE.n)
            out_ap = AP(tensor=W.bitcast(BF16).tensor,
                        offset=2 * Y2T_O + tc * 128,
                        ap=[[2 * WORK_SZ, 128], [T, 8], [1, 128]])
            o("vector", "tensor_copy", out=out_ap,
              in_=ps[:].rearrange("p (h j) -> p h j", h=8), inc=VE)
            ev[(ph, "y2ev", tc)] = VE.n

        layernorm(ph, absmax=False, ybf_consumer=y2_consumer)
        ev["y2T"] = VE.n

        def y2_ap(kc, t0, tw):
            return y2T.rearrange("p (k t) -> p k t", k=8)[:, kc, t0:t0 + tw]

        wt("gpsimd", VE, ev["y2T"])
        wt("gpsimd", SE, ev["tbl"])
        tk_wq = QG.dma(out=wbuf.rearrange("p (k d) -> p k d", k=8),
                       in_=Wq_d.rearrange("(k p) d -> p k d", p=128))
        ctx.wait_tok("tensor", tk_wq)
        wt("tensor", VE, ev["y2T"])
        QS0 = KMIX_O
        for dc in range(8):
            for tt in range(NTT):
                u = dc * NTT + tt
                slot = ps_big[:, (u % 4) * 512:(u % 4) * 512 + TTW]
                if u >= 4:
                    wt("tensor", SE, ev[(ph, "qev", u - 4)])
                for kc in range(8):
                    o("tensor", "matmul", out=slot,
                      lhsT=wbuf[:, kc * D + dc * 128:kc * D + (dc + 1) * 128],
                      rhs=y2_ap(kc, tt * TTW, TTW), start=(kc == 0), stop=(kc == 7),
                      inc=(TE if kc == 7 else None))
                wt("scalar", TE, TE.n)
                if u >= 2:
                    ctx.wait_tok("scalar", ev[(ph, "qdma", u - 2)])
                qstg = wB(QS0 + (u % 2) * 256, TTW)
                o("scalar", "activation", out=qstg, in_=slot, func=AF.Copy,
                  scale=SCALE_Q, inc=SE)
                ev[(ph, "qev", u)] = SE.n
                wt("scalar", SE, SE.n)
                ev[(ph, "qdma", u)] = QS.dma(
                    out=scr_q[dc, :, tt * TTW:(tt + 1) * TTW], in_=qstg)
        ev["qdma"] = QS.snapshot()
        ev["q_T"] = TE.n

        wt("gpsimd", TE, ev["q_T"])
        tk_wk = QG.dma(out=wbuf.rearrange("p (k d) -> p k d", k=8),
                       in_=Wk_d.rearrange("(k p) d -> p k d", p=128))
        ctx.wait_tok("vector", tk_wk)
        wvc = wB(QS0 + 512, 1024)
        ctx.wait_toks("gpsimd", ev["qdma"])
        last_tt = 0
        for kc8 in range(8):
            if kc8 > 0:
                wt("gpsimd", VE, last_tt)
            tk_wv = QG.dma(out=wvc, in_=Wv_d[kc8 * 128:(kc8 + 1) * 128, :])
            ctx.wait_tok("vector", tk_wv)
            o("vector", "tensor_add", out=wbuf[:, kc8 * D:(kc8 + 1) * D],
              in0=wbuf[:, kc8 * D:(kc8 + 1) * D], in1=wvc, inc=VE)
            dr("vector")
            last_tt = VE.n
        v_wsum = VE.n
        wdkv = wB(VTOK_O, 2048)
        wt("gpsimd", SE, ev[(ph, "qev", 8 * NTT - 1)])
        tk_wdkv = QG.dma(out=wdkv.rearrange("p (k l) -> p k l", k=8),
                         in_=Wdkv_d.rearrange("(k p) l -> p k l", p=128))

        latT = wB(LATT_O, 2 * T)
        kvbf = wB(QSTR_O, 8 * 512)
        for tt in range(NTT):
            for dc in range(8):
                slot = ps_big[:, (dc % 2) * 512:(dc % 2) * 512 + TTW]
                u = tt * 8 + dc
                if u >= 2:
                    pu = u - 2
                    wt("tensor", VE, ev[(ph, "kvev", pu // 8, pu % 8)])
                if u == 0:
                    wt("tensor", VE, v_wsum)
                for kc in range(8):
                    o("tensor", "matmul", out=slot,
                      lhsT=wbuf[:, kc * D + dc * 128:kc * D + (dc + 1) * 128],
                      rhs=y2_ap(kc, tt * TTW, TTW), start=(kc == 0), stop=(kc == 7),
                      inc=(TE if kc == 7 else None))
                wt("vector", TE, TE.n)
                if tt > 0 and dc == 0:
                    wt("vector", TE, ev[(ph, "latmm", tt - 1)])
                o("vector", "tensor_copy", out=kvbf[:, dc * TTW:(dc + 1) * TTW],
                  in_=slot, inc=VE)
                ev[(ph, "kvev", tt, dc)] = VE.n
            wt("tensor", VE, ev[(ph, "kvev", tt, 7)])
            if tt == 0:
                ctx.wait_tok("tensor", tk_wdkv)
            for lc in range(2):
                slot = ps_big[:, 1024 + lc * 512:1024 + lc * 512 + TTW]
                for kc in range(8):
                    o("tensor", "matmul", out=slot,
                      lhsT=wdkv[:, kc * 256 + lc * 128:kc * 256 + (lc + 1) * 128],
                      rhs=kvbf[:, kc * TTW:(kc + 1) * TTW], start=(kc == 0),
                      stop=(kc == 7), inc=(TE if kc == 7 else None))
                ev[(ph, "latmm", tt)] = TE.n
                wt("vector", TE, TE.n)
                o("vector", "tensor_copy",
                  out=latT.rearrange("p (l t) -> p l t", l=2)[
                      :, lc, tt * TTW:(tt + 1) * TTW], in_=slot, inc=VE)
            ev[(ph, "latev", tt)] = VE.n
        ev["lat"] = VE.n

        def lat_ap(lc, t0, tw):
            return latT.rearrange("p (l t) -> p l t", l=2)[:, lc, t0:t0 + tw]

        wupk = wB(WBUF_O, 2 * D)
        wt("gpsimd", VE, ev["lat"])
        tk_upk = QG.dma(out=wupk.rearrange("p (l d) -> p l d", l=2),
                        in_=Wupk_d.rearrange("(l p) d -> p l d", p=128))
        ctx.wait_tok("tensor", tk_upk)
        wt("tensor", VE, ev["lat"])
        for dc in range(8):
            for tt in range(NTT):
                u = dc * NTT + tt
                slot = ps_big[:, (u % 4) * 512:(u % 4) * 512 + TTW]
                if u >= 4:
                    wt("tensor", VE, ev[(ph, "krev", u - 4)])
                for lc in range(2):
                    o("tensor", "matmul", out=slot,
                      lhsT=wupk[:, lc * D + dc * 128:lc * D + (dc + 1) * 128],
                      rhs=lat_ap(lc, tt * TTW, TTW), start=(lc == 0), stop=(lc == 1),
                      inc=(TE if lc == 1 else None))
                wt("vector", TE, TE.n)
                if u == 0:
                    wt("vector", SE, ev["tbl"])
                o("vector", "tensor_tensor",
                  out=kX.rearrange("p (d t) -> p d t", d=8)[
                      :, dc, tt * TTW:(tt + 1) * TTW],
                  in0=slot, in1=EQB[:, tt * TTW:(tt + 1) * TTW],
                  op=ALU.mult, inc=VE)
                ev[(ph, "krev", u)] = VE.n
        ev["krec"] = VE.n
        ev["krec_T"] = TE.n

        wupv = wB(WBUF_O, 2 * D)
        wt("gpsimd", TE, ev["krec_T"])
        tk_upv = QG.dma(out=wupv.rearrange("p (l d) -> p l d", l=2),
                        in_=Wupv_d.rearrange("(l p) d -> p l d", p=128))
        vtok = wB(VTOK_O, NH * TC * 65)
        wt("vector", TE, ev[(ph, "latmm", NTT - 1)])
        o("vector", "memset",
          vtok.rearrange("p (n c) -> p n c", c=65)[:, :, 64:65], 1.0, inc=VE)
        dr("vector")
        v_ones = VE.n
        ctx.wait_tok("tensor", tk_upv)
        wt("tensor", VE, v_ones)
        for tc in range(TC):
            for dh in range(2):
                u = tc * 2 + dh
                slot = ps_big[:, (u % 4) * 512:(u % 4) * 512 + 512]
                if u >= 4:
                    wt("tensor", VE, ev[(ph, "vtev", u - 4)])
                for lc in range(2):
                    o("tensor", "matmul", out=slot,
                      lhsT=lat_ap(lc, tc * 128, 128),
                      rhs=wupv[:, lc * D + dh * 512:lc * D + (dh + 1) * 512],
                      start=(lc == 0), stop=(lc == 1),
                      inc=(TE if lc == 1 else None))
                wt("vector", TE, TE.n)
                out_ap = AP(tensor=W.bitcast(BF16).tensor,
                            offset=2 * VTOK_O + (dh * 8 * TC + tc) * 65,
                            ap=[[2 * WORK_SZ, 128], [65 * TC, 8], [1, 64]])
                o("vector", "tensor_copy", out=out_ap,
                  in_=slot.rearrange("p (h e) -> p h e", h=8), inc=VE)
                ev[(ph, "vtev", u)] = VE.n
        ev["vtok"] = VE.n

        wo = wB(WBUF_O, 8 * D)
        wt("gpsimd", TE, TE.n)
        tk_wo = QG.dma(out=wo.rearrange("p (k d) -> p k d", k=8),
                       in_=Wo_d.rearrange("(k p) d -> p k d", p=128))

        qA = wB(QSTR_O, 8 * QT)
        qB = wB(QSTR_O + 1024, 8 * QT)
        exp_hist = []
        lkx = kX.rearrange("p (d t) -> p d t", d=8)
        NG2 = TC // 2

        def eqslice(tbl, qt):
            return AP(tensor=W.bitcast(BF16).tensor,
                      offset=2 * TBL_O + (0 if tbl is EQA else
                                          (T if tbl is EQB else 2 * T)) + qt * QT,
                      ap=[[2 * WORK_SZ, 128], [0, 8], [1, QT]])

        def a3_emit(qt):
            wt("tensor", VE, ev[(ph, "avTT", qt, NH // 2 - 1)])
            if qt == 0:
                ctx.wait_tok("tensor", tk_wo)
            outT = wB(OUTT_O + (qt % 2) * 1024, 8 * QT)
            loT = outT.rearrange("p (c t) -> p c t", c=8)
            for tsub in range(QT // 128):
                tc = (qt * QT) // 128 + tsub
                for dh in range(2):
                    u = qt * (QT // 128) * 2 + tsub * 2 + dh
                    slot = (ps_ava if u % 2 == 0 else ps_avb)[:, 0:512]
                    if u >= 2:
                        wt("tensor", VE, ev[(ph, "xres", u - 2)])
                    for kc in range(8):
                        o("tensor", "matmul", out=slot,
                          lhsT=loT[:, kc, tsub * 128:(tsub + 1) * 128],
                          rhs=wo[:, kc * D + dh * 512:kc * D + (dh + 1) * 512],
                          start=(kc == 0), stop=(kc == 7),
                          inc=(TE if kc == 7 else None))
                    wt("vector", TE, TE.n)
                    xsl = X[:, tc * D + dh * 512:tc * D + (dh + 1) * 512]
                    o("vector", "tensor_add", out=xsl, in0=xsl, in1=slot, inc=VE)
                    ev[(ph, "xres", u)] = VE.n
                    dr("vector")

        def do_av_mm(h, qt):
            hp, hi = h // 2, h % 2
            avps = ps_ava if hi == 0 else ps_avb
            avslot = avps[0:65, 0:QT]
            if hi == 0 and (qt, hp) > (0, 0):
                ph_, hp_ = (qt, hp - 1) if hp >= 1 else (qt - 1, NH // 2 - 1)
                wt("tensor", VE, ev[(ph, "avTT", ph_, hp_)])
            if hi == 0 and hp == 0 and qt >= 1:
                wt("tensor", VE,
                   ev[(ph, "xres", qt * (QT // 128) * 2 - 1)])
            estrip = hi * TC * QT
            for g in range(NG2):
                wt("tensor", SE, ev[(ph, "exp", qt, hp, g)])
                if g == qt:
                    wt("tensor", VE, ev[(ph, "sel", qt, hp)])
                for j in range(2):
                    kc = g * 2 + j
                    o("tensor", "matmul", out=avslot,
                      lhsT=vtok[:, (h * TC + kc) * 65:(h * TC + kc) * 65 + 65],
                      rhs=exps[:, estrip + kc * QT:estrip + (kc + 1) * QT],
                      start=(kc == 0), stop=(kc == TC - 1),
                      inc=(TE if kc == TC - 1 else None))
            ev[(ph, "avmm", qt, h)] = TE.n

        def do_pair_norm(hp, qt):
            den0 = C[64:65, rec_o:rec_o + QT]
            den1 = C[64:65, rec_o + QT:rec_o + 2 * QT]
            wt("vector", TE, ev[(ph, "avmm", qt, 2 * hp)])
            o("vector", "tensor_copy", out=den0, in_=ps_ava[64:65, 0:QT],
              inc=VE)
            v_cp0 = VE.n
            wt("vector", TE, ev[(ph, "avmm", qt, 2 * hp + 1)])
            o("vector", "tensor_copy", out=den1, in_=ps_avb[64:65, 0:QT],
              inc=VE)
            v_cp1 = VE.n
            wt("tensor", VE, v_cp0)
            o("tensor", "matmul", out=ps_ava[0:64, 256:256 + QT],
              lhsT=onesf[64:65, 0:64], rhs=den0, start=True, stop=True)
            wt("tensor", VE, v_cp1)
            o("tensor", "matmul", out=ps_ava[64:128, 256:256 + QT],
              lhsT=onesf[64:65, 0:64], rhs=den1, start=True, stop=True, inc=TE)
            t_bc = TE.n
            rec128 = C[:, avtmp_o:avtmp_o + QT]
            wt("vector", TE, t_bc)
            o("vector", "reciprocal", out=rec128, in_=ps_ava[:, 256:256 + QT],
              inc=VE)
            dr("vector")
            outT = wB(OUTT_O + (qt % 2) * 1024, 8 * QT)
            oT = outT.rearrange("p (c t) -> p c t", c=8)
            o("vector", "tensor_tensor", out=oT[0:64, hp, :],
              in0=ps_ava[0:64, 0:QT], in1=rec128[0:64, :], op=ALU.mult)
            o("vector", "tensor_tensor", out=oT[64:128, hp, :],
              in0=ps_avb[0:64, 0:QT], in1=rec128[64:128, :], op=ALU.mult,
              inc=VE)
            ev[(ph, "avTT", qt, hp)] = VE.n

        def eqslice_hc(tbl, qt2):
            return AP(tensor=W.bitcast(BF16).tensor,
                      offset=2 * TBL_O + (0 if tbl is EQA else
                                          (T if tbl is EQB else 2 * T)) + qt2 * QT,
                      ap=[[2 * WORK_SZ, 128], [1, QT]])

        qAd = qA.rearrange("p (d t) -> p d t", d=8)
        qBd = qB.rearrange("p (d t) -> p d t", d=8)

        def qprep_hc(qt2, hc):
            wt("sync", TE, ev[(ph, "qrd", qt2 - 1, hc)])
            tk = QL.dma(out=qAd[:, hc, :],
                        in_=scr_q[hc, :, qt2 * QT:(qt2 + 1) * QT])
            ctx.wait_tok("vector", tk)
            wt("vector", TE, ev[(ph, "qrd", qt2 - 1, hc)])
            o("vector", "tensor_tensor", out=qBd[:, hc, :], in0=qAd[:, hc, :],
              in1=eqslice_hc(EQB, qt2), op=ALU.mult)
            dr("vector")
            o("vector", "tensor_tensor", out=qAd[:, hc, :], in0=qAd[:, hc, :],
              in1=eqslice_hc(EQA, qt2), op=ALU.mult, inc=VE)
            ev[(ph, "qp", qt2, hc)] = VE.n

        for qt in range(NQT):
            if qt == 0:
                ctx.wait_toks("sync", ev["qdma"])
                wt("sync", TE, ev[(ph, "latmm", NTT - 1)])
                wt("sync", VE, ev["lat"])
                tk_q = QL.dma(
                    out=qA.rearrange("p (d t) -> p d t", d=8),
                    in_=scr_q[:, :, qt * QT:(qt + 1) * QT].transpose([1, 0, 2]))
                ctx.wait_tok("vector", tk_q)
                o("vector", "tensor_tensor",
                  out=qB.rearrange("p (d t) -> p d t", d=8),
                  in0=qA.rearrange("p (d t) -> p d t", d=8),
                  in1=eqslice(EQB, qt), op=ALU.mult)
                dr("vector")
                o("vector", "tensor_tensor",
                  out=qA.rearrange("p (d t) -> p d t", d=8),
                  in0=qA.rearrange("p (d t) -> p d t", d=8),
                  in1=eqslice(EQA, qt), op=ALU.mult, inc=VE)
                v_qk = VE.n
            eqa_s = AP(tensor=W.bitcast(BF16).tensor,
                       offset=2 * TBL_O + 2 * qt * 128,
                       ap=[[2 * WORK_SZ, 128], [0, 8], [1, 256]])
            o("vector", "tensor_tensor",
              out=kmix.rearrange("p (d t) -> p d t", d=8),
              in0=AP(tensor=W.bitcast(BF16).tensor, offset=2 * KX_O + 2 * qt * 128,
                     ap=[[2 * WORK_SZ, 128], [T, 8], [1, 256]]),
              in1=eqa_s, op=ALU.mult)
            dr("vector")
            o("vector", "tensor_tensor",
              out=kmix.rearrange("p (d t) -> p d t", d=8),
              in0=kmix.rearrange("p (d t) -> p d t", d=8),
              in1=eqa_s, op=ALU.mult, inc=VE)
            v_km = VE.n
            for hp in range(NH // 2):
                hc = hp
                for g in range(NG2):
                    u = hp * NG2 + g
                    sslot = ps_big[:, (u % 2) * 1024:(u % 2) * 1024 + 4 * QT]
                    if len(exp_hist) >= 2:
                        wt("tensor", SE, exp_hist[-2])
                    if g == 0:
                        if qt == 0:
                            if hp == 0:
                                ctx.wait_tok("tensor", tk_q)
                                wt("tensor", VE, v_qk)
                                wt("tensor", VE, ev["krec"])
                                wt("tensor", VE, ev["vtok"])
                        else:
                            wt("tensor", VE, ev[(ph, "qp", qt, hp)])
                    if qt >= 1 and g == qt - 1:
                        wt("tensor", VE, ev[(ph, "kcv", qt - 1)])
                    for j in range(2):
                        kc = g * 2 + j
                        if kc >= 2 * qt + 2:
                            rhs_q = qA
                        elif kc <= 2 * qt - 1:
                            rhs_q = qB
                        else:
                            rhs_q = qA
                        rq = rhs_q.rearrange("p (d t) -> p d t", d=8)
                        for hi in range(2):
                            hr = hi * 64
                            o("tensor", "matmul",
                              out=sslot[:, hi * 512 + j * QT:hi * 512 + (j + 1) * QT],
                              lhsT=lkx[hr:hr + 64, hc, kc * 128:(kc + 1) * 128],
                              rhs=rq[hr:hr + 64, hc, :],
                              start=True, stop=True,
                              inc=(TE if (j == 1 and hi == 1) else None))
                    t_sc = TE.n
                    if g == qt:
                        if hp == 0:
                            wt("tensor", VE, v_km)
                        if hp >= 1:
                            wt("tensor", SE, ev[(ph, "expB", qt, hp - 1)])
                        elif qt >= 1:
                            wt("tensor", SE, ev[(ph, "expB", qt - 1, NH // 2 - 1)])
                        km = kmix.rearrange("p (d t) -> p d t", d=8)
                        qb = qB.rearrange("p (d t) -> p d t", d=8)
                        for j2 in range(2):
                            for hi in range(2):
                                hr = hi * 64
                                tp = ps_tp0f if hi == 0 else ps_tp1f
                                o("tensor", "matmul",
                                  out=tp[:, j2 * QT:(j2 + 1) * QT],
                                  lhsT=km[hr:hr + 64, hc, j2 * 128:(j2 + 1) * 128],
                                  rhs=qb[hr:hr + 64, hc, :],
                                  start=True, stop=True,
                                  inc=(TE if (j2 == 1 and hi == 1) else None))
                        t_b = TE.n
                        wt("scalar", TE, t_b)
                        if hp >= 1:
                            wt("scalar", VE, ev[(ph, "sel", qt, hp - 1)])
                        elif qt >= 1:
                            wt("scalar", VE, ev[(ph, "sel", qt - 1, NH // 2 - 1)])
                        o("scalar", "activation", out=scm[:, 0:2 * QT],
                          in_=ps_tp0f[:, 0:2 * QT], func=AF.Exp)
                        o("scalar", "activation", out=scm[:, 2 * QT:4 * QT],
                          in_=ps_tp1f[:, 0:2 * QT], func=AF.Exp, inc=SE)
                        ev[(ph, "expB", qt, hp)] = SE.n
                    wt("scalar", TE, t_sc)
                    if g == 0:
                        if hp >= 1:
                            wt("scalar", TE, ev[(ph, "avmm", qt, 2 * hp - 1)])
                        elif qt >= 1:
                            wt("scalar", TE, ev[(ph, "avmm", qt - 1, NH - 1)])
                    exp_out = AP(tensor=W.bitcast(BF16).tensor,
                                 offset=2 * EXPS_O + g * 2 * QT,
                                 ap=[[2 * WORK_SZ, 128], [TC * QT, 2], [1, 2 * QT]])
                    o("scalar", "activation", out=exp_out,
                      in_=sslot, func=AF.Exp, inc=SE)
                    ev[(ph, "exp", qt, hp, g)] = SE.n
                    exp_hist.append(SE.n)
                    if g == qt:
                        wt("scalar", VE, VE.n)
                        wt("vector", SE, SE.n)
                        for hi in range(2):
                            estrip = hi * TC * QT
                            for j2 in range(2):
                                kc = 2 * qt + j2
                                o("vector", "copy_predicated",
                                  out=exps[:, estrip + kc * QT:
                                           estrip + (kc + 1) * QT],
                                  mask=(MB0 if j2 == 0 else MB1),
                                  data=scm[:, (hi * 2 + j2) * QT:
                                           (hi * 2 + j2 + 1) * QT], inc=(
                                      VE if (hi == 1 and j2 == 1) else None))
                        ev[(ph, "sel", qt, hp)] = VE.n
                ev[(ph, "qrd", qt, hp)] = TE.n
                if hp == 0 and qt >= 1:
                    a3_emit(qt - 1)
                do_av_mm(2 * hp, qt)
                do_av_mm(2 * hp + 1, qt)
                do_pair_norm(hp, qt)
                if qt + 1 < NQT:
                    qprep_hc(qt + 1, hp)
            if qt == NQT - 1:
                ev["kx_done"] = TE.n
            ev[(ph, "qused", qt)] = TE.n
            wt("vector", TE, ev[(ph, "qused", qt)])
            o("vector", "tensor_copy",
              out=AP(tensor=W.bitcast(BF16).tensor, offset=2 * KX_O + 2 * qt * 128,
                     ap=[[2 * WORK_SZ, 128], [T, 8], [1, 256]]),
              in_=kmix.rearrange("p (d t) -> p d t", d=8), inc=VE)
            dr("vector")
            ev[(ph, "kcv", qt)] = VE.n
        a3_emit(NQT - 1)
        ev["attn_v"] = VE.n
        ev["attn_T"] = TE.n

    ffn_pre(1, 0)
    prep_gu(wg_d, scr_g, swg_o, "g")
    prep_gu(wu_d, scr_u, swu_o, "u")
    ffn_mm(1, 0)
    prep_down()
    ffn_down(1, 0)
    attn_block()
    ffn_pre(2, 2)
    wt("vector", TE, ev["kx_done"])
    wt("scalar", TE, ev["kx_done"])
    wt("sync", TE, ev["attn_T"])
    wt("sync", VE, ev["attn_v"])
    wt("sync", SE, SE.n)
    wt("gpsimd", TE, ev["attn_T"])
    wt("gpsimd", VE, ev["attn_v"])
    ffn_mm(2, 2)
    ffn_down(2, 2)
    ctx.wait_toks("sync", ev["outdma"])

    with nc.Block() as block:
        @block.sync
        def _(eng):
            for fn in ctx.streams["sync"]:
                fn(eng)

        @block.gpsimd
        def _(eng):
            for fn in ctx.streams["gpsimd"]:
                fn(eng)

        @block.vector
        def _(eng):
            for fn in ctx.streams["vector"]:
                fn(eng)

        @block.scalar
        def _(eng):
            for fn in ctx.streams["scalar"]:
                fn(eng)

        @block.tensor
        def _(eng):
            for fn in ctx.streams["tensor"]:
                fn(eng)

    es.close()
    nc._dbg_sems = {"CCS": CCS.n, "TE": TE.n, "VE": VE.n, "SE": SE.n,
                    "GE": GE.n}
    return nc


def kernel(**inputs):
    n = 8
    x = np.ascontiguousarray(np.asarray(inputs["L0_attn_x"], dtype=np.float32))
    T = x.shape[1]
    nc = build(T=T, n_cores=n)
    wk = {k: np.ascontiguousarray(np.asarray(v, dtype=np.float32))
          for k, v in inputs.items() if k != "L0_attn_x"}
    in_maps = [dict(wk, L0_attn_x=np.ascontiguousarray(x[c])) for c in range(n)]
    res = run_bass_kernel_spmd(nc, in_maps, core_ids=list(range(n)))
    return np.stack([res.results[c]["out"] for c in range(n)], axis=0)
